# version 24
# baseline (speedup 1.0000x reference)
import numpy as np
import concourse.bass as bass

DEBUG_TAPS = False
FLUSH_N = {"tensor": 1, "scalar": 1, "vector": 1, "sync": 1, "gpsimd": 1}

import concourse.mybir as mybir
from concourse.bass_utils import run_bass_kernel_spmd
from contextlib import ExitStack

F32 = mybir.dt.float32
BF16 = mybir.dt.bfloat16
FP16 = mybir.dt.float16
AF = mybir.ActivationFunctionType
ALU = mybir.AluOpType
AX = mybir.AxisListType

B, T, D = 2, 4096, 2048
H = 16
HF = 8192
NC = 8
R = (B * T) // NC
NT = R // 128
NDC = D // 128
SCALE = 0.125


class Prog:
    def __init__(self):
        nc = bass.Bass("TRN2", target_bir_lowering=False)
        self.nc = nc
        self.es = ExitStack()
        ap = lambda name, shape, dt, kind: nc.dram_tensor(name, shape, dt, kind=kind).ap()
        self.x = ap("x", [R, D], F32, "ExternalInput")
        self.wq = ap("wq", [128, 4096], F32, "ExternalInput")
        self.wk = ap("wk", [128, 4096], F32, "ExternalInput")
        self.wv = ap("wv", [128, 4096], F32, "ExternalInput")
        self.wo = ap("wo", [2, 4, 128, 512], F32, "ExternalInput")
        self.wg = ap("wg", [64, 128, 2048], F32, "ExternalInput")
        self.wu = ap("wu", [64, 128, 2048], F32, "ExternalInput")
        self.wd = ap("wd", [4, 64, 128, 512], F32, "ExternalInput")
        self.lam = ap("lam", [1, 2], F32, "ExternalInput")
        self.msk = ap("msk", [4, 128, 1024], F32, "ExternalInput")
        self.y = ap("y", [R, D], F32, "ExternalOutput")
        KI = "ExternalOutput" if DEBUG_TAPS else "Internal"
        aps = lambda name, shape, dt: nc.dram_tensor(name, shape, dt, kind="Internal",
                                                     addr_space="Shared").ap()
        self.cc_xn = ap("cc_xn", [R, D], BF16, "Internal")
        self.ag_xn = aps("ag_xn", [NC, R, D], BF16)
        self.cc_o = ap("cc_o", [2, 2, 128, T], BF16, KI)
        self.cc_s = [ap(f"cc_s{i}", [T, D], BF16, "Internal") for i in range(2)]
        self.cc_g = ap("cc_g", [R, D], BF16, KI)
        self.rs_s = [ap(f"rs_s{i}", [512, D], BF16, "Internal") for i in range(2)]
        self.wq_d = ap("wq_d", [128, 4096], BF16, "Internal")
        self.wk_d = ap("wk_d", [128, 4096], BF16, "Internal")
        self.wv_d = ap("wv_d", [128, 4096], BF16, "Internal")
        self.msk_d = ap("msk_d", [128, 4096], BF16, "Internal")
        self.wob_d = ap("wob_d", [128, 4096], BF16, "Internal")
        if DEBUG_TAPS:
            self.tp_xn = ap("tp_xn", [R, D], BF16, "ExternalOutput")
            self.tp_s = ap("tp_s", [NC * R, D], BF16, "ExternalOutput")
            self.tp_rs = ap("tp_rs", [R, D], BF16, "ExternalOutput")

        E = self.es.enter_context
        self.s_t = E(nc.semaphore("s_t"))
        self.s_a = E(nc.semaphore("s_a"))
        self.s_v = E(nc.semaphore("s_v"))
        self.s_s = E(nc.semaphore("s_s"))
        self.s_g = E(nc.semaphore("s_g"))
        self.s_c = E(nc.semaphore("s_c"))
        self.s_w = E(nc.semaphore("s_w"))
        self.cnt = {"t": 0, "a": 0, "v": 0, "s": 0, "g": 0, "c": 0, "w": 0}
        self.sem = {"t": self.s_t, "a": self.s_a, "v": self.s_v,
                    "s": self.s_s, "g": self.s_g, "c": self.s_c, "w": self.s_w}

        sb = lambda n, s, d: E(nc.sbuf_tensor(n, s, d))
        self.xqT = None
        self.ones_col = sb("ones_col", [128, 1], BF16)
        self.ones_mat = sb("ones_mat", [128, 128], BF16)
        self.ones_row = sb("ones_row", [1, 128], BF16)
        self.onesr_f = sb("onesr_f", [1, 128], F32)
        self.b1536 = sb("b1536", [128, 1], F32)
        self.lamneg = sb("lamneg", [128, 2], F32)
        self.lam_sig = sb("lam_sig", [1, 2], F32)
        self.lam_sb = sb("lam_sb", [1, 2], F32)

        self.block = E(nc.Block())
    def sec(self, eng, fn):
        getattr(self.block, eng)(fn)

    def flush(self):
        pass

    def inc(self, instr, key, by=None):
        by = by if by is not None else (16 if key in ("s", "g", "w") else 1)
        instr.then_inc(self.sem[key], by)
        self.cnt[key] += by
        return self.cnt[key]

    def barrier(self):
        self.flush()
        snap = dict(self.cnt)

        def mk():
            def f(e):
                for k in ("t", "a", "v", "s", "g", "w"):
                    if snap[k]:
                        e.wait_ge(self.sem[k], snap[k])
            return f
        for eng in ("tensor", "scalar", "vector", "sync", "gpsimd"):
            self.sec(eng, mk())

    def setup(self):
        p = self

        def vec(v):
            v.memset(p.ones_col[:], 1.0)
            v.memset(p.ones_mat[:], 1.0)
            v.memset(p.ones_row[:], 1.0)
            v.memset(p.onesr_f[:], 1.0)
            p.inc(v.memset(p.b1536[:], 1536.0), "v")
        p.sec("vector", vec)

        def gp(g):
            p.inc(g.dma_start(out=p.lam_sb[:], in_=p.lam), "g")
        p.sec("gpsimd", gp)
        self.g_lam = p.cnt["g"]

    def phase1(self):
        p, nc = self, self.nc
        with ExitStack() as es:
            E = es.enter_context
            x_f = [E(nc.sbuf_tensor(f"x_f{i}", [128, D], F32)) for i in range(2)]
            xc_f = [E(nc.sbuf_tensor(f"xc_f{i}", [128, D], F32)) for i in range(2)]
            xn_f = [E(nc.sbuf_tensor(f"xn_f{i}", [128, D], F32)) for i in range(2)]
            xnn_f = [E(nc.sbuf_tensor(f"xnn_f{i}", [128, D], F32)) for i in range(2)]
            xn_bf = [E(nc.sbuf_tensor(f"xn_bf{i}", [128, D], BF16)) for i in range(2)]
            xq_bf = [E(nc.sbuf_tensor(f"xq_bf{i}", [128, D], BF16)) for i in range(2)]
            f16q = [E(nc.sbuf_tensor(f"f16q{i}", [128, D], FP16)) for i in range(2)]
            xq_f = [E(nc.sbuf_tensor(f"xq_f{i}", [128, D], F32)) for i in range(2)]
            dump = E(nc.sbuf_tensor("dump", [128, D], F32))
            c2 = lambda n: [E(nc.sbuf_tensor(n + str(i), [128, 1], F32)) for i in range(2)]
            s_sum, s_sq, mneg, e2eps, msqn, vare, sqv, rs = (
                c2("s_sum"), c2("s_sq"), c2("mneg"), c2("e2eps"), c2("msqn"),
                c2("vare"), c2("sqv"), c2("rs"))
            s_sum2, s_sq2, m2neg, e2eps2, msqn2, vare2, sqv2, rs2 = (
                c2("s_sum2"), c2("s_sq2"), c2("m2neg"), c2("e2eps2"), c2("msqn2"),
                c2("vare2"), c2("sqv2"), c2("rs2"))
            amax, rtmp, sq, invsq, vgap = c2("amax"), c2("rtmp"), c2("sq"), c2("invsq"), c2("vgap")

            a_last = [0, 0]
            v1_of = [0] * (NT + 2)
            s_tr_of = [0] * NT
            g_xn_of = [0, 0]

            for i0 in range(0, NT, 2):
                pair = (i0, i0 + 1)

                def lx(sy, pair=pair, an=(a_last[0], a_last[1])):
                    for w, i in enumerate(pair):
                        if an[i % 2]:
                            sy.wait_ge(p.s_a, an[i % 2])
                        p.inc(sy.dma_start(out=x_f[i % 2][:], in_=p.x[i * 128:(i + 1) * 128, :]), "s")
                p.sec("sync", lx)
                s_x = p.cnt["s"]

                def a1(s, pair=pair, s_x=s_x,
                       vn=(v1_of[i0 - 2] if i0 >= 2 else 0, v1_of[i0 - 1] if i0 >= 1 else 0)):
                    for w, i in enumerate(pair):
                        par = i % 2
                        s.wait_ge(p.s_s, s_x - 16 * (1 - w))
                        if vn[par]:
                            s.wait_ge(p.s_v, vn[par])
                        s.activation(dump[:], x_f[par][:], AF.Identity, accum_out=s_sum[par][:, 0:1])
                        p.inc(s.activation(dump[:], x_f[par][:], AF.Square,
                                           accum_out=s_sq[par][:, 0:1]), "a")
                p.sec("scalar", a1)
                a_n1 = p.cnt["a"]

                def v1(v, pair=pair, a_n1=a_n1):
                    for w, i in enumerate(pair):
                        par = i % 2
                        v.wait_ge(p.s_a, a_n1 - (1 - w))
                        v.tensor_scalar(mneg[par][:], s_sum[par][:], -1.0 / D, None, op0=ALU.mult)
                        p.inc(v.tensor_scalar(e2eps[par][:], s_sq[par][:], 1.0 / D, 1e-5,
                                              op0=ALU.mult, op1=ALU.add), "v")
                p.sec("vector", v1)
                v1_of[i0] = p.cnt["v"] - 1
                v1_of[i0 + 1] = p.cnt["v"]
                v_n1 = p.cnt["v"]

                def a2(s, pair=pair, v_n=v_n1):
                    for w, i in enumerate(pair):
                        par = i % 2
                        s.wait_ge(p.s_v, v_n - (1 - w))
                        s.activation(xc_f[par][:], x_f[par][:], AF.Identity, bias=mneg[par][:, 0:1])
                        p.inc(s.activation(msqn[par][:], mneg[par][:], AF.Square), "a")
                p.sec("scalar", a2)
                a_n2 = p.cnt["a"]

                def v2(v, pair=pair, a_n=a_n2):
                    for w, i in enumerate(pair):
                        par = i % 2
                        v.wait_ge(p.s_a, a_n - (1 - w))
                        p.inc(v.tensor_tensor(vare[par][:], e2eps[par][:], msqn[par][:],
                                              op=ALU.subtract), "v")
                p.sec("vector", v2)
                v_n2 = p.cnt["v"]

                def a3(s, pair=pair, v_n=v_n2):
                    for w, i in enumerate(pair):
                        par = i % 2
                        s.wait_ge(p.s_v, v_n - (1 - w))
                        p.inc(s.activation(sqv[par][:], vare[par][:], AF.Sqrt), "a")
                p.sec("scalar", a3)
                a_n3 = p.cnt["a"]

                def v3(v, pair=pair, a_n=a_n3):
                    for w, i in enumerate(pair):
                        par = i % 2
                        v.wait_ge(p.s_a, a_n - (1 - w))
                        p.inc(v.reciprocal(rs[par][:], sqv[par][:]), "v")
                p.sec("vector", v3)
                v_n3 = p.cnt["v"]

                def a4(s, pair=pair, v_n=v_n3):
                    for w, i in enumerate(pair):
                        par = i % 2
                        s.wait_ge(p.s_v, v_n - (1 - w))
                        s.activation(xn_f[par][:], xc_f[par][:], AF.Identity, scale=rs[par][:, 0:1],
                                     accum_out=s_sum2[par][:, 0:1])
                        p.inc(s.activation(dump[:], xn_f[par][:], AF.Square,
                                           accum_out=s_sq2[par][:, 0:1]), "a")
                p.sec("scalar", a4)
                a_n4 = p.cnt["a"]

                def v4(v, pair=pair, a_n=a_n4, gn=(g_xn_of[0], g_xn_of[1])):
                    for w, i in enumerate(pair):
                        par = i % 2
                        v.wait_ge(p.s_a, a_n - (1 - w))
                        if gn[par]:
                            v.wait_ge(p.s_g, gn[par])
                        v.tensor_scalar(m2neg[par][:], s_sum2[par][:], -1.0 / D, None, op0=ALU.mult)
                        v.tensor_scalar(e2eps2[par][:], s_sq2[par][:], 1.0 / D, 1e-8,
                                        op0=ALU.mult, op1=ALU.add)
                        p.inc(v.tensor_copy(xn_bf[par][:], xn_f[par][:]), "v")
                p.sec("vector", v4)
                v_n4 = p.cnt["v"]

                def gs(g, pair=pair, v_n=v_n4):
                    for w, i in enumerate(pair):
                        par = i % 2
                        g.wait_ge(p.s_v, v_n - (1 - w))
                        p.inc(g.dma_start(out=p.cc_xn[i * 128:(i + 1) * 128, :],
                                          in_=xn_bf[par][:]), "g")
                p.sec("gpsimd", gs)
                g_xn_of[i0 % 2] = p.cnt["g"] - 16
                g_xn_of[(i0 + 1) % 2] = p.cnt["g"]

                def a5(s, pair=pair, v_n=v_n4):
                    for w, i in enumerate(pair):
                        par = i % 2
                        s.wait_ge(p.s_v, v_n - (1 - w))
                        s.activation(xc_f[par][:], xn_f[par][:], AF.Identity, bias=m2neg[par][:, 0:1])
                        p.inc(s.activation(msqn2[par][:], m2neg[par][:], AF.Square), "a")
                p.sec("scalar", a5)
                a_n5 = p.cnt["a"]

                def v5(v, pair=pair, a_n=a_n5):
                    for w, i in enumerate(pair):
                        par = i % 2
                        v.wait_ge(p.s_a, a_n - (1 - w))
                        p.inc(v.tensor_tensor(vare2[par][:], e2eps2[par][:], msqn2[par][:],
                                              op=ALU.subtract), "v")
                p.sec("vector", v5)
                v_n5 = p.cnt["v"]

                def a6(s, pair=pair, v_n=v_n5):
                    for w, i in enumerate(pair):
                        par = i % 2
                        s.wait_ge(p.s_v, v_n - (1 - w))
                        p.inc(s.activation(sqv2[par][:], vare2[par][:], AF.Sqrt), "a")
                p.sec("scalar", a6)
                a_n6 = p.cnt["a"]

                def v6(v, pair=pair, a_n=a_n6):
                    for w, i in enumerate(pair):
                        par = i % 2
                        v.wait_ge(p.s_a, a_n - (1 - w))
                        p.inc(v.reciprocal(rs2[par][:], sqv2[par][:]), "v")
                p.sec("vector", v6)
                v_n6 = p.cnt["v"]

                def a7(s, pair=pair, v_n=v_n6):
                    for w, i in enumerate(pair):
                        par = i % 2
                        s.wait_ge(p.s_v, v_n - (1 - w))
                        p.inc(s.activation(xnn_f[par][:], xc_f[par][:], AF.Identity,
                                           scale=rs2[par][:, 0:1]), "a")
                p.sec("scalar", a7)
                a_n7 = p.cnt["a"]

                def v7(v, pair=pair, a_n=a_n7):
                    for w, i in enumerate(pair):
                        par = i % 2
                        v.wait_ge(p.s_a, a_n - (1 - w))
                        v.reduce_max(amax[par][:, 0:1], xnn_f[par][:], axis=AX.X,
                                     apply_absolute_value=True)
                        v.tensor_scalar(vgap[par][:], amax[par][:], 1.0, None, op0=ALU.mult)
                        v.reciprocal(rtmp[par][:], amax[par][:])
                        v.tensor_scalar(invsq[par][:], amax[par][:], 1.0 / 127.0, None, op0=ALU.mult)
                        p.inc(v.tensor_scalar(sq[par][:], rtmp[par][:], 127.0, None, op0=ALU.mult), "v")
                p.sec("vector", v7)
                v_n7 = p.cnt["v"]

                def a8(s, pair=pair, v_n=v_n7):
                    for w, i in enumerate(pair):
                        par = i % 2
                        s.wait_ge(p.s_v, v_n - (1 - w))
                        p.inc(s.activation(f16q[par][:], xnn_f[par][:], AF.Identity,
                                           bias=p.b1536[:, 0:1], scale=sq[par][:, 0:1]), "a")
                p.sec("scalar", a8)
                a_n8 = p.cnt["a"]
                a_last[i0 % 2] = p.cnt["a"] - 1
                a_last[(i0 + 1) % 2] = p.cnt["a"]

                def v8(v, pair=pair, a_n=a_n8,
                       sn=(s_tr_of[i0 - 2] if i0 >= 2 else 0, s_tr_of[i0 - 1] if i0 >= 1 else 0)):
                    for w, i in enumerate(pair):
                        par = i % 2
                        v.wait_ge(p.s_a, a_n - (1 - w))
                        if sn[par]:
                            v.wait_ge(p.s_s, sn[par])
                        v.tensor_scalar(xq_f[par][:], f16q[par][:], -1536.0, invsq[par][:, 0:1],
                                        op0=ALU.add, op1=ALU.mult)
                        p.inc(v.tensor_copy(xq_bf[par][:], xq_f[par][:]), "v")
                p.sec("vector", v8)
                v_n8 = p.cnt["v"]

                def tr(sy, pair=pair, v_n=v_n8):
                    for w, i in enumerate(pair):
                        par = i % 2
                        sy.wait_ge(p.s_v, v_n - (1 - w))
                        for d in range(NDC):
                            p.inc(sy.dma_start(
                                out=p.xqT[:, d * R + i * 128: d * R + (i + 1) * 128],
                                in_=xq_bf[par][:, d * 128:(d + 1) * 128], transpose=True), "s")
                p.sec("sync", tr)
                s_tr_of[i0] = p.cnt["s"] - 16 * NDC
                s_tr_of[i0 + 1] = p.cnt["s"]
                p.flush()

            self.s_xqT = p.cnt["s"]
            g_all = p.cnt["g"]

            def coll(g, g_all=g_all):
                g.wait_ge(p.s_g, g_all)
                p.inc(g.collective_compute("AllGather", ALU.bypass, ins=[p.cc_xn],
                                           outs=[p.ag_xn],
                                           replica_groups=[list(range(NC))]), "c")
            p.sec("gpsimd", coll)
            self.c_xn = p.cnt["c"]
            self.barrier()

    def ffn(self):
        p, nc = self, self.nc
        with nc.psum_tensor("ps_lam", [128, 2], F32) as ps_lam:
            def alam(s, gl=self.g_lam):
                s.wait_ge(p.s_g, gl)
                p.inc(s.activation(p.lam_sig[:], p.lam_sb[:], AF.Sigmoid), "a")
            p.sec("scalar", alam)

            def tlam(t, a_lam=p.cnt["a"]):
                t.wait_ge(p.s_a, a_lam)
                p.inc(t.matmul(ps_lam[:], p.onesr_f[:], p.lam_sig[:], start=True, stop=True), "t")
            p.sec("tensor", tlam)

            def vlam(v, t_lam=p.cnt["t"]):
                v.wait_ge(p.s_t, t_lam)
                p.inc(v.tensor_scalar(p.lamneg[:], ps_lam[:], -1.0, None, op0=ALU.mult), "v")
            p.sec("vector", vlam)

        def wstage(g):
            p.inc(g.dma_start(out=p.wq_d, in_=p.wq), "g")
            p.inc(g.dma_start(out=p.wk_d, in_=p.wk), "g")
            p.inc(g.dma_start(out=p.wv_d, in_=p.wv), "g")
            p.inc(g.dma_start(out=p.msk_d.rearrange("b (a c) -> b a c", c=1024),
                              in_=p.msk.rearrange("a b c -> b a c")), "g")
            p.inc(g.dma_start(out=p.wob_d.rearrange("p (a b f) -> p a b f", b=4, f=512),
                              in_=p.wo.rearrange("a b p f -> p a b f")), "g")
        p.sec("gpsimd", wstage)
        self.g_stage = p.cnt["g"]

        with ExitStack() as es:
            E = es.enter_context
            slab = [E(nc.sbuf_tensor(f"wslab{i}", [128, 2048], F32)) for i in range(2)]
            sgW = [E(nc.sbuf_tensor(f"sgW{i}", [128, 2048], BF16)) for i in range(4)]
            sgm = E(nc.sbuf_tensor("sgm", [128, 1024], F32))
            sil = E(nc.sbuf_tensor("sil", [128, 1024], F32))
            pT = E(nc.sbuf_tensor("pT", [128, 16 * R], BF16))
            f_sb = E(nc.sbuf_tensor("f_sb", [128, NT * D], F32))
            wdf = [E(nc.sbuf_tensor(f"wdf{i}", [128, 2048], F32)) for i in range(2)]
            wdb = [E(nc.sbuf_tensor(f"wdb{i}", [128, 2048], BF16)) for i in range(2)]
            g_t = [E(nc.sbuf_tensor(f"g_t{i}", [128, D], BF16)) for i in range(2)]

            sign_a = 0
            sg_mm_t = [0, 0]
            a_w_of = {}
            evac_v = 0

            def emit_lw(hb):
                def lw(sy, hb=hb, fa=sign_a):
                    if fa:
                        sy.wait_ge(p.s_a, fa)
                    p.inc(sy.dma_start(out=slab[0][:], in_=p.wg[hb]), "s")
                    p.inc(sy.dma_start(out=slab[1][:], in_=p.wu[hb]), "s")
                p.sec("sync", lw)
                return p.cnt["s"]

            for q in range(4):
                with nc.psum_tensor(f"ps_yga{q}", [128, 1024], F32) as ps_yga, \
                     nc.psum_tensor(f"ps_yua{q}", [128, 1024], F32) as ps_yua, \
                     nc.psum_tensor(f"ps_ygb{q}", [128, 1024], F32) as ps_ygb, \
                     nc.psum_tensor(f"ps_yub{q}", [128, 1024], F32) as ps_yub:
                    evac_pair = [0, 0]
                    if q == 0:
                        s_w0 = emit_lw(0)

                        def sw0(sct, s_w0=s_w0):
                            sct.wait_ge(p.s_s, s_w0)
                            sct.activation(sgW[0][:], slab[0][:], AF.Sign)
                            p.inc(sct.activation(sgW[1][:], slab[1][:], AF.Sign), "a")
                        p.sec("scalar", sw0)
                        a_w_of[0] = p.cnt["a"]
                        sign_a = p.cnt["a"]
                    for jb in range(16):
                        hb = q * 16 + jb
                        par = hb % 2
                        ps_yg = ps_yga if par == 0 else ps_ygb
                        ps_yu = ps_yua if par == 0 else ps_yub

                        s_wn = emit_lw(hb + 1) if hb + 1 < 64 else 0

                        def mm(t, par=par, a_w=a_w_of[hb], ev=evac_pair[par],
                               s_xqT=self.s_xqT, first=(q == 0 and jb == 0),
                               ev0=(evac_v if jb <= 1 else 0)):
                            if first:
                                t.wait_ge(p.s_s, s_xqT)
                            t.wait_ge(p.s_a, a_w)
                            if ev:
                                t.wait_ge(p.s_v, ev)
                            elif ev0:
                                t.wait_ge(p.s_v, ev0)
                            mmi = None
                            for which in range(2):
                                dst = ps_yg if which == 0 else ps_yu
                                src = sgW[2 * par + which]
                                for th in range(2):
                                    for j in range(NDC):
                                        mmi = t.matmul(dst[:, th * 512:(th + 1) * 512],
                                                       src[:, j * 128:(j + 1) * 128],
                                                       p.xqT[:, j * R + th * 512: j * R + (th + 1) * 512],
                                                       start=(j == 0), stop=(j == NDC - 1),
                                                       skip_group_check=True)
                            p.inc(mmi, "t")
                        p.sec("tensor", mm)
                        t_mm = p.cnt["t"]
                        sg_mm_t[par] = t_mm

                        def ev_a(sct, hb=hb, s_wn=s_wn, t_mm=t_mm, v_prev=evac_v,
                                 npar=(hb + 1) % 2, ftn=sg_mm_t[(hb + 1) % 2]):
                            if s_wn:
                                sct.wait_ge(p.s_s, s_wn)
                                if ftn:
                                    sct.wait_ge(p.s_t, ftn)
                                sct.activation(sgW[2 * npar][:], slab[0][:], AF.Sign)
                                p.inc(sct.activation(sgW[2 * npar + 1][:], slab[1][:],
                                                     AF.Sign), "a")
                            sct.wait_ge(p.s_t, t_mm)
                            if v_prev:
                                sct.wait_ge(p.s_v, v_prev)
                            p.inc(sct.activation(sgm[:], ps_yg[:], AF.Sigmoid), "a")
                        p.sec("scalar", ev_a)
                        if hb + 1 < 64:
                            a_w_of[hb + 1] = p.cnt["a"] - 1
                            sign_a = p.cnt["a"] - 1

                        def ev_v(v, jb=jb, a_sg=p.cnt["a"], t_mm=t_mm):
                            v.wait_ge(p.s_a, a_sg)
                            v.tensor_tensor(sil[:], ps_yg[:], sgm[:], op=ALU.mult)
                            p.inc(v.tensor_tensor(pT[:, jb * R:(jb + 1) * R], sil[:], ps_yu[:],
                                                  op=ALU.mult), "v")
                        p.sec("vector", ev_v)
                        evac_v = p.cnt["v"]

                v_pT = p.cnt["v"]
                if q == 0:
                    wd_sign_a = [0, 0]
                    wd_mm_t = [0, 0]
                with nc.psum_tensor(f"ps_f{q}", [128, 8 * 512], F32) as ps_f:
                    bank_ev = [0] * 8
                    for dt in range(4):
                        for jh in range(2):
                            g_wd_of = [0, 0]
                            for par in range(2):
                                jq = 2 * jh + par

                                def lwd(g, dt=dt, hb0=q * 16 + jq * 4, par=par,
                                        fa=wd_sign_a[par]):
                                    if fa:
                                        g.wait_ge(p.s_a, fa)
                                    p.inc(g.dma_start(
                                        out=wdf[par][:].rearrange("p (c f) -> p c f", f=512),
                                        in_=p.wd[dt, hb0:hb0 + 4].rearrange("c p f -> p c f")), "g")
                                p.sec("gpsimd", lwd)
                                g_wd_of[par] = p.cnt["g"]

                            def swd2(sct, g_wd_of=list(g_wd_of),
                                     fts=(wd_mm_t[0], wd_mm_t[1])):
                                for par in range(2):
                                    sct.wait_ge(p.s_g, g_wd_of[par])
                                    if fts[par]:
                                        sct.wait_ge(p.s_t, fts[par])
                                    p.inc(sct.activation(wdb[par][:], wdf[par][:],
                                                         AF.Sign), "a")
                            p.sec("scalar", swd2)
                            wd_sign_a[0] = p.cnt["a"] - 1
                            wd_sign_a[1] = p.cnt["a"]

                            for par in range(2):
                                jq = 2 * jh + par
                                a_wd = p.cnt["a"] - 1 + par

                                def mm3(t, jq=jq, par=par, a_wd=a_wd, v_pT=v_pT,
                                        ev=(list(bank_ev) if jq == 0 else None)):
                                    t.wait_ge(p.s_a, a_wd)
                                    t.wait_ge(p.s_v, v_pT)
                                    mmi = None
                                    for tb in range(8):
                                        if ev and ev[tb]:
                                            t.wait_ge(p.s_v, ev[tb])
                                        for jc in range(4):
                                            j = jq * 4 + jc
                                            mmi = t.matmul(ps_f[:, tb * 512:(tb + 1) * 512],
                                                           pT[:, j * R + tb * 128: j * R + tb * 128 + 128],
                                                           wdb[par][:, jc * 512:(jc + 1) * 512],
                                                           start=(j == 0), stop=(j == 15),
                                                           skip_group_check=True)
                                    p.inc(mmi, "t")
                                p.sec("tensor", mm3)
                                wd_mm_t[par] = p.cnt["t"]
                        t_dt = p.cnt["t"]

                        def ev3(v, q=q, dt=dt, t_dt=t_dt):
                            v.wait_ge(p.s_t, t_dt)
                            ins = None
                            for tb in range(8):
                                src = ps_f[:, tb * 512:(tb + 1) * 512]
                                dst = f_sb[:, tb * D + dt * 512: tb * D + (dt + 1) * 512]
                                if q == 0:
                                    ins = v.tensor_copy(dst, src)
                                else:
                                    ins = v.tensor_tensor(dst, dst, src, op=ALU.add)
                            p.inc(ins, "v")
                        p.sec("vector", ev3)
                        for tb in range(8):
                            bank_ev[tb] = p.cnt["v"]
                evac_v = p.cnt["v"]

            v_f = p.cnt["v"]
            g_store = [0, 0]
            for tb in range(NT):
                par = tb % 2

                def gsig(s, tb=tb, v_f=v_f):
                    if tb == 0:
                        s.wait_ge(p.s_v, v_f)
                    p.inc(s.activation(sgm[:], f_sb[:, tb * D: tb * D + 1024], AF.Sigmoid), "a")
                    p.inc(s.activation(sil[:], f_sb[:, tb * D + 1024: tb * D + 2048],
                                       AF.Sigmoid), "a")
                p.sec("scalar", gsig)

                def gmul(v, tb=tb, par=par, a_g=p.cnt["a"], g_need=g_store[par]):
                    v.wait_ge(p.s_a, a_g)
                    if g_need:
                        v.wait_ge(p.s_g, g_need)
                    v.tensor_tensor(g_t[par][:, 0:1024],
                                    f_sb[:, tb * D: tb * D + 1024], sgm[:], op=ALU.mult)
                    p.inc(v.tensor_tensor(g_t[par][:, 1024:2048],
                                          f_sb[:, tb * D + 1024: tb * D + 2048], sil[:],
                                          op=ALU.mult), "v")
                p.sec("vector", gmul)

                def gst(g, tb=tb, par=par, v_g=p.cnt["v"]):
                    g.wait_ge(p.s_v, v_g)
                    p.inc(g.dma_start(out=p.cc_g[tb * 128:(tb + 1) * 128, :], in_=g_t[par][:]), "g")
                p.sec("gpsimd", gst)
                g_store[par] = p.cnt["g"]
            self.barrier()

    def attention(self):
        p, nc = self, self.nc
        with ExitStack() as es:
            E = es.enter_context
            wqb = E(nc.sbuf_tensor("wqb", [128, 4096], BF16))
            wkb = E(nc.sbuf_tensor("wkb", [128, 4096], BF16))
            wvb = E(nc.sbuf_tensor("wvb", [128, 4096], BF16))
            mask_bf = E(nc.sbuf_tensor("mask_bf", [128, 4096], BF16))
            qT = E(nc.sbuf_tensor("qT", [128, 2 * T], BF16))
            kT = E(nc.sbuf_tensor("kT", [128, 2 * T], BF16))
            vT = E(nc.sbuf_tensor("vT", [128, 2 * T], BF16))
            vtok = E(nc.sbuf_tensor("vtok", [128, 2 * T], BF16))
            xnT = [E(nc.sbuf_tensor(f"xnT{i}", [128, NDC * 512], BF16)) for i in range(2)]
            strip = E(nc.sbuf_tensor("strip", [128, 4 * 1024], BF16))
            mtmp = [E(nc.sbuf_tensor(f"mtmp{i}", [128, 1024], BF16)) for i in range(2)]
            rl = E(nc.sbuf_tensor("rl", [1, 1024], F32))
            rl_bf = E(nc.sbuf_tensor("rl_bf", [1, 1024], BF16))
            Bsb = E(nc.sbuf_tensor("Bsb", [128, 1024], F32))
            Osb = E(nc.sbuf_tensor("Osb", [128, 1024], F32))
            cmb = E(nc.sbuf_tensor("cmb", [128, 1024], F32))
            ocm = E(nc.sbuf_tensor("ocm", [128, T], BF16))
            self.wob = E(nc.sbuf_tensor("wob", [128, 2 * 4 * 512], BF16))
            self.oc = [E(nc.sbuf_tensor(f"oc{i}", [128, 2 * 1024], BF16)) for i in range(4)]
            self.s_sb = [E(nc.sbuf_tensor(f"ssb{i}", [128, D], BF16)) for i in range(2)]
            self.wo_st = {"oc_mm_t": [0, 0, 0, 0], "sb_dma_g": [0, 0],
                          "bank_ev": [0, 0, 0, 0], "c_rs": [0, 0]}
            g_ocm_of = [0, 0]

            def gw(sy, g_stage=self.g_stage):
                sy.wait_ge(p.s_g, g_stage)
                p.inc(sy.dma_start(out=wqb[:], in_=p.wq_d), "s")
                p.inc(sy.dma_start(out=wkb[:], in_=p.wk_d), "s")
                p.inc(sy.dma_start(out=wvb[:], in_=p.wv_d), "s")
                p.inc(sy.dma_start(out=mask_bf[:], in_=p.msk_d), "s")
                p.inc(sy.dma_start(out=self.wob[:], in_=p.wob_d), "s")
            p.sec("sync", gw)
            g_w = p.cnt["s"]
            self.g_wo = p.cnt["s"]

            agxn = p.ag_xn.rearrange("a b c -> (a b) c")
            xnT_mm_t = [0, 0]
            g_ocm = 0
            for b in range(2):
                with nc.psum_tensor(f"ps_qkv{b}", [128, 4 * 512], F32) as ps4:
                    slot_ev = [0, 0, 0, 0]
                    for ts in range(8):
                        par = ts % 2

                        def ltr(sy, b=b, ts=ts, par=par, t_need=xnT_mm_t[par],
                                c_xn=self.c_xn):
                            sy.wait_ge(p.s_c, c_xn)
                            if t_need:
                                sy.wait_ge(p.s_t, t_need)
                            base = ts * 1024 + b * 512
                            for d in range(NDC):
                                p.inc(sy.dma_start(
                                    out=xnT[par][:, d * 512:(d + 1) * 512],
                                    in_=agxn[base:base + 512, d * 128:(d + 1) * 128],
                                    transpose=True), "s")
                        p.sec("sync", ltr)
                        s_tr = p.cnt["s"]

                        t_base = p.cnt["t"]
                        v_base = p.cnt["v"]

                        def mmqkv(t, par=par, s_tr=s_tr, g_w=g_w, ev=list(slot_ev),
                                  v_base=v_base, first=(ts == 0),
                                  a_pre=p.cnt["a"], v_pre=p.cnt["v"]):
                            t.wait_ge(p.s_s, max(s_tr, g_w))
                            if first:
                                t.wait_ge(p.s_a, a_pre)
                                t.wait_ge(p.s_v, v_pre)
                            k = 0
                            for hb in range(2):
                                for wsl in (wqb, wkb, wvb):
                                    pi = k % 4
                                    if k >= 4:
                                        t.wait_ge(p.s_v, v_base + (k - 4) + 1)
                                    elif ev[pi]:
                                        t.wait_ge(p.s_v, ev[pi])
                                    mmi = None
                                    for j in range(NDC):
                                        mmi = t.matmul(
                                            ps4[:, pi * 512:(pi + 1) * 512],
                                            wsl[:, (hb * 16 + j) * 128:(hb * 16 + j + 1) * 128],
                                            xnT[par][:, j * 512:(j + 1) * 512],
                                            start=(j == 0), stop=(j == NDC - 1),
                                            skip_group_check=True)
                                    p.inc(mmi, "t")
                                    k += 1
                        p.sec("tensor", mmqkv)
                        xnT_mm_t[par] = p.cnt["t"]

                        def evqkv(v, ts=ts, t_base=t_base):
                            k = 0
                            for hb in range(2):
                                for dst in (qT, kT, vT):
                                    pi = k % 4
                                    v.wait_ge(p.s_t, t_base + k + 1)
                                    p.inc(v.tensor_copy(
                                        dst[:, hb * T + ts * 512:(hb * T + (ts + 1) * 512)],
                                        ps4[:, pi * 512:(pi + 1) * 512]), "v")
                                    k += 1
                        p.sec("vector", evqkv)
                        for k in range(4):
                            slot_ev[k] = p.cnt["v"]
                        p.flush()

                def vtr(sy, v_q=p.cnt["v"], t_pre=p.cnt["t"]):
                    sy.wait_ge(p.s_v, v_q)
                    sy.wait_ge(p.s_t, t_pre)
                    for hb in range(2):
                        for sc in range(32):
                            p.inc(sy.dma_start(
                                out=vtok[:, hb * T + sc * 128: hb * T + (sc + 1) * 128],
                                in_=vT[:, hb * T + sc * 128: hb * T + (sc + 1) * 128],
                                transpose=True), "s")
                p.sec("sync", vtr)
                s_vt = p.cnt["s"]
                if b == 1:
                    self.wo_batch(0, g_ocm_of[0], p.cnt["v"])

                for hb in range(2):
                    with nc.psum_tensor(f"ps_s{b}{hb}", [128, 2 * 1024], F32) as ps_s, \
                         nc.psum_tensor(f"ps_O1{b}{hb}", [128, 512], F32) as ps_O1, \
                         nc.psum_tensor(f"ps_O2{b}{hb}", [128, 512], F32) as ps_O2, \
                         nc.psum_tensor(f"ps_L1{b}{hb}", [128, 512], F32) as ps_L1, \
                         nc.psum_tensor(f"ps_L2{b}{hb}", [128, 512], F32) as ps_L2:
                        exp_a_par = [0, 0]
                        bs_v_par = [0, 0]
                        o_free_v = 0
                        slot_t = [0, 0, 0, 0]
                        mtmp_v = [0, 0]
                        pending = None
                        for tt in range(8):
                            ns = 4 * (tt + 1)
                            npair = ns // 2
                            t0 = p.cnt["t"]
                            has_p = pending is not None

                            if has_p:
                                t_end_p, tt_p, g_need_p = pending

                                def pvec(v, t_end_p=t_end_p, g_need=g_need_p,
                                         hb=hb, tt_p=tt_p):
                                    v.wait_ge(p.s_t, t_end_p)
                                    if g_need:
                                        v.wait_ge(p.s_g, g_need)
                                    v.reciprocal(Bsb[:, 0:512], ps_L1[:, :])
                                    p.inc(v.reciprocal(Bsb[:, 512:1024],
                                                       ps_L2[:, :]), "v")
                                    v.tensor_tensor(cmb[:, 0:512], Osb[:, 0:512],
                                                    Bsb[:, 0:512], op=ALU.mult)
                                    v.scalar_tensor_tensor(
                                        cmb[:, 512:1024], Osb[:, 512:1024],
                                        p.lamneg[:, hb:hb + 1], Bsb[:, 512:1024],
                                        op0=ALU.mult, op1=ALU.mult)
                                    p.inc(v.tensor_tensor(
                                        ocm[:, tt_p * 512:(tt_p + 1) * 512],
                                        cmb[:, 0:512], cmb[:, 512:1024],
                                        op=ALU.add), "v")
                                p.sec("vector", pvec)
                                bs_v = p.cnt["v"] - 1
                            else:
                                bs_v = 0

                            a0 = p.cnt["a"]
                            v0m = p.cnt["v"]

                            def a_pred(idx, w):
                                return a0 + 2 * idx + w + 1

                            def m_pred(k):
                                return v0m + k + 1

                            slot_snap = {}
                            t_of = {}
                            sim_t = t0
                            fw = (p.cnt["a"], p.cnt["v"]) if tt == 0 else None
                            an0 = (exp_a_par[0], exp_a_par[1])
                            o_need0 = o_free_v
                            mv_prev = tuple(mtmp_v)

                            def tsec(t, tt=tt, hb=hb, npair=npair, t0=t0,
                                     fw=fw, an0=an0, o_need0=o_need0, bs_v=bs_v,
                                     s_vt=s_vt, a0=a0, v0m=v0m):
                                for idx in range(npair):
                                    si0 = 2 * idx
                                    if idx == 0 and fw is not None:
                                        t.wait_ge(p.s_s, s_vt)
                                        t.wait_ge(p.s_a, fw[0])
                                        t.wait_ge(p.s_v, fw[1])
                                    for w, si in enumerate((si0, si0 + 1)):
                                        if idx == 0:
                                            if an0[w]:
                                                t.wait_ge(p.s_a, an0[w])
                                        else:
                                            t.wait_ge(p.s_a, a0 + 2 * (idx - 1) + w + 1)
                                        par = si % 2
                                        t.matmul(ps_s[:, par * 1024: par * 1024 + 512],
                                                 kT[0:64, hb * T + si * 128: hb * T + (si + 1) * 128],
                                                 qT[0:64, hb * T + tt * 512: hb * T + (tt + 1) * 512],
                                                 start=True, stop=True, tile_position=(0, 0))
                                        p.inc(t.matmul(
                                            ps_s[:, par * 1024 + 512: par * 1024 + 1024],
                                            kT[64:128, hb * T + si * 128: hb * T + (si + 1) * 128],
                                            qT[64:128, hb * T + tt * 512: hb * T + (tt + 1) * 512],
                                            start=True, stop=True, tile_position=(64, 0)), "t")
                                    diag = si0 >= 4 * tt
                                    for w, si in enumerate((si0, si0 + 1)):
                                        slot = si % 4
                                        st_ = idx == 0 and w == 0
                                        sp_ = idx == npair - 1 and w == 1
                                        if diag:
                                            k = (si - 4 * tt)
                                            t.wait_ge(p.s_v, v0m + k + 1)
                                        else:
                                            t.wait_ge(p.s_a, a0 + 2 * idx + w + 1)
                                        if st_:
                                            if o_need0:
                                                t.wait_ge(p.s_v, o_need0)
                                            if bs_v:
                                                t.wait_ge(p.s_v, bs_v)
                                        t.matmul(ps_L1[:, :], p.ones_mat[:, :],
                                                 strip[:, slot * 1024: slot * 1024 + 512],
                                                 start=st_, stop=sp_,
                                                 skip_group_check=True)
                                        t.matmul(ps_L2[:, :], p.ones_mat[:, :],
                                                 strip[:, slot * 1024 + 512: (slot + 1) * 1024],
                                                 start=st_, stop=sp_,
                                                 skip_group_check=True)
                                        t.matmul(ps_O1[:],
                                                 vtok[:, hb * T + si * 128: hb * T + (si + 1) * 128],
                                                 strip[:, slot * 1024: slot * 1024 + 512],
                                                 start=st_, stop=sp_,
                                                 skip_group_check=True)
                                        p.inc(t.matmul(ps_O2[:],
                                                       vtok[:, hb * T + si * 128: hb * T + (si + 1) * 128],
                                                       strip[:, slot * 1024 + 512:(slot + 1) * 1024],
                                                       start=st_, stop=sp_,
                                                       skip_group_check=True), "t")
                            for idx in range(npair):
                                si0 = 2 * idx
                                sim_t += 2
                                t_of[si0] = sim_t - 1
                                t_of[si0 + 1] = sim_t
                                slot_snap[si0] = slot_t[si0 % 4]
                                slot_snap[si0 + 1] = slot_t[(si0 + 1) % 4]
                                sim_t += 2
                                slot_t[si0 % 4] = sim_t - 1
                                slot_t[(si0 + 1) % 4] = sim_t
                            p.sec("tensor", tsec)
                            assert p.cnt["t"] == sim_t, (p.cnt["t"], sim_t)
                            t_end = sim_t
                            exp_a_par[0] = a_pred(npair - 1, 0)
                            exp_a_par[1] = a_pred(npair - 1, 1)

                            def asec(sct, tt=tt, npair=npair, t_of=dict(t_of),
                                     slot_snap=dict(slot_snap), mv_prev=mv_prev,
                                     v0m=v0m):
                                for idx in range(npair):
                                    si0 = 2 * idx
                                    for w, si in enumerate((si0, si0 + 1)):
                                        par = si % 2
                                        slot = si % 4
                                        diag = si >= 4 * tt
                                        sct.wait_ge(p.s_t, max(t_of[si], slot_snap[si]))
                                        if diag:
                                            k = si - 4 * tt
                                            if k < 2:
                                                if mv_prev[si % 2]:
                                                    sct.wait_ge(p.s_v, mv_prev[si % 2])
                                            else:
                                                sct.wait_ge(p.s_v, v0m + k - 1)
                                        dst = (mtmp[si % 2][:] if diag
                                               else strip[:, slot * 1024:(slot + 1) * 1024])
                                        p.inc(sct.activation(
                                            dst, ps_s[:, par * 1024:(par + 1) * 1024],
                                            AF.Exp, scale=SCALE), "a")
                            p.sec("scalar", asec)
                            assert p.cnt["a"] == a0 + ns, (p.cnt["a"], a0, ns)

                            def msec(v, tt=tt, a0=a0, ns=ns,
                                     sl=(slot_t[0], slot_t[1], slot_t[2], slot_t[3]),
                                     slot_snap=dict(slot_snap)):
                                for k in range(4):
                                    si = 4 * tt + k
                                    slot = si % 4
                                    v.wait_ge(p.s_a, a0 + ns - 4 + k + 1)
                                    tsl = slot_snap[si]
                                    if tsl:
                                        v.wait_ge(p.s_t, tsl)
                                    p.inc(v.tensor_tensor(
                                        strip[:, slot * 1024:(slot + 1) * 1024],
                                        mtmp[si % 2][:],
                                        mask_bf[:, k * 1024:(k + 1) * 1024],
                                        op=ALU.mult), "v")
                            p.sec("vector", msec)
                            assert p.cnt["v"] == v0m + 4, (p.cnt["v"], v0m)
                            mtmp_v[0] = v0m + 3
                            mtmp_v[1] = v0m + 4

                            def vfl(v, t_end=t_end):
                                v.wait_ge(p.s_t, t_end)
                                v.tensor_copy(Osb[:, 0:512], ps_O1[:])
                                p.inc(v.tensor_copy(Osb[:, 512:1024], ps_O2[:]), "v")
                            p.sec("vector", vfl)
                            o_free_v = p.cnt["v"]
                            pending = (t_end, tt, (g_ocm if tt == 0 else 0))

                        t_end_p, tt_p, g_need_p = pending

                        def fvec(v, t_end_p=t_end_p, g_need=g_need_p, hb=hb,
                                 tt_p=tt_p):
                            v.wait_ge(p.s_t, t_end_p)
                            if g_need:
                                v.wait_ge(p.s_g, g_need)
                            v.reciprocal(Bsb[:, 0:512], ps_L1[:, :])
                            v.reciprocal(Bsb[:, 512:1024], ps_L2[:, :])
                            v.tensor_tensor(cmb[:, 0:512], Osb[:, 0:512],
                                            Bsb[:, 0:512], op=ALU.mult)
                            v.scalar_tensor_tensor(
                                cmb[:, 512:1024], Osb[:, 512:1024],
                                p.lamneg[:, hb:hb + 1], Bsb[:, 512:1024],
                                op0=ALU.mult, op1=ALU.mult)
                            p.inc(v.tensor_tensor(
                                ocm[:, tt_p * 512:(tt_p + 1) * 512],
                                cmb[:, 0:512], cmb[:, 512:1024], op=ALU.add), "v")
                        p.sec("vector", fvec)
                        pending = None


                    def soc(g, b=b, hb=hb, v_oc=p.cnt["v"]):
                        g.wait_ge(p.s_v, v_oc)
                        p.inc(g.dma_start(out=p.cc_o[b, hb], in_=ocm[:]), "g")
                    p.sec("gpsimd", soc)
                    g_ocm = p.cnt["g"]
                g_ocm_of[b] = g_ocm
            self.wo_batch(1, g_ocm_of[1], p.cnt["v"])
            self.barrier()

    def wo_batch(self, b, g_ocm_b, v_pre):
        p, nc = self, self.nc
        st = self.wo_st
        wob, oc, s_sb = self.wob, self.oc, self.s_sb
        with nc.psum_tensor(f"ps_sp{b}", [128, 4 * 512], F32) as ps_sp:
            g_oc_of = [0] * 4
            for tq in range(4):
                def loc(g, b=b, tq=tq, t_need=st["oc_mm_t"][tq],
                        g_ocm_b=(g_ocm_b if tq == 0 else 0)):
                    if g_ocm_b:
                        g.wait_ge(p.s_g, g_ocm_b)
                    if t_need:
                        g.wait_ge(p.s_t, t_need)
                    for hb in range(2):
                        p.inc(g.dma_start(
                            out=oc[tq][:, hb * 1024:(hb + 1) * 1024],
                            in_=p.cc_o[b, hb][:, tq * 1024:(tq + 1) * 1024]), "g")
                p.sec("gpsimd", loc)
                g_oc_of[tq] = p.cnt["g"]

            for tq in range(4):
                par = tq % 2
                for tb in range(8):
                    def mmsp(t, tq=tq, tb=tb, g_oc=g_oc_of[tq], g_wo=self.g_wo,
                             ev=list(st["bank_ev"]),
                             v_pre=(v_pre if tq == 0 and tb == 0 else 0)):
                        t.wait_ge(p.s_g, g_oc)
                        t.wait_ge(p.s_s, g_wo)
                        if v_pre:
                            t.wait_ge(p.s_v, v_pre)
                        for dt in range(4):
                            if ev[dt]:
                                t.wait_ge(p.s_v, ev[dt])
                            mmi = None
                            for hb in range(2):
                                mmi = t.matmul(
                                    ps_sp[:, dt * 512:(dt + 1) * 512],
                                    oc[tq][:, hb * 1024 + tb * 128:
                                            hb * 1024 + (tb + 1) * 128],
                                    wob[:, (hb * 4 + dt) * 512:(hb * 4 + dt + 1) * 512],
                                    start=(hb == 0), stop=(hb == 1),
                                    skip_group_check=True)
                            p.inc(mmi, "t")
                    p.sec("tensor", mmsp)
                    t_sp = p.cnt["t"]

                    def evd(v, par=par, t_sp=t_sp, g_need=st["sb_dma_g"][par]):
                        if g_need:
                            v.wait_ge(p.s_g, g_need)
                        ins = None
                        for dt in range(4):
                            v.wait_ge(p.s_t, t_sp - 3 + dt)
                            ins = v.tensor_copy(s_sb[par][:, dt * 512:(dt + 1) * 512],
                                                ps_sp[:, dt * 512:(dt + 1) * 512])
                        p.inc(ins, "v")
                    p.sec("vector", evd)
                    for dt in range(4):
                        st["bank_ev"][dt] = p.cnt["v"]

                    def ssb(g, b=b, tq=tq, tb=tb, par=par, v_d=p.cnt["v"]):
                        g.wait_ge(p.s_v, v_d)
                        row = tq * 1024 + tb * 128
                        p.inc(g.dma_start(out=p.cc_s[b][row:row + 128, :],
                                          in_=s_sb[par][:]), "g")
                    p.sec("gpsimd", ssb)
                    st["sb_dma_g"][par] = p.cnt["g"]
                st["oc_mm_t"][tq] = p.cnt["t"]
                p.flush()

        g_all = p.cnt["g"]

        def coll2(g, g_all=g_all, b=b):
            g.wait_ge(p.s_g, g_all)
            p.inc(g.collective_compute("ReduceScatter", ALU.add, ins=[p.cc_s[b]],
                                       outs=[p.rs_s[b]],
                                       replica_groups=[list(range(NC))]), "c")
        p.sec("gpsimd", coll2)
        st["c_rs"][b] = p.cnt["c"]

    def wo_final(self):
        p, nc = self, self.nc
        with ExitStack() as es:
            E = es.enter_context
            x_f = [E(nc.sbuf_tensor(f"xr{i}", [128, D], F32)) for i in range(2)]
            st_bf = [E(nc.sbuf_tensor(f"stb{i}", [128, D], BF16)) for i in range(2)]
            gt_bf = [E(nc.sbuf_tensor(f"gtb{i}", [128, D], BF16)) for i in range(2)]
            s_f = E(nc.sbuf_tensor("s_fin", [128, D], F32))
            gg_f = E(nc.sbuf_tensor("gg_f", [128, D], F32))
            y_sb = [E(nc.sbuf_tensor(f"ysb{i}", [128, D], F32)) for i in range(2)]
            v_done = [0, 0]
            g_out_of = [0, 0]
            for tb in range(NT):
                par = tb % 2

                half = 0 if tb < 4 else 1
                rs_src = p.rs_s[half][(tb - 4 * half) * 128:(tb - 4 * half + 1) * 128, :]

                def lfin(g, tb=tb, par=par, v_need=v_done[par],
                         c_rs=self.wo_st["c_rs"][half], rs_src=rs_src):
                    g.wait_ge(p.s_c, c_rs)
                    if v_need:
                        g.wait_ge(p.s_v, v_need)
                    p.inc(g.dma_start(out=x_f[par][:], in_=p.x[tb * 128:(tb + 1) * 128, :]), "g")
                    p.inc(g.dma_start(out=st_bf[par][:], in_=rs_src), "g")
                    p.inc(g.dma_start(out=gt_bf[par][:],
                                      in_=p.cc_g[tb * 128:(tb + 1) * 128, :]), "g")
                p.sec("gpsimd", lfin)

                def vfin(v, tb=tb, par=par, g_n=p.cnt["g"], g_y=g_out_of[par]):
                    v.wait_ge(p.s_g, max(g_n, g_y))
                    v.tensor_copy(s_f[:], st_bf[par][:])
                    v.tensor_copy(gg_f[:], gt_bf[par][:])
                    v.tensor_tensor(y_sb[par][:], s_f[:], gg_f[:], op=ALU.mult)
                    p.inc(v.tensor_tensor(y_sb[par][:], y_sb[par][:], x_f[par][:],
                                          op=ALU.add), "v")
                p.sec("vector", vfin)
                v_yt = p.cnt["v"]
                v_done[par] = v_yt

                def sout(g, tb=tb, par=par, v_yt=v_yt):
                    g.wait_ge(p.s_v, v_yt)
                    p.inc(g.dma_start(out=p.y[tb * 128:(tb + 1) * 128, :], in_=y_sb[par][:]), "g")
                p.sec("gpsimd", sout)
                g_out_of[par] = p.cnt["g"]
                if tb % 2 == 1:
                    p.flush()

            if DEBUG_TAPS:
                tapb = E(nc.sbuf_tensor("tapb", [128, D], BF16))
                for (src, dst, n) in ((p.cc_xn, p.tp_xn, NT),):
                    for i in range(n):
                        def tl(g, src=src, i=i, gn=p.cnt["g"]):
                            g.wait_ge(p.s_g, gn)
                            p.inc(g.dma_start(out=tapb[:], in_=src[i * 128:(i + 1) * 128, :]), "g")
                        p.sec("gpsimd", tl)
                        def ts_(g, dst=dst, i=i, gn=p.cnt["g"]):
                            g.wait_ge(p.s_g, gn)
                            p.inc(g.dma_start(out=dst[i * 128:(i + 1) * 128, :], in_=tapb[:]), "g")
                        p.sec("gpsimd", ts_)

            g_fin = p.cnt["g"]

            def fin(g, g_fin=g_fin):
                g.wait_ge(p.s_g, g_fin)
            p.sec("gpsimd", fin)

    def close(self):
        self.es.close()


def build_program():
    p = Prog()
    p.setup()
    with ExitStack() as es:
        p.xqT = es.enter_context(p.nc.sbuf_tensor("xqT", [128, NDC * R], BF16))
        p.phase1()
        p.ffn()
    p.attention()
    p.wo_final()
    p.flush()
    p.close()
    return p.nc


_masks_cache = None


def _masks():
    global _masks_cache
    if _masks_cache is None:
        m = np.zeros((4, 128, 1024), np.float32)
        pp = np.arange(128)[:, None]
        ff = np.arange(512)[None, :]
        for v in range(4):
            mv = (pp + v * 128 <= ff).astype(np.float32)
            m[v, :, 0:512] = mv
            m[v, :, 512:1024] = mv
        _masks_cache = m
    return _masks_cache


def kernel(L0_attn_x, L0_attn_Wq, L0_attn_Wk, L0_attn_Wv, L0_attn_Wo,
           L0_attn_lambda_init, L0_ffn_W_g, L0_ffn_W_u, L0_ffn_W_d):
    x = np.ascontiguousarray(np.asarray(L0_attn_x, np.float32)).reshape(B * T, D)
    Wq = np.asarray(L0_attn_Wq, np.float32)
    Wk = np.asarray(L0_attn_Wk, np.float32)
    Wv = np.asarray(L0_attn_Wv, np.float32)
    Wo = np.asarray(L0_attn_Wo, np.float32)
    lam = np.asarray(L0_attn_lambda_init, np.float32)
    Wg = np.asarray(L0_ffn_W_g, np.float32)
    Wu = np.asarray(L0_ffn_W_u, np.float32)
    Wd = np.asarray(L0_ffn_W_d, np.float32)

    def qkv_img(W, c):
        S = W[:, 256 * c:256 * (c + 1)]
        return np.ascontiguousarray(
            S.reshape(16, 128, 2, 128).transpose(1, 2, 0, 3).reshape(128, 4096))

    wg_img = np.ascontiguousarray(
        Wg.reshape(64, 128, 16, 128).transpose(0, 3, 2, 1).reshape(64, 128, 2048))
    wu_img = np.ascontiguousarray(
        Wu.reshape(64, 128, 16, 128).transpose(0, 3, 2, 1).reshape(64, 128, 2048))
    wd_img = np.ascontiguousarray(
        Wd.reshape(4, 512, 64, 128).transpose(0, 2, 3, 1))
    msk = _masks()

    in_maps = []
    for c in range(NC):
        wo_c = np.ascontiguousarray(
            Wo[256 * c:256 * (c + 1)].reshape(2, 128, 4, 512).transpose(0, 2, 1, 3))
        in_maps.append({
            "x": np.ascontiguousarray(np.concatenate(
                [x[c * 512:(c + 1) * 512], x[T + c * 512:T + (c + 1) * 512]])),
            "wq": qkv_img(Wq, c), "wk": qkv_img(Wk, c), "wv": qkv_img(Wv, c),
            "wo": wo_c, "wg": wg_img, "wu": wu_img, "wd": wd_img,
            "lam": np.ascontiguousarray(lam[2 * c:2 * c + 2].reshape(1, 2)),
            "msk": msk,
        })

    nc = build_program()
    res = run_bass_kernel_spmd(nc, in_maps, core_ids=list(range(NC)))
    out = np.empty((B * T, D), np.float32)
    for c in range(NC):
        yc = np.asarray(res.results[c]["y"], np.float32)
        out[c * 512:(c + 1) * 512] = yc[0:512]
        out[T + c * 512:T + (c + 1) * 512] = yc[512:1024]
    return out.reshape(B, T, D)


if __name__ == "__main__":
    import reference as ref
    ins = ref.setup_inputs(0)
    got = kernel(**{k: np.asarray(v) for k, v in ins.items()})
    exp = np.asarray(ref.reference(**ins))
    err = np.abs(got - exp)
    rv = float(np.mean((got - exp) ** 2) / np.mean(np.asarray(exp) ** 2))
    print("maxabs", err.max(), "resvar", rv)


# version 26
# speedup vs baseline: 1.0160x; 1.0160x over previous
import numpy as np
import concourse.bass as bass

DEBUG_TAPS = False
FLUSH_N = {"tensor": 1, "scalar": 1, "vector": 1, "sync": 1, "gpsimd": 1}

import concourse.mybir as mybir
from concourse.bass_utils import run_bass_kernel_spmd
from contextlib import ExitStack

F32 = mybir.dt.float32
BF16 = mybir.dt.bfloat16
FP16 = mybir.dt.float16
AF = mybir.ActivationFunctionType
ALU = mybir.AluOpType
AX = mybir.AxisListType

B, T, D = 2, 4096, 2048
H = 16
HF = 8192
NC = 8
R = (B * T) // NC
NT = R // 128
NDC = D // 128
SCALE = 0.125


class Prog:
    def __init__(self):
        nc = bass.Bass("TRN2", target_bir_lowering=False)
        self.nc = nc
        self.es = ExitStack()
        ap = lambda name, shape, dt, kind: nc.dram_tensor(name, shape, dt, kind=kind).ap()
        self.x = ap("x", [R, D], F32, "ExternalInput")
        self.wq = ap("wq", [128, 4096], F32, "ExternalInput")
        self.wk = ap("wk", [128, 4096], F32, "ExternalInput")
        self.wv = ap("wv", [128, 4096], F32, "ExternalInput")
        self.wo = ap("wo", [2, 4, 128, 512], F32, "ExternalInput")
        self.wg = ap("wg", [64, 128, 2048], F32, "ExternalInput")
        self.wu = ap("wu", [64, 128, 2048], F32, "ExternalInput")
        self.wd = ap("wd", [4, 64, 128, 512], F32, "ExternalInput")
        self.lam = ap("lam", [1, 2], F32, "ExternalInput")
        self.msk = ap("msk", [4, 128, 1024], F32, "ExternalInput")
        self.y = ap("y", [R, D], F32, "ExternalOutput")
        KI = "ExternalOutput" if DEBUG_TAPS else "Internal"
        aps = lambda name, shape, dt: nc.dram_tensor(name, shape, dt, kind="Internal",
                                                     addr_space="Shared").ap()
        self.cc_xn = ap("cc_xn", [R, D], BF16, "Internal")
        self.ag_xn = aps("ag_xn", [NC, R, D], BF16)
        self.cc_o = ap("cc_o", [2, 2, 128, T], BF16, KI)
        self.cc_s = [ap(f"cc_s{i}", [T, D], BF16, "Internal") for i in range(2)]
        self.cc_g = ap("cc_g", [R, D], BF16, KI)
        self.rs_s = [ap(f"rs_s{i}", [512, D], BF16, "Internal") for i in range(2)]
        self.wq_d = ap("wq_d", [128, 4096], BF16, "Internal")
        self.wk_d = ap("wk_d", [128, 4096], BF16, "Internal")
        self.wv_d = ap("wv_d", [128, 4096], BF16, "Internal")
        self.msk_d = ap("msk_d", [128, 4096], BF16, "Internal")
        self.wob_d = ap("wob_d", [128, 4096], BF16, "Internal")
        if DEBUG_TAPS:
            self.tp_xn = ap("tp_xn", [R, D], BF16, "ExternalOutput")
            self.tp_s = ap("tp_s", [NC * R, D], BF16, "ExternalOutput")
            self.tp_rs = ap("tp_rs", [R, D], BF16, "ExternalOutput")

        E = self.es.enter_context
        self.s_t = E(nc.semaphore("s_t"))
        self.s_a = E(nc.semaphore("s_a"))
        self.s_v = E(nc.semaphore("s_v"))
        self.s_s = E(nc.semaphore("s_s"))
        self.s_g = E(nc.semaphore("s_g"))
        self.s_c = E(nc.semaphore("s_c"))
        self.s_w = E(nc.semaphore("s_w"))
        self.cnt = {"t": 0, "a": 0, "v": 0, "s": 0, "g": 0, "c": 0, "w": 0}
        self.sem = {"t": self.s_t, "a": self.s_a, "v": self.s_v,
                    "s": self.s_s, "g": self.s_g, "c": self.s_c, "w": self.s_w}

        sb = lambda n, s, d: E(nc.sbuf_tensor(n, s, d))
        self.xqT = None
        self.ones_col = sb("ones_col", [128, 1], BF16)
        self.ones_mat = sb("ones_mat", [128, 128], BF16)
        self.ones_row = sb("ones_row", [1, 128], BF16)
        self.onesr_f = sb("onesr_f", [1, 128], F32)
        self.b1536 = sb("b1536", [128, 1], F32)
        self.lamneg = sb("lamneg", [128, 2], F32)
        self.lam_sig = sb("lam_sig", [1, 2], F32)
        self.lam_sb = sb("lam_sb", [1, 2], F32)

        self.block = E(nc.Block())
    def sec(self, eng, fn):
        getattr(self.block, eng)(fn)

    def flush(self):
        pass

    def inc(self, instr, key, by=None):
        by = by if by is not None else (16 if key in ("s", "g", "w") else 1)
        instr.then_inc(self.sem[key], by)
        self.cnt[key] += by
        return self.cnt[key]

    def barrier(self):
        self.flush()
        snap = dict(self.cnt)

        def mk():
            def f(e):
                for k in ("t", "a", "v", "s", "g", "w"):
                    if snap[k]:
                        e.wait_ge(self.sem[k], snap[k])
            return f
        for eng in ("tensor", "scalar", "vector", "sync", "gpsimd"):
            self.sec(eng, mk())

    def setup(self):
        p = self

        def vec(v):
            v.memset(p.ones_col[:], 1.0)
            v.memset(p.ones_mat[:], 1.0)
            v.memset(p.ones_row[:], 1.0)
            v.memset(p.onesr_f[:], 1.0)
            p.inc(v.memset(p.b1536[:], 1536.0), "v")
        p.sec("vector", vec)

        def gp(g):
            p.inc(g.dma_start(out=p.lam_sb[:], in_=p.lam), "g")
        p.sec("gpsimd", gp)
        self.g_lam = p.cnt["g"]

    def phase1(self):
        p, nc = self, self.nc
        with ExitStack() as es:
            E = es.enter_context
            x_f = [E(nc.sbuf_tensor(f"x_f{i}", [128, D], F32)) for i in range(2)]
            xc_f = [E(nc.sbuf_tensor(f"xc_f{i}", [128, D], F32)) for i in range(2)]
            xn_f = [E(nc.sbuf_tensor(f"xn_f{i}", [128, D], F32)) for i in range(2)]
            xnn_f = [E(nc.sbuf_tensor(f"xnn_f{i}", [128, D], F32)) for i in range(2)]
            xn_bf = [E(nc.sbuf_tensor(f"xn_bf{i}", [128, D], BF16)) for i in range(2)]
            xq_bf = [E(nc.sbuf_tensor(f"xq_bf{i}", [128, D], BF16)) for i in range(2)]
            f16q = [E(nc.sbuf_tensor(f"f16q{i}", [128, D], FP16)) for i in range(2)]
            xq_f = [E(nc.sbuf_tensor(f"xq_f{i}", [128, D], F32)) for i in range(2)]
            dump = E(nc.sbuf_tensor("dump", [128, D], F32))
            c2 = lambda n: [E(nc.sbuf_tensor(n + str(i), [128, 1], F32)) for i in range(2)]
            s_sum, s_sq, mneg, e2eps, msqn, vare, sqv, rs = (
                c2("s_sum"), c2("s_sq"), c2("mneg"), c2("e2eps"), c2("msqn"),
                c2("vare"), c2("sqv"), c2("rs"))
            s_sum2, s_sq2, m2neg, e2eps2, msqn2, vare2, sqv2, rs2 = (
                c2("s_sum2"), c2("s_sq2"), c2("m2neg"), c2("e2eps2"), c2("msqn2"),
                c2("vare2"), c2("sqv2"), c2("rs2"))
            amax, rtmp, sq, invsq, vgap = c2("amax"), c2("rtmp"), c2("sq"), c2("invsq"), c2("vgap")

            a_last = [0, 0]
            v1_of = [0] * (NT + 2)
            s_tr_of = [0] * NT
            g_xn_of = [0, 0]

            for i0 in range(0, NT, 2):
                pair = (i0, i0 + 1)

                def lx(sy, pair=pair, an=(a_last[0], a_last[1])):
                    for w, i in enumerate(pair):
                        if an[i % 2]:
                            sy.wait_ge(p.s_a, an[i % 2])
                        p.inc(sy.dma_start(out=x_f[i % 2][:], in_=p.x[i * 128:(i + 1) * 128, :]), "s")
                p.sec("sync", lx)
                s_x = p.cnt["s"]

                def a1(s, pair=pair, s_x=s_x,
                       vn=(v1_of[i0 - 2] if i0 >= 2 else 0, v1_of[i0 - 1] if i0 >= 1 else 0)):
                    for w, i in enumerate(pair):
                        par = i % 2
                        s.wait_ge(p.s_s, s_x - 16 * (1 - w))
                        if vn[par]:
                            s.wait_ge(p.s_v, vn[par])
                        s.activation(dump[:], x_f[par][:], AF.Identity, accum_out=s_sum[par][:, 0:1])
                        p.inc(s.activation(dump[:], x_f[par][:], AF.Square,
                                           accum_out=s_sq[par][:, 0:1]), "a")
                p.sec("scalar", a1)
                a_n1 = p.cnt["a"]

                def v1(v, pair=pair, a_n1=a_n1):
                    for w, i in enumerate(pair):
                        par = i % 2
                        v.wait_ge(p.s_a, a_n1 - (1 - w))
                        v.tensor_scalar(mneg[par][:], s_sum[par][:], -1.0 / D, None, op0=ALU.mult)
                        p.inc(v.tensor_scalar(e2eps[par][:], s_sq[par][:], 1.0 / D, 1e-5,
                                              op0=ALU.mult, op1=ALU.add), "v")
                p.sec("vector", v1)
                v1_of[i0] = p.cnt["v"] - 1
                v1_of[i0 + 1] = p.cnt["v"]
                v_n1 = p.cnt["v"]

                def a2(s, pair=pair, v_n=v_n1):
                    for w, i in enumerate(pair):
                        par = i % 2
                        s.wait_ge(p.s_v, v_n - (1 - w))
                        s.activation(xc_f[par][:], x_f[par][:], AF.Identity, bias=mneg[par][:, 0:1])
                        p.inc(s.activation(msqn[par][:], mneg[par][:], AF.Square), "a")
                p.sec("scalar", a2)
                a_n2 = p.cnt["a"]

                def v2(v, pair=pair, a_n=a_n2):
                    for w, i in enumerate(pair):
                        par = i % 2
                        v.wait_ge(p.s_a, a_n - (1 - w))
                        p.inc(v.tensor_tensor(vare[par][:], e2eps[par][:], msqn[par][:],
                                              op=ALU.subtract), "v")
                p.sec("vector", v2)
                v_n2 = p.cnt["v"]

                def a3(s, pair=pair, v_n=v_n2):
                    for w, i in enumerate(pair):
                        par = i % 2
                        s.wait_ge(p.s_v, v_n - (1 - w))
                        p.inc(s.activation(sqv[par][:], vare[par][:], AF.Sqrt), "a")
                p.sec("scalar", a3)
                a_n3 = p.cnt["a"]

                def v3(v, pair=pair, a_n=a_n3):
                    for w, i in enumerate(pair):
                        par = i % 2
                        v.wait_ge(p.s_a, a_n - (1 - w))
                        p.inc(v.reciprocal(rs[par][:], sqv[par][:]), "v")
                p.sec("vector", v3)
                v_n3 = p.cnt["v"]

                def a4(s, pair=pair, v_n=v_n3):
                    for w, i in enumerate(pair):
                        par = i % 2
                        s.wait_ge(p.s_v, v_n - (1 - w))
                        s.activation(xn_f[par][:], xc_f[par][:], AF.Identity, scale=rs[par][:, 0:1],
                                     accum_out=s_sum2[par][:, 0:1])
                        p.inc(s.activation(dump[:], xn_f[par][:], AF.Square,
                                           accum_out=s_sq2[par][:, 0:1]), "a")
                p.sec("scalar", a4)
                a_n4 = p.cnt["a"]

                def v4(v, pair=pair, a_n=a_n4, gn=(g_xn_of[0], g_xn_of[1])):
                    for w, i in enumerate(pair):
                        par = i % 2
                        v.wait_ge(p.s_a, a_n - (1 - w))
                        if gn[par]:
                            v.wait_ge(p.s_g, gn[par])
                        v.tensor_scalar(m2neg[par][:], s_sum2[par][:], -1.0 / D, None, op0=ALU.mult)
                        v.tensor_scalar(e2eps2[par][:], s_sq2[par][:], 1.0 / D, 1e-8,
                                        op0=ALU.mult, op1=ALU.add)
                        p.inc(v.tensor_copy(xn_bf[par][:], xn_f[par][:]), "v")
                p.sec("vector", v4)
                v_n4 = p.cnt["v"]

                def gs(g, pair=pair, v_n=v_n4):
                    for w, i in enumerate(pair):
                        par = i % 2
                        g.wait_ge(p.s_v, v_n - (1 - w))
                        p.inc(g.dma_start(out=p.cc_xn[i * 128:(i + 1) * 128, :],
                                          in_=xn_bf[par][:]), "g")
                p.sec("gpsimd", gs)
                g_xn_of[i0 % 2] = p.cnt["g"] - 16
                g_xn_of[(i0 + 1) % 2] = p.cnt["g"]

                def a5(s, pair=pair, v_n=v_n4):
                    for w, i in enumerate(pair):
                        par = i % 2
                        s.wait_ge(p.s_v, v_n - (1 - w))
                        s.activation(xc_f[par][:], xn_f[par][:], AF.Identity, bias=m2neg[par][:, 0:1])
                        p.inc(s.activation(msqn2[par][:], m2neg[par][:], AF.Square), "a")
                p.sec("scalar", a5)
                a_n5 = p.cnt["a"]

                def v5(v, pair=pair, a_n=a_n5):
                    for w, i in enumerate(pair):
                        par = i % 2
                        v.wait_ge(p.s_a, a_n - (1 - w))
                        p.inc(v.tensor_tensor(vare2[par][:], e2eps2[par][:], msqn2[par][:],
                                              op=ALU.subtract), "v")
                p.sec("vector", v5)
                v_n5 = p.cnt["v"]

                def a6(s, pair=pair, v_n=v_n5):
                    for w, i in enumerate(pair):
                        par = i % 2
                        s.wait_ge(p.s_v, v_n - (1 - w))
                        p.inc(s.activation(sqv2[par][:], vare2[par][:], AF.Sqrt), "a")
                p.sec("scalar", a6)
                a_n6 = p.cnt["a"]

                def v6(v, pair=pair, a_n=a_n6):
                    for w, i in enumerate(pair):
                        par = i % 2
                        v.wait_ge(p.s_a, a_n - (1 - w))
                        p.inc(v.reciprocal(rs2[par][:], sqv2[par][:]), "v")
                p.sec("vector", v6)
                v_n6 = p.cnt["v"]

                def a7(s, pair=pair, v_n=v_n6):
                    for w, i in enumerate(pair):
                        par = i % 2
                        s.wait_ge(p.s_v, v_n - (1 - w))
                        p.inc(s.activation(xnn_f[par][:], xc_f[par][:], AF.Identity,
                                           scale=rs2[par][:, 0:1]), "a")
                p.sec("scalar", a7)
                a_n7 = p.cnt["a"]

                def v7(v, pair=pair, a_n=a_n7):
                    for w, i in enumerate(pair):
                        par = i % 2
                        v.wait_ge(p.s_a, a_n - (1 - w))
                        v.reduce_max(amax[par][:, 0:1], xnn_f[par][:], axis=AX.X,
                                     apply_absolute_value=True)
                        v.tensor_scalar(vgap[par][:], amax[par][:], 1.0, None, op0=ALU.mult)
                        v.reciprocal(rtmp[par][:], amax[par][:])
                        v.tensor_scalar(invsq[par][:], amax[par][:], 1.0 / 127.0, None, op0=ALU.mult)
                        p.inc(v.tensor_scalar(sq[par][:], rtmp[par][:], 127.0, None, op0=ALU.mult), "v")
                p.sec("vector", v7)
                v_n7 = p.cnt["v"]

                def a8(s, pair=pair, v_n=v_n7):
                    for w, i in enumerate(pair):
                        par = i % 2
                        s.wait_ge(p.s_v, v_n - (1 - w))
                        p.inc(s.activation(f16q[par][:], xnn_f[par][:], AF.Identity,
                                           bias=p.b1536[:, 0:1], scale=sq[par][:, 0:1]), "a")
                p.sec("scalar", a8)
                a_n8 = p.cnt["a"]
                a_last[i0 % 2] = p.cnt["a"] - 1
                a_last[(i0 + 1) % 2] = p.cnt["a"]

                def v8(v, pair=pair, a_n=a_n8,
                       sn=(s_tr_of[i0 - 2] if i0 >= 2 else 0, s_tr_of[i0 - 1] if i0 >= 1 else 0)):
                    for w, i in enumerate(pair):
                        par = i % 2
                        v.wait_ge(p.s_a, a_n - (1 - w))
                        if sn[par]:
                            v.wait_ge(p.s_s, sn[par])
                        v.tensor_scalar(xq_f[par][:], f16q[par][:], -1536.0, invsq[par][:, 0:1],
                                        op0=ALU.add, op1=ALU.mult)
                        p.inc(v.tensor_copy(xq_bf[par][:], xq_f[par][:]), "v")
                p.sec("vector", v8)
                v_n8 = p.cnt["v"]

                def tr(sy, pair=pair, v_n=v_n8):
                    for w, i in enumerate(pair):
                        par = i % 2
                        sy.wait_ge(p.s_v, v_n - (1 - w))
                        for d in range(NDC):
                            p.inc(sy.dma_start(
                                out=p.xqT[:, d * R + i * 128: d * R + (i + 1) * 128],
                                in_=xq_bf[par][:, d * 128:(d + 1) * 128], transpose=True), "s")
                p.sec("sync", tr)
                s_tr_of[i0] = p.cnt["s"] - 16 * NDC
                s_tr_of[i0 + 1] = p.cnt["s"]
                p.flush()

            self.s_xqT = p.cnt["s"]
            g_all = p.cnt["g"]

            def coll(g, g_all=g_all):
                g.wait_ge(p.s_g, g_all)
                p.inc(g.collective_compute("AllGather", ALU.bypass, ins=[p.cc_xn],
                                           outs=[p.ag_xn],
                                           replica_groups=[list(range(NC))]), "c")
            p.sec("gpsimd", coll)
            self.c_xn = p.cnt["c"]
            self.barrier()

    def ffn(self):
        p, nc = self, self.nc
        with nc.psum_tensor("ps_lam", [128, 2], F32) as ps_lam:
            def alam(s, gl=self.g_lam):
                s.wait_ge(p.s_g, gl)
                p.inc(s.activation(p.lam_sig[:], p.lam_sb[:], AF.Sigmoid), "a")
            p.sec("scalar", alam)

            def tlam(t, a_lam=p.cnt["a"]):
                t.wait_ge(p.s_a, a_lam)
                p.inc(t.matmul(ps_lam[:], p.onesr_f[:], p.lam_sig[:], start=True, stop=True), "t")
            p.sec("tensor", tlam)

            def vlam(v, t_lam=p.cnt["t"]):
                v.wait_ge(p.s_t, t_lam)
                p.inc(v.tensor_scalar(p.lamneg[:], ps_lam[:], -1.0, None, op0=ALU.mult), "v")
            p.sec("vector", vlam)

        def wstage(g):
            p.inc(g.dma_start(out=p.wq_d, in_=p.wq), "g")
            p.inc(g.dma_start(out=p.wk_d, in_=p.wk), "g")
            p.inc(g.dma_start(out=p.wv_d, in_=p.wv), "g")
            p.inc(g.dma_start(out=p.msk_d.rearrange("b (a c) -> b a c", c=1024),
                              in_=p.msk.rearrange("a b c -> b a c")), "g")
            p.inc(g.dma_start(out=p.wob_d.rearrange("p (a b f) -> p a b f", b=4, f=512),
                              in_=p.wo.rearrange("a b p f -> p a b f")), "g")
        p.sec("gpsimd", wstage)
        self.g_stage = p.cnt["g"]

        with ExitStack() as es:
            E = es.enter_context
            slab = [E(nc.sbuf_tensor(f"wslab{i}", [128, 2048], F32)) for i in range(2)]
            sgW = [E(nc.sbuf_tensor(f"sgW{i}", [128, 2048], BF16)) for i in range(4)]
            sgm = E(nc.sbuf_tensor("sgm", [128, 1024], F32))
            sil = E(nc.sbuf_tensor("sil", [128, 1024], F32))
            pT = E(nc.sbuf_tensor("pT", [128, 16 * R], BF16))
            f_sb = E(nc.sbuf_tensor("f_sb", [128, NT * D], F32))
            wdf = [E(nc.sbuf_tensor(f"wdf{i}", [128, 2048], F32)) for i in range(2)]
            wdb = [E(nc.sbuf_tensor(f"wdb{i}", [128, 2048], BF16)) for i in range(2)]
            g_t = [E(nc.sbuf_tensor(f"g_t{i}", [128, D], BF16)) for i in range(2)]

            sign_a = 0
            sg_mm_t = [0, 0]
            a_w_of = {}
            evac_v = 0

            def emit_lw(hb):
                def lw(sy, hb=hb, fa=sign_a):
                    if fa:
                        sy.wait_ge(p.s_a, fa)
                    p.inc(sy.dma_start(out=slab[0][:], in_=p.wg[hb]), "s")
                    p.inc(sy.dma_start(out=slab[1][:], in_=p.wu[hb]), "s")
                p.sec("sync", lw)
                return p.cnt["s"]

            for q in range(4):
                with nc.psum_tensor(f"ps_yga{q}", [128, 1024], F32) as ps_yga, \
                     nc.psum_tensor(f"ps_yua{q}", [128, 1024], F32) as ps_yua, \
                     nc.psum_tensor(f"ps_ygb{q}", [128, 1024], F32) as ps_ygb, \
                     nc.psum_tensor(f"ps_yub{q}", [128, 1024], F32) as ps_yub:
                    evac_pair = [0, 0]
                    if q == 0:
                        s_w0 = emit_lw(0)

                        def sw0(sct, s_w0=s_w0):
                            sct.wait_ge(p.s_s, s_w0)
                            sct.activation(sgW[0][:], slab[0][:], AF.Sign)
                            p.inc(sct.activation(sgW[1][:], slab[1][:], AF.Sign), "a")
                        p.sec("scalar", sw0)
                        a_w_of[0] = p.cnt["a"]
                        sign_a = p.cnt["a"]
                    for jb in range(16):
                        hb = q * 16 + jb
                        par = hb % 2
                        ps_yg = ps_yga if par == 0 else ps_ygb
                        ps_yu = ps_yua if par == 0 else ps_yub

                        s_wn = emit_lw(hb + 1) if hb + 1 < 64 else 0

                        def mm(t, par=par, a_w=a_w_of[hb], ev=evac_pair[par],
                               s_xqT=self.s_xqT, first=(q == 0 and jb == 0),
                               ev0=(evac_v if jb <= 1 else 0)):
                            if first:
                                t.wait_ge(p.s_s, s_xqT)
                            t.wait_ge(p.s_a, a_w)
                            if ev:
                                t.wait_ge(p.s_v, ev)
                            elif ev0:
                                t.wait_ge(p.s_v, ev0)
                            mmi = None
                            for which in range(2):
                                dst = ps_yg if which == 0 else ps_yu
                                src = sgW[2 * par + which]
                                for th in range(2):
                                    for j in range(NDC):
                                        mmi = t.matmul(dst[:, th * 512:(th + 1) * 512],
                                                       src[:, j * 128:(j + 1) * 128],
                                                       p.xqT[:, j * R + th * 512: j * R + (th + 1) * 512],
                                                       start=(j == 0), stop=(j == NDC - 1),
                                                       skip_group_check=True)
                            p.inc(mmi, "t")
                        p.sec("tensor", mm)
                        t_mm = p.cnt["t"]
                        sg_mm_t[par] = t_mm

                        def ev_a(sct, hb=hb, s_wn=s_wn, t_mm=t_mm, v_prev=evac_v,
                                 npar=(hb + 1) % 2, ftn=sg_mm_t[(hb + 1) % 2]):
                            if s_wn:
                                sct.wait_ge(p.s_s, s_wn)
                                if ftn:
                                    sct.wait_ge(p.s_t, ftn)
                                sct.activation(sgW[2 * npar][:], slab[0][:], AF.Sign)
                                p.inc(sct.activation(sgW[2 * npar + 1][:], slab[1][:],
                                                     AF.Sign), "a")
                            sct.wait_ge(p.s_t, t_mm)
                            if v_prev:
                                sct.wait_ge(p.s_v, v_prev)
                            p.inc(sct.activation(sgm[:], ps_yg[:], AF.Sigmoid), "a")
                        p.sec("scalar", ev_a)
                        if hb + 1 < 64:
                            a_w_of[hb + 1] = p.cnt["a"] - 1
                            sign_a = p.cnt["a"] - 1

                        def ev_v(v, jb=jb, a_sg=p.cnt["a"], t_mm=t_mm):
                            v.wait_ge(p.s_a, a_sg)
                            v.tensor_tensor(sil[:], ps_yg[:], sgm[:], op=ALU.mult)
                            p.inc(v.tensor_tensor(pT[:, jb * R:(jb + 1) * R], sil[:], ps_yu[:],
                                                  op=ALU.mult), "v")
                        p.sec("vector", ev_v)
                        evac_v = p.cnt["v"]

                v_pT = p.cnt["v"]
                if q == 0:
                    wd_sign_a = [0, 0]
                    wd_mm_t = [0, 0]
                with nc.psum_tensor(f"ps_f{q}", [128, 8 * 512], F32) as ps_f:
                    bank_ev = [0] * 8
                    for dt in range(4):
                        for jh in range(2):
                            g_wd_of = [0, 0]
                            for par in range(2):
                                jq = 2 * jh + par

                                def lwd(g, dt=dt, hb0=q * 16 + jq * 4, par=par,
                                        fa=wd_sign_a[par]):
                                    if fa:
                                        g.wait_ge(p.s_a, fa)
                                    p.inc(g.dma_start(
                                        out=wdf[par][:].rearrange("p (c f) -> p c f", f=512),
                                        in_=p.wd[dt, hb0:hb0 + 4].rearrange("c p f -> p c f")), "g")
                                p.sec("gpsimd", lwd)
                                g_wd_of[par] = p.cnt["g"]

                            def swd2(sct, g_wd_of=list(g_wd_of),
                                     fts=(wd_mm_t[0], wd_mm_t[1])):
                                for par in range(2):
                                    sct.wait_ge(p.s_g, g_wd_of[par])
                                    if fts[par]:
                                        sct.wait_ge(p.s_t, fts[par])
                                    p.inc(sct.activation(wdb[par][:], wdf[par][:],
                                                         AF.Sign), "a")
                            p.sec("scalar", swd2)
                            wd_sign_a[0] = p.cnt["a"] - 1
                            wd_sign_a[1] = p.cnt["a"]
                            a_wd = p.cnt["a"]

                            def mm3(t, jh=jh, a_wd=a_wd, v_pT=v_pT,
                                    ev=(list(bank_ev) if jh == 0 else None)):
                                t.wait_ge(p.s_a, a_wd)
                                t.wait_ge(p.s_v, v_pT)
                                mmi = None
                                for tb in range(8):
                                    if ev and ev[tb]:
                                        t.wait_ge(p.s_v, ev[tb])
                                    for jc8 in range(8):
                                        j = jh * 8 + jc8
                                        mmi = t.matmul(ps_f[:, tb * 512:(tb + 1) * 512],
                                                       pT[:, j * R + tb * 128: j * R + tb * 128 + 128],
                                                       wdb[jc8 // 4][:, (jc8 % 4) * 512:(jc8 % 4 + 1) * 512],
                                                       start=(j == 0), stop=(j == 15),
                                                       skip_group_check=True)
                                p.inc(mmi, "t")
                            p.sec("tensor", mm3)
                            wd_mm_t[0] = p.cnt["t"]
                            wd_mm_t[1] = p.cnt["t"]
                        t_dt = p.cnt["t"]

                        def ev3(v, q=q, dt=dt, t_dt=t_dt):
                            v.wait_ge(p.s_t, t_dt)
                            ins = None
                            for tb in range(8):
                                src = ps_f[:, tb * 512:(tb + 1) * 512]
                                dst = f_sb[:, tb * D + dt * 512: tb * D + (dt + 1) * 512]
                                if q == 0:
                                    ins = v.tensor_copy(dst, src)
                                else:
                                    ins = v.tensor_tensor(dst, dst, src, op=ALU.add)
                            p.inc(ins, "v")
                        p.sec("vector", ev3)
                        for tb in range(8):
                            bank_ev[tb] = p.cnt["v"]
                evac_v = p.cnt["v"]

            v_f = p.cnt["v"]
            g_store = [0, 0]
            for tb in range(NT):
                par = tb % 2

                def gsig(s, tb=tb, v_f=v_f):
                    if tb == 0:
                        s.wait_ge(p.s_v, v_f)
                    p.inc(s.activation(sgm[:], f_sb[:, tb * D: tb * D + 1024], AF.Sigmoid), "a")
                    p.inc(s.activation(sil[:], f_sb[:, tb * D + 1024: tb * D + 2048],
                                       AF.Sigmoid), "a")
                p.sec("scalar", gsig)

                def gmul(v, tb=tb, par=par, a_g=p.cnt["a"], g_need=g_store[par]):
                    v.wait_ge(p.s_a, a_g)
                    if g_need:
                        v.wait_ge(p.s_g, g_need)
                    v.tensor_tensor(g_t[par][:, 0:1024],
                                    f_sb[:, tb * D: tb * D + 1024], sgm[:], op=ALU.mult)
                    p.inc(v.tensor_tensor(g_t[par][:, 1024:2048],
                                          f_sb[:, tb * D + 1024: tb * D + 2048], sil[:],
                                          op=ALU.mult), "v")
                p.sec("vector", gmul)

                def gst(g, tb=tb, par=par, v_g=p.cnt["v"]):
                    g.wait_ge(p.s_v, v_g)
                    p.inc(g.dma_start(out=p.cc_g[tb * 128:(tb + 1) * 128, :], in_=g_t[par][:]), "g")
                p.sec("gpsimd", gst)
                g_store[par] = p.cnt["g"]
            self.barrier()

    def attention(self):
        p, nc = self, self.nc
        with ExitStack() as es:
            E = es.enter_context
            wqb = E(nc.sbuf_tensor("wqb", [128, 4096], BF16))
            wkb = E(nc.sbuf_tensor("wkb", [128, 4096], BF16))
            wvb = E(nc.sbuf_tensor("wvb", [128, 4096], BF16))
            mask_bf = E(nc.sbuf_tensor("mask_bf", [128, 4096], BF16))
            qT = E(nc.sbuf_tensor("qT", [128, 2 * T], BF16))
            kT = E(nc.sbuf_tensor("kT", [128, 2 * T], BF16))
            vT = E(nc.sbuf_tensor("vT", [128, 2 * T], BF16))
            vtok = E(nc.sbuf_tensor("vtok", [128, 2 * T], BF16))
            xnT = [E(nc.sbuf_tensor(f"xnT{i}", [128, NDC * 512], BF16)) for i in range(2)]
            strip = E(nc.sbuf_tensor("strip", [128, 4 * 1024], BF16))
            mtmp = [E(nc.sbuf_tensor(f"mtmp{i}", [128, 1024], BF16)) for i in range(2)]
            rl = E(nc.sbuf_tensor("rl", [1, 1024], F32))
            rl_bf = E(nc.sbuf_tensor("rl_bf", [1, 1024], BF16))
            Bsb = E(nc.sbuf_tensor("Bsb", [128, 1024], F32))
            Osb = E(nc.sbuf_tensor("Osb", [128, 1024], F32))
            cmb = E(nc.sbuf_tensor("cmb", [128, 1024], F32))
            ocm = E(nc.sbuf_tensor("ocm", [128, T], BF16))
            self.wob = E(nc.sbuf_tensor("wob", [128, 2 * 4 * 512], BF16))
            self.oc = [E(nc.sbuf_tensor(f"oc{i}", [128, 2 * 1024], BF16)) for i in range(4)]
            self.s_sb = [E(nc.sbuf_tensor(f"ssb{i}", [128, D], BF16)) for i in range(2)]
            self.wo_st = {"oc_mm_t": [0, 0, 0, 0], "sb_dma_g": [0, 0],
                          "bank_ev": [0, 0, 0, 0], "c_rs": [0, 0]}
            g_ocm_of = [0, 0]

            def gw(sy, g_stage=self.g_stage):
                sy.wait_ge(p.s_g, g_stage)
                p.inc(sy.dma_start(out=wqb[:], in_=p.wq_d), "s")
                p.inc(sy.dma_start(out=wkb[:], in_=p.wk_d), "s")
                p.inc(sy.dma_start(out=wvb[:], in_=p.wv_d), "s")
                p.inc(sy.dma_start(out=mask_bf[:], in_=p.msk_d), "s")
                p.inc(sy.dma_start(out=self.wob[:], in_=p.wob_d), "s")
            p.sec("sync", gw)
            g_w = p.cnt["s"]
            self.g_wo = p.cnt["s"]

            agxn = p.ag_xn.rearrange("a b c -> (a b) c")
            xnT_mm_t = [0, 0]
            g_ocm = 0
            for b in range(2):
                with nc.psum_tensor(f"ps_qkv{b}", [128, 4 * 512], F32) as ps4:
                    slot_ev = [0, 0, 0, 0]
                    for ts in range(8):
                        par = ts % 2

                        def ltr(sy, b=b, ts=ts, par=par, t_need=xnT_mm_t[par],
                                c_xn=self.c_xn):
                            sy.wait_ge(p.s_c, c_xn)
                            if t_need:
                                sy.wait_ge(p.s_t, t_need)
                            base = ts * 1024 + b * 512
                            for d in range(NDC):
                                p.inc(sy.dma_start(
                                    out=xnT[par][:, d * 512:(d + 1) * 512],
                                    in_=agxn[base:base + 512, d * 128:(d + 1) * 128],
                                    transpose=True), "s")
                        p.sec("sync", ltr)
                        s_tr = p.cnt["s"]

                        t_base = p.cnt["t"]
                        v_base = p.cnt["v"]

                        def mmqkv(t, par=par, s_tr=s_tr, g_w=g_w, ev=list(slot_ev),
                                  v_base=v_base, first=(ts == 0),
                                  a_pre=p.cnt["a"], v_pre=p.cnt["v"]):
                            t.wait_ge(p.s_s, max(s_tr, g_w))
                            if first:
                                t.wait_ge(p.s_a, a_pre)
                                t.wait_ge(p.s_v, v_pre)
                            k = 0
                            for hb in range(2):
                                for wsl in (wqb, wkb, wvb):
                                    pi = k % 4
                                    if k >= 4:
                                        t.wait_ge(p.s_v, v_base + (k - 4) + 1)
                                    elif ev[pi]:
                                        t.wait_ge(p.s_v, ev[pi])
                                    mmi = None
                                    for j in range(NDC):
                                        mmi = t.matmul(
                                            ps4[:, pi * 512:(pi + 1) * 512],
                                            wsl[:, (hb * 16 + j) * 128:(hb * 16 + j + 1) * 128],
                                            xnT[par][:, j * 512:(j + 1) * 512],
                                            start=(j == 0), stop=(j == NDC - 1),
                                            skip_group_check=True)
                                    p.inc(mmi, "t")
                                    k += 1
                        p.sec("tensor", mmqkv)
                        xnT_mm_t[par] = p.cnt["t"]

                        def evqkv(v, ts=ts, t_base=t_base):
                            k = 0
                            for hb in range(2):
                                for dst in (qT, kT, vT):
                                    pi = k % 4
                                    v.wait_ge(p.s_t, t_base + k + 1)
                                    p.inc(v.tensor_copy(
                                        dst[:, hb * T + ts * 512:(hb * T + (ts + 1) * 512)],
                                        ps4[:, pi * 512:(pi + 1) * 512]), "v")
                                    k += 1
                        p.sec("vector", evqkv)
                        for k in range(4):
                            slot_ev[k] = p.cnt["v"]
                        p.flush()

                def vtr(sy, v_q=p.cnt["v"], t_pre=p.cnt["t"]):
                    sy.wait_ge(p.s_v, v_q)
                    sy.wait_ge(p.s_t, t_pre)
                    for hb in range(2):
                        for sc in range(32):
                            p.inc(sy.dma_start(
                                out=vtok[:, hb * T + sc * 128: hb * T + (sc + 1) * 128],
                                in_=vT[:, hb * T + sc * 128: hb * T + (sc + 1) * 128],
                                transpose=True), "s")
                p.sec("sync", vtr)
                s_vt = p.cnt["s"]
                if b == 1:
                    self.wo_batch(0, g_ocm_of[0], p.cnt["v"])

                for hb in range(2):
                    with nc.psum_tensor(f"ps_s{b}{hb}", [128, 2 * 1024], F32) as ps_s, \
                         nc.psum_tensor(f"ps_O1{b}{hb}", [128, 512], F32) as ps_O1, \
                         nc.psum_tensor(f"ps_O2{b}{hb}", [128, 512], F32) as ps_O2, \
                         nc.psum_tensor(f"ps_L1{b}{hb}", [128, 512], F32) as ps_L1, \
                         nc.psum_tensor(f"ps_L2{b}{hb}", [128, 512], F32) as ps_L2:
                        exp_a_par = [0, 0]
                        bs_v_par = [0, 0]
                        o_free_v = 0
                        slot_t = [0, 0, 0, 0]
                        mtmp_v = [0, 0]
                        pending = None
                        for tt in range(8):
                            ns = 4 * (tt + 1)
                            npair = ns // 2
                            t0 = p.cnt["t"]
                            has_p = pending is not None

                            if has_p:
                                t_end_p, tt_p, g_need_p = pending

                                def pvec(v, t_end_p=t_end_p, g_need=g_need_p,
                                         hb=hb, tt_p=tt_p):
                                    v.wait_ge(p.s_t, t_end_p)
                                    if g_need:
                                        v.wait_ge(p.s_g, g_need)
                                    v.reciprocal(Bsb[:, 0:512], ps_L1[:, :])
                                    p.inc(v.reciprocal(Bsb[:, 512:1024],
                                                       ps_L2[:, :]), "v")
                                    v.tensor_tensor(cmb[:, 0:512], Osb[:, 0:512],
                                                    Bsb[:, 0:512], op=ALU.mult)
                                    v.scalar_tensor_tensor(
                                        cmb[:, 512:1024], Osb[:, 512:1024],
                                        p.lamneg[:, hb:hb + 1], Bsb[:, 512:1024],
                                        op0=ALU.mult, op1=ALU.mult)
                                    p.inc(v.tensor_tensor(
                                        ocm[:, tt_p * 512:(tt_p + 1) * 512],
                                        cmb[:, 0:512], cmb[:, 512:1024],
                                        op=ALU.add), "v")
                                p.sec("vector", pvec)
                                bs_v = p.cnt["v"] - 1
                            else:
                                bs_v = 0

                            a0 = p.cnt["a"]
                            v0m = p.cnt["v"]

                            def a_pred(idx, w):
                                return a0 + 2 * idx + w + 1

                            def m_pred(k):
                                return v0m + k + 1

                            slot_snap = {}
                            t_of = {}
                            sim_t = t0
                            fw = (p.cnt["a"], p.cnt["v"]) if tt == 0 else None
                            an0 = (exp_a_par[0], exp_a_par[1])
                            o_need0 = o_free_v
                            mv_prev = tuple(mtmp_v)

                            def tsec(t, tt=tt, hb=hb, npair=npair, t0=t0,
                                     fw=fw, an0=an0, o_need0=o_need0, bs_v=bs_v,
                                     s_vt=s_vt, a0=a0, v0m=v0m):
                                for idx in range(npair):
                                    si0 = 2 * idx
                                    if idx == 0 and fw is not None:
                                        t.wait_ge(p.s_s, s_vt)
                                        t.wait_ge(p.s_a, fw[0])
                                        t.wait_ge(p.s_v, fw[1])
                                    for w, si in enumerate((si0, si0 + 1)):
                                        if idx == 0:
                                            if an0[w]:
                                                t.wait_ge(p.s_a, an0[w])
                                        else:
                                            t.wait_ge(p.s_a, a0 + 2 * (idx - 1) + w + 1)
                                        par = si % 2
                                        t.matmul(ps_s[:, par * 1024: par * 1024 + 512],
                                                 kT[0:64, hb * T + si * 128: hb * T + (si + 1) * 128],
                                                 qT[0:64, hb * T + tt * 512: hb * T + (tt + 1) * 512],
                                                 start=True, stop=True, tile_position=(0, 0))
                                        p.inc(t.matmul(
                                            ps_s[:, par * 1024 + 512: par * 1024 + 1024],
                                            kT[64:128, hb * T + si * 128: hb * T + (si + 1) * 128],
                                            qT[64:128, hb * T + tt * 512: hb * T + (tt + 1) * 512],
                                            start=True, stop=True, tile_position=(64, 0)), "t")
                                    diag = si0 >= 4 * tt
                                    for w, si in enumerate((si0, si0 + 1)):
                                        slot = si % 4
                                        st_ = idx == 0 and w == 0
                                        sp_ = idx == npair - 1 and w == 1
                                        if diag:
                                            k = (si - 4 * tt)
                                            t.wait_ge(p.s_v, v0m + k + 1)
                                        else:
                                            t.wait_ge(p.s_a, a0 + 2 * idx + w + 1)
                                        if st_:
                                            if o_need0:
                                                t.wait_ge(p.s_v, o_need0)
                                            if bs_v:
                                                t.wait_ge(p.s_v, bs_v)
                                        t.matmul(ps_L1[:, :], p.ones_mat[:, :],
                                                 strip[:, slot * 1024: slot * 1024 + 512],
                                                 start=st_, stop=sp_,
                                                 skip_group_check=True)
                                        t.matmul(ps_L2[:, :], p.ones_mat[:, :],
                                                 strip[:, slot * 1024 + 512: (slot + 1) * 1024],
                                                 start=st_, stop=sp_,
                                                 skip_group_check=True)
                                        t.matmul(ps_O1[:],
                                                 vtok[:, hb * T + si * 128: hb * T + (si + 1) * 128],
                                                 strip[:, slot * 1024: slot * 1024 + 512],
                                                 start=st_, stop=sp_,
                                                 skip_group_check=True)
                                        p.inc(t.matmul(ps_O2[:],
                                                       vtok[:, hb * T + si * 128: hb * T + (si + 1) * 128],
                                                       strip[:, slot * 1024 + 512:(slot + 1) * 1024],
                                                       start=st_, stop=sp_,
                                                       skip_group_check=True), "t")
                            for idx in range(npair):
                                si0 = 2 * idx
                                sim_t += 2
                                t_of[si0] = sim_t - 1
                                t_of[si0 + 1] = sim_t
                                slot_snap[si0] = slot_t[si0 % 4]
                                slot_snap[si0 + 1] = slot_t[(si0 + 1) % 4]
                                sim_t += 2
                                slot_t[si0 % 4] = sim_t - 1
                                slot_t[(si0 + 1) % 4] = sim_t
                            p.sec("tensor", tsec)
                            assert p.cnt["t"] == sim_t, (p.cnt["t"], sim_t)
                            t_end = sim_t
                            exp_a_par[0] = a_pred(npair - 1, 0)
                            exp_a_par[1] = a_pred(npair - 1, 1)

                            def asec(sct, tt=tt, npair=npair, t_of=dict(t_of),
                                     slot_snap=dict(slot_snap), mv_prev=mv_prev,
                                     v0m=v0m):
                                for idx in range(npair):
                                    si0 = 2 * idx
                                    for w, si in enumerate((si0, si0 + 1)):
                                        par = si % 2
                                        slot = si % 4
                                        diag = si >= 4 * tt
                                        sct.wait_ge(p.s_t, max(t_of[si], slot_snap[si]))
                                        if diag:
                                            k = si - 4 * tt
                                            if k < 2:
                                                if mv_prev[si % 2]:
                                                    sct.wait_ge(p.s_v, mv_prev[si % 2])
                                            else:
                                                sct.wait_ge(p.s_v, v0m + k - 1)
                                        dst = (mtmp[si % 2][:] if diag
                                               else strip[:, slot * 1024:(slot + 1) * 1024])
                                        p.inc(sct.activation(
                                            dst, ps_s[:, par * 1024:(par + 1) * 1024],
                                            AF.Exp, scale=SCALE), "a")
                            p.sec("scalar", asec)
                            assert p.cnt["a"] == a0 + ns, (p.cnt["a"], a0, ns)

                            def msec(v, tt=tt, a0=a0, ns=ns,
                                     sl=(slot_t[0], slot_t[1], slot_t[2], slot_t[3]),
                                     slot_snap=dict(slot_snap)):
                                for k in range(4):
                                    si = 4 * tt + k
                                    slot = si % 4
                                    v.wait_ge(p.s_a, a0 + ns - 4 + k + 1)
                                    tsl = slot_snap[si]
                                    if tsl:
                                        v.wait_ge(p.s_t, tsl)
                                    p.inc(v.tensor_tensor(
                                        strip[:, slot * 1024:(slot + 1) * 1024],
                                        mtmp[si % 2][:],
                                        mask_bf[:, k * 1024:(k + 1) * 1024],
                                        op=ALU.mult), "v")
                            p.sec("vector", msec)
                            assert p.cnt["v"] == v0m + 4, (p.cnt["v"], v0m)
                            mtmp_v[0] = v0m + 3
                            mtmp_v[1] = v0m + 4

                            def vfl(v, t_end=t_end):
                                v.wait_ge(p.s_t, t_end)
                                v.tensor_copy(Osb[:, 0:512], ps_O1[:])
                                p.inc(v.tensor_copy(Osb[:, 512:1024], ps_O2[:]), "v")
                            p.sec("vector", vfl)
                            o_free_v = p.cnt["v"]
                            pending = (t_end, tt, (g_ocm if tt == 0 else 0))

                        t_end_p, tt_p, g_need_p = pending

                        def fvec(v, t_end_p=t_end_p, g_need=g_need_p, hb=hb,
                                 tt_p=tt_p):
                            v.wait_ge(p.s_t, t_end_p)
                            if g_need:
                                v.wait_ge(p.s_g, g_need)
                            v.reciprocal(Bsb[:, 0:512], ps_L1[:, :])
                            v.reciprocal(Bsb[:, 512:1024], ps_L2[:, :])
                            v.tensor_tensor(cmb[:, 0:512], Osb[:, 0:512],
                                            Bsb[:, 0:512], op=ALU.mult)
                            v.scalar_tensor_tensor(
                                cmb[:, 512:1024], Osb[:, 512:1024],
                                p.lamneg[:, hb:hb + 1], Bsb[:, 512:1024],
                                op0=ALU.mult, op1=ALU.mult)
                            p.inc(v.tensor_tensor(
                                ocm[:, tt_p * 512:(tt_p + 1) * 512],
                                cmb[:, 0:512], cmb[:, 512:1024], op=ALU.add), "v")
                        p.sec("vector", fvec)
                        pending = None


                    def soc(g, b=b, hb=hb, v_oc=p.cnt["v"]):
                        g.wait_ge(p.s_v, v_oc)
                        p.inc(g.dma_start(out=p.cc_o[b, hb], in_=ocm[:]), "g")
                    p.sec("gpsimd", soc)
                    g_ocm = p.cnt["g"]
                g_ocm_of[b] = g_ocm
            self.wo_batch(1, g_ocm_of[1], p.cnt["v"])
            self.barrier()

    def wo_batch(self, b, g_ocm_b, v_pre):
        p, nc = self, self.nc
        st = self.wo_st
        wob, oc, s_sb = self.wob, self.oc, self.s_sb
        with nc.psum_tensor(f"ps_sp{b}", [128, 8 * 512], F32) as ps_sp:
            g_oc_of = [0] * 4
            for tq in range(4):
                def loc(g, b=b, tq=tq, t_need=st["oc_mm_t"][tq],
                        g_ocm_b=(g_ocm_b if tq == 0 else 0)):
                    if g_ocm_b:
                        g.wait_ge(p.s_g, g_ocm_b)
                    if t_need:
                        g.wait_ge(p.s_t, t_need)
                    for hb in range(2):
                        p.inc(g.dma_start(
                            out=oc[tq][:, hb * 1024:(hb + 1) * 1024],
                            in_=p.cc_o[b, hb][:, tq * 1024:(tq + 1) * 1024]), "g")
                p.sec("gpsimd", loc)
                g_oc_of[tq] = p.cnt["g"]

            for tq in range(4):
                t_base = p.cnt["t"]
                a_base = p.cnt["a"]
                s_base = p.cnt["s"]
                ev_prev = (st["bank_ev"][0], st["bank_ev"][1])
                sb_prev = (st["sb_dma_g"][0], st["sb_dma_g"][1])
                for tb in range(8):
                    pp = tb % 2

                    def mmsp(t, tq=tq, tb=tb, pp=pp, g_oc=g_oc_of[tq],
                             g_wo=self.g_wo,
                             ev=(ev_prev[pp] if tb < 2 else a_base + tb - 1),
                             v_pre=(v_pre if tq == 0 and tb == 0 else 0)):
                        t.wait_ge(p.s_g, g_oc)
                        t.wait_ge(p.s_s, g_wo)
                        if v_pre:
                            t.wait_ge(p.s_v, v_pre)
                        if ev:
                            t.wait_ge(p.s_a, ev)
                        for dt in range(4):
                            mmi = None
                            for hb in range(2):
                                mmi = t.matmul(
                                    ps_sp[:, (pp * 4 + dt) * 512:(pp * 4 + dt + 1) * 512],
                                    oc[tq][:, hb * 1024 + tb * 128:
                                            hb * 1024 + (tb + 1) * 128],
                                    wob[:, (hb * 4 + dt) * 512:(hb * 4 + dt + 1) * 512],
                                    start=(hb == 0), stop=(hb == 1),
                                    skip_group_check=True)
                            p.inc(mmi, "t")
                    p.sec("tensor", mmsp)

                def evq(sct, t_base=t_base, s_base=s_base, sb_prev=sb_prev):
                    for tb in range(8):
                        pp = tb % 2
                        par = tb % 2
                        sn = sb_prev[par] if tb < 2 else s_base + 16 * (tb - 1)
                        if sn:
                            sct.wait_ge(p.s_s, sn)
                        sct.wait_ge(p.s_t, t_base + 4 * (tb + 1))
                        p.inc(sct.activation(s_sb[par][:, :],
                                             ps_sp[:, pp * 2048:(pp + 1) * 2048],
                                             AF.Identity), "a")
                p.sec("scalar", evq)
                assert p.cnt["a"] == a_base + 8

                for tb in range(8):
                    par = tb % 2

                    def ssb(sy, b=b, tq=tq, tb=tb, par=par, a_d=a_base + tb + 1):
                        sy.wait_ge(p.s_a, a_d)
                        row = tq * 1024 + tb * 128
                        p.inc(sy.dma_start(out=p.cc_s[b][row:row + 128, :],
                                           in_=s_sb[par][:]), "s")
                    p.sec("sync", ssb)
                assert p.cnt["s"] == s_base + 16 * 8
                st["bank_ev"][0] = a_base + 7
                st["bank_ev"][1] = a_base + 8
                st["sb_dma_g"][0] = s_base + 16 * 7
                st["sb_dma_g"][1] = s_base + 16 * 8
                st["oc_mm_t"][tq] = p.cnt["t"]
                p.flush()

        s_all = p.cnt["s"]

        def coll2(g, s_all=s_all, b=b):
            g.wait_ge(p.s_s, s_all)
            p.inc(g.collective_compute("ReduceScatter", ALU.add, ins=[p.cc_s[b]],
                                       outs=[p.rs_s[b]],
                                       replica_groups=[list(range(NC))]), "c")
        p.sec("gpsimd", coll2)
        st["c_rs"][b] = p.cnt["c"]

    def wo_final(self):
        p, nc = self, self.nc
        with ExitStack() as es:
            E = es.enter_context
            x_f = [E(nc.sbuf_tensor(f"xr{i}", [128, D], F32)) for i in range(2)]
            st_bf = [E(nc.sbuf_tensor(f"stb{i}", [128, D], BF16)) for i in range(2)]
            gt_bf = [E(nc.sbuf_tensor(f"gtb{i}", [128, D], BF16)) for i in range(2)]
            s_f = E(nc.sbuf_tensor("s_fin", [128, D], F32))
            gg_f = E(nc.sbuf_tensor("gg_f", [128, D], F32))
            y_sb = [E(nc.sbuf_tensor(f"ysb{i}", [128, D], F32)) for i in range(2)]
            v_done = [0, 0]
            g_out_of = [0, 0]
            for tb in range(NT):
                par = tb % 2

                half = 0 if tb < 4 else 1
                rs_src = p.rs_s[half][(tb - 4 * half) * 128:(tb - 4 * half + 1) * 128, :]

                def lfin(sy, tb=tb, par=par, v_need=v_done[par],
                         c_rs=self.wo_st["c_rs"][half], rs_src=rs_src):
                    sy.wait_ge(p.s_c, c_rs)
                    if v_need:
                        sy.wait_ge(p.s_v, v_need)
                    p.inc(sy.dma_start(out=x_f[par][:], in_=p.x[tb * 128:(tb + 1) * 128, :]), "s")
                    p.inc(sy.dma_start(out=st_bf[par][:], in_=rs_src), "s")
                    p.inc(sy.dma_start(out=gt_bf[par][:],
                                       in_=p.cc_g[tb * 128:(tb + 1) * 128, :]), "s")
                p.sec("sync", lfin)

                def vfin(v, tb=tb, par=par, s_n=p.cnt["s"], s_y=g_out_of[par]):
                    v.wait_ge(p.s_s, max(s_n, s_y))
                    v.tensor_copy(s_f[:], st_bf[par][:])
                    v.tensor_copy(gg_f[:], gt_bf[par][:])
                    v.tensor_tensor(y_sb[par][:], s_f[:], gg_f[:], op=ALU.mult)
                    p.inc(v.tensor_tensor(y_sb[par][:], y_sb[par][:], x_f[par][:],
                                          op=ALU.add), "v")
                p.sec("vector", vfin)
                v_yt = p.cnt["v"]
                v_done[par] = v_yt

                def sout(sy, tb=tb, par=par, v_yt=v_yt):
                    sy.wait_ge(p.s_v, v_yt)
                    p.inc(sy.dma_start(out=p.y[tb * 128:(tb + 1) * 128, :], in_=y_sb[par][:]), "s")
                p.sec("sync", sout)
                g_out_of[par] = p.cnt["s"]
                if tb % 2 == 1:
                    p.flush()

            if DEBUG_TAPS:
                tapb = E(nc.sbuf_tensor("tapb", [128, D], BF16))
                for (src, dst, n) in ((p.cc_xn, p.tp_xn, NT),):
                    for i in range(n):
                        def tl(g, src=src, i=i, gn=p.cnt["g"]):
                            g.wait_ge(p.s_g, gn)
                            p.inc(g.dma_start(out=tapb[:], in_=src[i * 128:(i + 1) * 128, :]), "g")
                        p.sec("gpsimd", tl)
                        def ts_(g, dst=dst, i=i, gn=p.cnt["g"]):
                            g.wait_ge(p.s_g, gn)
                            p.inc(g.dma_start(out=dst[i * 128:(i + 1) * 128, :], in_=tapb[:]), "g")
                        p.sec("gpsimd", ts_)

            g_fin = p.cnt["g"]
            s_fin = p.cnt["s"]

            def fin(g, g_fin=g_fin, s_fin=s_fin):
                g.wait_ge(p.s_g, g_fin)
                g.wait_ge(p.s_s, s_fin)
            p.sec("gpsimd", fin)

    def close(self):
        self.es.close()


def build_program():
    p = Prog()
    p.setup()
    with ExitStack() as es:
        p.xqT = es.enter_context(p.nc.sbuf_tensor("xqT", [128, NDC * R], BF16))
        p.phase1()
        p.ffn()
    p.attention()
    p.wo_final()
    p.flush()
    p.close()
    return p.nc


_masks_cache = None


def _masks():
    global _masks_cache
    if _masks_cache is None:
        m = np.zeros((4, 128, 1024), np.float32)
        pp = np.arange(128)[:, None]
        ff = np.arange(512)[None, :]
        for v in range(4):
            mv = (pp + v * 128 <= ff).astype(np.float32)
            m[v, :, 0:512] = mv
            m[v, :, 512:1024] = mv
        _masks_cache = m
    return _masks_cache


def kernel(L0_attn_x, L0_attn_Wq, L0_attn_Wk, L0_attn_Wv, L0_attn_Wo,
           L0_attn_lambda_init, L0_ffn_W_g, L0_ffn_W_u, L0_ffn_W_d):
    x = np.ascontiguousarray(np.asarray(L0_attn_x, np.float32)).reshape(B * T, D)
    Wq = np.asarray(L0_attn_Wq, np.float32)
    Wk = np.asarray(L0_attn_Wk, np.float32)
    Wv = np.asarray(L0_attn_Wv, np.float32)
    Wo = np.asarray(L0_attn_Wo, np.float32)
    lam = np.asarray(L0_attn_lambda_init, np.float32)
    Wg = np.asarray(L0_ffn_W_g, np.float32)
    Wu = np.asarray(L0_ffn_W_u, np.float32)
    Wd = np.asarray(L0_ffn_W_d, np.float32)

    def qkv_img(W, c):
        S = W[:, 256 * c:256 * (c + 1)]
        return np.ascontiguousarray(
            S.reshape(16, 128, 2, 128).transpose(1, 2, 0, 3).reshape(128, 4096))

    wg_img = np.ascontiguousarray(
        Wg.reshape(64, 128, 16, 128).transpose(0, 3, 2, 1).reshape(64, 128, 2048))
    wu_img = np.ascontiguousarray(
        Wu.reshape(64, 128, 16, 128).transpose(0, 3, 2, 1).reshape(64, 128, 2048))
    wd_img = np.ascontiguousarray(
        Wd.reshape(4, 512, 64, 128).transpose(0, 2, 3, 1))
    msk = _masks()

    in_maps = []
    for c in range(NC):
        wo_c = np.ascontiguousarray(
            Wo[256 * c:256 * (c + 1)].reshape(2, 128, 4, 512).transpose(0, 2, 1, 3))
        in_maps.append({
            "x": np.ascontiguousarray(np.concatenate(
                [x[c * 512:(c + 1) * 512], x[T + c * 512:T + (c + 1) * 512]])),
            "wq": qkv_img(Wq, c), "wk": qkv_img(Wk, c), "wv": qkv_img(Wv, c),
            "wo": wo_c, "wg": wg_img, "wu": wu_img, "wd": wd_img,
            "lam": np.ascontiguousarray(lam[2 * c:2 * c + 2].reshape(1, 2)),
            "msk": msk,
        })

    nc = build_program()
    res = run_bass_kernel_spmd(nc, in_maps, core_ids=list(range(NC)))
    out = np.empty((B * T, D), np.float32)
    for c in range(NC):
        yc = np.asarray(res.results[c]["y"], np.float32)
        out[c * 512:(c + 1) * 512] = yc[0:512]
        out[T + c * 512:T + (c + 1) * 512] = yc[512:1024]
    return out.reshape(B, T, D)


if __name__ == "__main__":
    import reference as ref
    ins = ref.setup_inputs(0)
    got = kernel(**{k: np.asarray(v) for k, v in ins.items()})
    exp = np.asarray(ref.reference(**ins))
    err = np.abs(got - exp)
    rv = float(np.mean((got - exp) ** 2) / np.mean(np.asarray(exp) ** 2))
    print("maxabs", err.max(), "resvar", rv)


# version 29
# speedup vs baseline: 1.0527x; 1.0361x over previous
import numpy as np
import concourse.bass as bass

DEBUG_TAPS = False
FLUSH_N = {"tensor": 1, "scalar": 1, "vector": 1, "sync": 1, "gpsimd": 1}

import concourse.mybir as mybir
from concourse.bass_utils import run_bass_kernel_spmd
from contextlib import ExitStack

F32 = mybir.dt.float32
BF16 = mybir.dt.bfloat16
FP16 = mybir.dt.float16
AF = mybir.ActivationFunctionType
ALU = mybir.AluOpType
AX = mybir.AxisListType

B, T, D = 2, 4096, 2048
H = 16
HF = 8192
NC = 8
R = (B * T) // NC
NT = R // 128
NDC = D // 128
SCALE = 0.125


class Prog:
    def __init__(self):
        nc = bass.Bass("TRN2", target_bir_lowering=False)
        self.nc = nc
        self.es = ExitStack()
        ap = lambda name, shape, dt, kind: nc.dram_tensor(name, shape, dt, kind=kind).ap()
        self.x = ap("x", [R, D], F32, "ExternalInput")
        self.wq = ap("wq", [128, 4096], F32, "ExternalInput")
        self.wk = ap("wk", [128, 4096], F32, "ExternalInput")
        self.wv = ap("wv", [128, 4096], F32, "ExternalInput")
        self.wo = ap("wo", [2, 4, 128, 512], F32, "ExternalInput")
        self.wg = ap("wg", [64, 128, 2048], F32, "ExternalInput")
        self.wu = ap("wu", [64, 128, 2048], F32, "ExternalInput")
        self.wd = ap("wd", [4, 64, 128, 512], F32, "ExternalInput")
        self.lam = ap("lam", [1, 2], F32, "ExternalInput")
        self.msk = ap("msk", [4, 128, 1024], F32, "ExternalInput")
        self.y = ap("y", [R, D], F32, "ExternalOutput")
        KI = "ExternalOutput" if DEBUG_TAPS else "Internal"
        aps = lambda name, shape, dt: nc.dram_tensor(name, shape, dt, kind="Internal",
                                                     addr_space="Shared").ap()
        self.cc_xn = ap("cc_xn", [R, D], BF16, "Internal")
        self.ag_xn = aps("ag_xn", [NC, R, D], BF16)
        self.cc_o = ap("cc_o", [2, 2, 128, T], BF16, KI)
        self.cc_s = [ap(f"cc_s{i}", [T, D], BF16, "Internal") for i in range(2)]
        self.cc_g = ap("cc_g", [R, D], BF16, KI)
        self.rs_s = [ap(f"rs_s{i}", [512, D], BF16, "Internal") for i in range(2)]
        self.wq_d = ap("wq_d", [128, 4096], BF16, "Internal")
        self.wk_d = ap("wk_d", [128, 4096], BF16, "Internal")
        self.wv_d = ap("wv_d", [128, 4096], BF16, "Internal")
        self.msk_d = ap("msk_d", [128, 4096], BF16, "Internal")
        self.wob_d = ap("wob_d", [128, 4096], BF16, "Internal")
        if DEBUG_TAPS:
            self.tp_xn = ap("tp_xn", [R, D], BF16, "ExternalOutput")
            self.tp_s = ap("tp_s", [NC * R, D], BF16, "ExternalOutput")
            self.tp_rs = ap("tp_rs", [R, D], BF16, "ExternalOutput")

        E = self.es.enter_context
        self.s_t = E(nc.semaphore("s_t"))
        self.s_a = E(nc.semaphore("s_a"))
        self.s_v = E(nc.semaphore("s_v"))
        self.s_s = E(nc.semaphore("s_s"))
        self.s_g = E(nc.semaphore("s_g"))
        self.s_c = E(nc.semaphore("s_c"))
        self.s_w = E(nc.semaphore("s_w"))
        self.cnt = {"t": 0, "a": 0, "v": 0, "s": 0, "g": 0, "c": 0, "w": 0}
        self.sem = {"t": self.s_t, "a": self.s_a, "v": self.s_v,
                    "s": self.s_s, "g": self.s_g, "c": self.s_c, "w": self.s_w}

        sb = lambda n, s, d: E(nc.sbuf_tensor(n, s, d))
        self.xqT = None
        self.ones_col = sb("ones_col", [128, 1], BF16)
        self.ones_mat = sb("ones_mat", [128, 128], BF16)
        self.ones_row = sb("ones_row", [1, 128], BF16)
        self.onesr_f = sb("onesr_f", [1, 128], F32)
        self.b1536 = sb("b1536", [128, 1], F32)
        self.lamneg = sb("lamneg", [128, 2], F32)
        self.lam_sig = sb("lam_sig", [1, 2], F32)
        self.lam_sb = sb("lam_sb", [1, 2], F32)

        self.block = E(nc.Block())
    def sec(self, eng, fn):
        getattr(self.block, eng)(fn)

    def flush(self):
        pass

    def inc(self, instr, key, by=None):
        by = by if by is not None else (16 if key in ("s", "g", "w") else 1)
        instr.then_inc(self.sem[key], by)
        self.cnt[key] += by
        return self.cnt[key]

    def barrier(self):
        self.flush()
        snap = dict(self.cnt)

        def mk():
            def f(e):
                for k in ("t", "a", "v", "s", "g", "w"):
                    if snap[k]:
                        e.wait_ge(self.sem[k], snap[k])
            return f
        for eng in ("tensor", "scalar", "vector", "sync", "gpsimd"):
            self.sec(eng, mk())

    def setup(self):
        p = self

        def vec(v):
            v.memset(p.ones_col[:], 1.0)
            v.memset(p.ones_mat[:], 1.0)
            v.memset(p.ones_row[:], 1.0)
            v.memset(p.onesr_f[:], 1.0)
            p.inc(v.memset(p.b1536[:], 1536.0), "v")
        p.sec("vector", vec)

        def gp(g):
            p.inc(g.dma_start(out=p.lam_sb[:], in_=p.lam), "g")
        p.sec("gpsimd", gp)
        self.g_lam = p.cnt["g"]

    def phase1(self):
        p, nc = self, self.nc
        with ExitStack() as es:
            E = es.enter_context
            x_f = [E(nc.sbuf_tensor(f"x_f{i}", [128, D], F32)) for i in range(2)]
            xc_f = [E(nc.sbuf_tensor(f"xc_f{i}", [128, D], F32)) for i in range(2)]
            xn_f = [E(nc.sbuf_tensor(f"xn_f{i}", [128, D], F32)) for i in range(2)]
            xnn_f = [E(nc.sbuf_tensor(f"xnn_f{i}", [128, D], F32)) for i in range(2)]
            xn_bf = [E(nc.sbuf_tensor(f"xn_bf{i}", [128, D], BF16)) for i in range(2)]
            xq_bf = [E(nc.sbuf_tensor(f"xq_bf{i}", [128, D], BF16)) for i in range(2)]
            f16q = [E(nc.sbuf_tensor(f"f16q{i}", [128, D], FP16)) for i in range(2)]
            xq_f = [E(nc.sbuf_tensor(f"xq_f{i}", [128, D], F32)) for i in range(2)]
            dump = E(nc.sbuf_tensor("dump", [128, D], F32))
            c2 = lambda n: [E(nc.sbuf_tensor(n + str(i), [128, 1], F32)) for i in range(2)]
            s_sum, s_sq, mneg, e2eps, msqn, vare, sqv, rs = (
                c2("s_sum"), c2("s_sq"), c2("mneg"), c2("e2eps"), c2("msqn"),
                c2("vare"), c2("sqv"), c2("rs"))
            s_sum2, s_sq2, m2neg, e2eps2, msqn2, vare2, sqv2, rs2 = (
                c2("s_sum2"), c2("s_sq2"), c2("m2neg"), c2("e2eps2"), c2("msqn2"),
                c2("vare2"), c2("sqv2"), c2("rs2"))
            amax, rtmp, sq, invsq, vgap = c2("amax"), c2("rtmp"), c2("sq"), c2("invsq"), c2("vgap")

            a_last = [0, 0]
            v1_of = [0] * (NT + 2)
            s_tr_of = [0] * NT
            g_xn_of = [0, 0]

            for i0 in range(0, NT, 2):
                pair = (i0, i0 + 1)

                def lx(sy, pair=pair, an=(a_last[0], a_last[1])):
                    for w, i in enumerate(pair):
                        if an[i % 2]:
                            sy.wait_ge(p.s_a, an[i % 2])
                        p.inc(sy.dma_start(out=x_f[i % 2][:], in_=p.x[i * 128:(i + 1) * 128, :]), "s")
                p.sec("sync", lx)
                s_x = p.cnt["s"]

                a0 = p.cnt["a"]
                v0 = p.cnt["v"]

                def asec(s, pair=pair, s_x=s_x, a0=a0, v0=v0,
                         vn=(v1_of[i0 - 2] if i0 >= 2 else 0,
                             v1_of[i0 - 1] if i0 >= 1 else 0)):
                    for w, i in enumerate(pair):
                        par = i % 2
                        s.wait_ge(p.s_s, s_x - 16 * (1 - w))
                        if vn[par]:
                            s.wait_ge(p.s_v, vn[par])
                        s.activation(dump[:], x_f[par][:], AF.Identity, accum_out=s_sum[par][:, 0:1])
                        p.inc(s.activation(dump[:], x_f[par][:], AF.Square,
                                           accum_out=s_sq[par][:, 0:1]), "a")
                    for w in range(2):
                        par = w
                        s.wait_ge(p.s_v, v0 + 1 + w)
                        s.activation(xc_f[par][:], x_f[par][:], AF.Identity, bias=mneg[par][:, 0:1])
                        p.inc(s.activation(msqn[par][:], mneg[par][:], AF.Square), "a")
                    for w in range(2):
                        par = w
                        s.wait_ge(p.s_v, v0 + 3 + w)
                        p.inc(s.activation(sqv[par][:], vare[par][:], AF.Sqrt), "a")
                    for w in range(2):
                        par = w
                        s.wait_ge(p.s_v, v0 + 5 + w)
                        s.activation(xn_f[par][:], xc_f[par][:], AF.Identity, scale=rs[par][:, 0:1],
                                     accum_out=s_sum2[par][:, 0:1])
                        p.inc(s.activation(dump[:], xn_f[par][:], AF.Square,
                                           accum_out=s_sq2[par][:, 0:1]), "a")
                    for w in range(2):
                        par = w
                        s.wait_ge(p.s_v, v0 + 7 + w)
                        s.activation(xc_f[par][:], xn_f[par][:], AF.Identity, bias=m2neg[par][:, 0:1])
                        p.inc(s.activation(msqn2[par][:], m2neg[par][:], AF.Square), "a")
                    for w in range(2):
                        par = w
                        s.wait_ge(p.s_v, v0 + 9 + w)
                        p.inc(s.activation(sqv2[par][:], vare2[par][:], AF.Sqrt), "a")
                    for w in range(2):
                        par = w
                        s.wait_ge(p.s_v, v0 + 11 + w)
                        p.inc(s.activation(xnn_f[par][:], xc_f[par][:], AF.Identity,
                                           scale=rs2[par][:, 0:1]), "a")
                    for w in range(2):
                        par = w
                        s.wait_ge(p.s_v, v0 + 13 + w)
                        p.inc(s.activation(f16q[par][:], xnn_f[par][:], AF.Identity,
                                           bias=p.b1536[:, 0:1], scale=sq[par][:, 0:1]), "a")
                p.sec("scalar", asec)
                assert p.cnt["a"] == a0 + 16

                def vsec(v, a0=a0, v0=v0, gn=(g_xn_of[0], g_xn_of[1]),
                         sn=(s_tr_of[i0 - 2] if i0 >= 2 else 0,
                             s_tr_of[i0 - 1] if i0 >= 1 else 0)):
                    for w in range(2):
                        par = w
                        v.wait_ge(p.s_a, a0 + 1 + w)
                        v.tensor_scalar(mneg[par][:], s_sum[par][:], -1.0 / D, None, op0=ALU.mult)
                        p.inc(v.tensor_scalar(e2eps[par][:], s_sq[par][:], 1.0 / D, 1e-5,
                                              op0=ALU.mult, op1=ALU.add), "v")
                    for w in range(2):
                        par = w
                        v.wait_ge(p.s_a, a0 + 3 + w)
                        p.inc(v.tensor_tensor(vare[par][:], e2eps[par][:], msqn[par][:],
                                              op=ALU.subtract), "v")
                    for w in range(2):
                        par = w
                        v.wait_ge(p.s_a, a0 + 5 + w)
                        p.inc(v.reciprocal(rs[par][:], sqv[par][:]), "v")
                    for w in range(2):
                        par = w
                        v.wait_ge(p.s_a, a0 + 7 + w)
                        if gn[par]:
                            v.wait_ge(p.s_g, gn[par])
                        v.tensor_scalar(m2neg[par][:], s_sum2[par][:], -1.0 / D, None, op0=ALU.mult)
                        v.tensor_scalar(e2eps2[par][:], s_sq2[par][:], 1.0 / D, 1e-8,
                                        op0=ALU.mult, op1=ALU.add)
                        p.inc(v.tensor_copy(xn_bf[par][:], xn_f[par][:]), "v")
                    for w in range(2):
                        par = w
                        v.wait_ge(p.s_a, a0 + 9 + w)
                        p.inc(v.tensor_tensor(vare2[par][:], e2eps2[par][:], msqn2[par][:],
                                              op=ALU.subtract), "v")
                    for w in range(2):
                        par = w
                        v.wait_ge(p.s_a, a0 + 11 + w)
                        p.inc(v.reciprocal(rs2[par][:], sqv2[par][:]), "v")
                    for w in range(2):
                        par = w
                        v.wait_ge(p.s_a, a0 + 13 + w)
                        v.reduce_max(amax[par][:, 0:1], xnn_f[par][:], axis=AX.X,
                                     apply_absolute_value=True)
                        v.tensor_scalar(vgap[par][:], amax[par][:], 1.0, None, op0=ALU.mult)
                        v.reciprocal(rtmp[par][:], amax[par][:])
                        v.tensor_scalar(invsq[par][:], amax[par][:], 1.0 / 127.0, None, op0=ALU.mult)
                        p.inc(v.tensor_scalar(sq[par][:], rtmp[par][:], 127.0, None, op0=ALU.mult), "v")
                    for w in range(2):
                        par = w
                        v.wait_ge(p.s_a, a0 + 15 + w)
                        if sn[par]:
                            v.wait_ge(p.s_s, sn[par])
                        v.tensor_scalar(xq_f[par][:], f16q[par][:], -1536.0, invsq[par][:, 0:1],
                                        op0=ALU.add, op1=ALU.mult)
                        p.inc(v.tensor_copy(xq_bf[par][:], xq_f[par][:]), "v")
                p.sec("vector", vsec)
                assert p.cnt["v"] == v0 + 16
                v1_of[i0] = v0 + 1
                v1_of[i0 + 1] = v0 + 2
                a_last[i0 % 2] = a0 + 15
                a_last[(i0 + 1) % 2] = a0 + 16

                def gs(g, pair=pair, v0=v0):
                    for w, i in enumerate(pair):
                        par = i % 2
                        g.wait_ge(p.s_v, v0 + 7 + w)
                        p.inc(g.dma_start(out=p.cc_xn[i * 128:(i + 1) * 128, :],
                                          in_=xn_bf[par][:]), "g")
                p.sec("gpsimd", gs)
                g_xn_of[i0 % 2] = p.cnt["g"] - 16
                g_xn_of[(i0 + 1) % 2] = p.cnt["g"]

                def tr(sy, pair=pair, v0=v0):
                    for w, i in enumerate(pair):
                        par = i % 2
                        sy.wait_ge(p.s_v, v0 + 15 + w)
                        for d in range(NDC):
                            p.inc(sy.dma_start(
                                out=p.xqT[:, d * R + i * 128: d * R + (i + 1) * 128],
                                in_=xq_bf[par][:, d * 128:(d + 1) * 128], transpose=True), "s")
                p.sec("sync", tr)
                s_tr_of[i0] = p.cnt["s"] - 16 * NDC
                s_tr_of[i0 + 1] = p.cnt["s"]
                p.flush()


            self.s_xqT = p.cnt["s"]
            g_all = p.cnt["g"]

            def coll(g, g_all=g_all):
                g.wait_ge(p.s_g, g_all)
                p.inc(g.collective_compute("AllGather", ALU.bypass, ins=[p.cc_xn],
                                           outs=[p.ag_xn],
                                           replica_groups=[list(range(NC))]), "c")
            p.sec("gpsimd", coll)
            self.c_xn = p.cnt["c"]
            self.barrier()

    def ffn(self):
        p, nc = self, self.nc
        with nc.psum_tensor("ps_lam", [128, 2], F32) as ps_lam:
            def alam(s, gl=self.g_lam):
                s.wait_ge(p.s_g, gl)
                p.inc(s.activation(p.lam_sig[:], p.lam_sb[:], AF.Sigmoid), "a")
            p.sec("scalar", alam)

            def tlam(t, a_lam=p.cnt["a"]):
                t.wait_ge(p.s_a, a_lam)
                p.inc(t.matmul(ps_lam[:], p.onesr_f[:], p.lam_sig[:], start=True, stop=True), "t")
            p.sec("tensor", tlam)

            def vlam(v, t_lam=p.cnt["t"]):
                v.wait_ge(p.s_t, t_lam)
                p.inc(v.tensor_scalar(p.lamneg[:], ps_lam[:], -1.0, None, op0=ALU.mult), "v")
            p.sec("vector", vlam)

        def wstage(g):
            p.inc(g.dma_start(out=p.wq_d, in_=p.wq), "g")
            p.inc(g.dma_start(out=p.wk_d, in_=p.wk), "g")
            p.inc(g.dma_start(out=p.wv_d, in_=p.wv), "g")
            p.inc(g.dma_start(out=p.msk_d.rearrange("b (a c) -> b a c", c=1024),
                              in_=p.msk.rearrange("a b c -> b a c")), "g")
            p.inc(g.dma_start(out=p.wob_d.rearrange("p (a b f) -> p a b f", b=4, f=512),
                              in_=p.wo.rearrange("a b p f -> p a b f")), "g")
        p.sec("gpsimd", wstage)
        self.g_stage = p.cnt["g"]

        with ExitStack() as es:
            E = es.enter_context
            slab = [E(nc.sbuf_tensor(f"wslab{i}", [128, 2048], F32)) for i in range(2)]
            sgW = [E(nc.sbuf_tensor(f"sgW{i}", [128, 2048], BF16)) for i in range(4)]
            sgm = E(nc.sbuf_tensor("sgm", [128, 1024], F32))
            sil = E(nc.sbuf_tensor("sil", [128, 1024], F32))
            pT = E(nc.sbuf_tensor("pT", [128, 16 * R], BF16))
            f_sb = E(nc.sbuf_tensor("f_sb", [128, NT * D], F32))
            wdf = [E(nc.sbuf_tensor(f"wdf{i}", [128, 2048], F32)) for i in range(2)]
            wdb = [E(nc.sbuf_tensor(f"wdb{i}", [128, 2048], BF16)) for i in range(2)]
            g_t = [E(nc.sbuf_tensor(f"g_t{i}", [128, D], BF16)) for i in range(2)]

            sign_a = 0
            sg_mm_t = [0, 0]
            a_w_of = {}
            evac_v = 0

            def emit_lw(hb):
                def lw(sy, hb=hb, fa=sign_a):
                    if fa:
                        sy.wait_ge(p.s_a, fa)
                    p.inc(sy.dma_start(out=slab[0][:], in_=p.wg[hb]), "s")
                    p.inc(sy.dma_start(out=slab[1][:], in_=p.wu[hb]), "s")
                p.sec("sync", lw)
                return p.cnt["s"]

            for q in range(4):
                with nc.psum_tensor(f"ps_yga{q}", [128, 1024], F32) as ps_yga, \
                     nc.psum_tensor(f"ps_yua{q}", [128, 1024], F32) as ps_yua, \
                     nc.psum_tensor(f"ps_ygb{q}", [128, 1024], F32) as ps_ygb, \
                     nc.psum_tensor(f"ps_yub{q}", [128, 1024], F32) as ps_yub:
                    evac_pair = [0, 0]
                    if q == 0:
                        s_w0 = emit_lw(0)

                        def sw0(sct, s_w0=s_w0):
                            sct.wait_ge(p.s_s, s_w0)
                            sct.activation(sgW[0][:], slab[0][:], AF.Sign)
                            p.inc(sct.activation(sgW[1][:], slab[1][:], AF.Sign), "a")
                        p.sec("scalar", sw0)
                        a_w_of[0] = p.cnt["a"]
                        sign_a = p.cnt["a"]
                    for jb in range(16):
                        hb = q * 16 + jb
                        par = hb % 2
                        ps_yg = ps_yga if par == 0 else ps_ygb
                        ps_yu = ps_yua if par == 0 else ps_yub

                        s_wn = emit_lw(hb + 1) if hb + 1 < 64 else 0

                        def mm(t, par=par, a_w=a_w_of[hb], ev=evac_pair[par],
                               s_xqT=self.s_xqT, first=(q == 0 and jb == 0),
                               ev0=(evac_v if jb <= 1 else 0)):
                            if first:
                                t.wait_ge(p.s_s, s_xqT)
                            t.wait_ge(p.s_a, a_w)
                            if ev:
                                t.wait_ge(p.s_v, ev)
                            elif ev0:
                                t.wait_ge(p.s_v, ev0)
                            mmi = None
                            for which in range(2):
                                dst = ps_yg if which == 0 else ps_yu
                                src = sgW[2 * par + which]
                                for th in range(2):
                                    for j in range(NDC):
                                        mmi = t.matmul(dst[:, th * 512:(th + 1) * 512],
                                                       src[:, j * 128:(j + 1) * 128],
                                                       p.xqT[:, j * R + th * 512: j * R + (th + 1) * 512],
                                                       start=(j == 0), stop=(j == NDC - 1),
                                                       skip_group_check=True)
                            p.inc(mmi, "t")
                        p.sec("tensor", mm)
                        t_mm = p.cnt["t"]
                        sg_mm_t[par] = t_mm

                        def ev_a(sct, hb=hb, s_wn=s_wn, t_mm=t_mm, v_prev=evac_v,
                                 npar=(hb + 1) % 2, ftn=sg_mm_t[(hb + 1) % 2]):
                            if s_wn:
                                sct.wait_ge(p.s_s, s_wn)
                                if ftn:
                                    sct.wait_ge(p.s_t, ftn)
                                sct.activation(sgW[2 * npar][:], slab[0][:], AF.Sign)
                                p.inc(sct.activation(sgW[2 * npar + 1][:], slab[1][:],
                                                     AF.Sign), "a")
                            sct.wait_ge(p.s_t, t_mm)
                            if v_prev:
                                sct.wait_ge(p.s_v, v_prev)
                            p.inc(sct.activation(sgm[:], ps_yg[:], AF.Sigmoid), "a")
                        p.sec("scalar", ev_a)
                        if hb + 1 < 64:
                            a_w_of[hb + 1] = p.cnt["a"] - 1
                            sign_a = p.cnt["a"] - 1

                        def ev_v(v, jb=jb, a_sg=p.cnt["a"], t_mm=t_mm):
                            v.wait_ge(p.s_a, a_sg)
                            v.tensor_tensor(sil[:], ps_yg[:], sgm[:], op=ALU.mult)
                            p.inc(v.tensor_tensor(pT[:, jb * R:(jb + 1) * R], sil[:], ps_yu[:],
                                                  op=ALU.mult), "v")
                        p.sec("vector", ev_v)
                        evac_v = p.cnt["v"]

                v_pT = p.cnt["v"]
                if q == 0:
                    wd_sign_a = [0, 0]
                    wd_mm_t = [0, 0]
                with nc.psum_tensor(f"ps_f{q}", [128, 8 * 512], F32) as ps_f:
                    bank_ev = [0] * 8
                    for dt in range(4):
                        for jh in range(2):
                            g_wd_of = [0, 0]
                            for par in range(2):
                                jq = 2 * jh + par

                                def lwd(g, dt=dt, hb0=q * 16 + jq * 4, par=par,
                                        fa=wd_sign_a[par]):
                                    if fa:
                                        g.wait_ge(p.s_a, fa)
                                    p.inc(g.dma_start(
                                        out=wdf[par][:].rearrange("p (c f) -> p c f", f=512),
                                        in_=p.wd[dt, hb0:hb0 + 4].rearrange("c p f -> p c f")), "g")
                                p.sec("gpsimd", lwd)
                                g_wd_of[par] = p.cnt["g"]

                            def swd2(sct, g_wd_of=list(g_wd_of),
                                     fts=(wd_mm_t[0], wd_mm_t[1])):
                                for par in range(2):
                                    sct.wait_ge(p.s_g, g_wd_of[par])
                                    if fts[par]:
                                        sct.wait_ge(p.s_t, fts[par])
                                    p.inc(sct.activation(wdb[par][:], wdf[par][:],
                                                         AF.Sign), "a")
                            p.sec("scalar", swd2)
                            wd_sign_a[0] = p.cnt["a"] - 1
                            wd_sign_a[1] = p.cnt["a"]

                            for par in range(2):
                                jq = 2 * jh + par
                                a_wd = p.cnt["a"] - 1 + par

                                def mm3(t, jq=jq, par=par, a_wd=a_wd, v_pT=v_pT,
                                        ev=(list(bank_ev) if jq == 0 else None)):
                                    t.wait_ge(p.s_a, a_wd)
                                    t.wait_ge(p.s_v, v_pT)
                                    mmi = None
                                    for tb in range(8):
                                        if ev and ev[tb]:
                                            t.wait_ge(p.s_v, ev[tb])
                                        for jc in range(4):
                                            j = jq * 4 + jc
                                            mmi = t.matmul(ps_f[:, tb * 512:(tb + 1) * 512],
                                                           pT[:, j * R + tb * 128: j * R + tb * 128 + 128],
                                                           wdb[par][:, jc * 512:(jc + 1) * 512],
                                                           start=(j == 0), stop=(j == 15),
                                                           skip_group_check=True)
                                    p.inc(mmi, "t")
                                p.sec("tensor", mm3)
                                wd_mm_t[par] = p.cnt["t"]
                        t_dt = p.cnt["t"]

                        def ev3(v, q=q, dt=dt, t_dt=t_dt):
                            v.wait_ge(p.s_t, t_dt)
                            ins = None
                            for tb in range(8):
                                src = ps_f[:, tb * 512:(tb + 1) * 512]
                                dst = f_sb[:, tb * D + dt * 512: tb * D + (dt + 1) * 512]
                                if q == 0:
                                    ins = v.tensor_copy(dst, src)
                                else:
                                    ins = v.tensor_tensor(dst, dst, src, op=ALU.add)
                            p.inc(ins, "v")
                        p.sec("vector", ev3)
                        for tb in range(8):
                            bank_ev[tb] = p.cnt["v"]
                evac_v = p.cnt["v"]

            v_f = p.cnt["v"]
            g_store = [0, 0]
            for tb in range(NT):
                par = tb % 2

                def gsig(s, tb=tb, v_f=v_f):
                    if tb == 0:
                        s.wait_ge(p.s_v, v_f)
                    p.inc(s.activation(sgm[:], f_sb[:, tb * D: tb * D + 1024], AF.Sigmoid), "a")
                    p.inc(s.activation(sil[:], f_sb[:, tb * D + 1024: tb * D + 2048],
                                       AF.Sigmoid), "a")
                p.sec("scalar", gsig)

                def gmul(v, tb=tb, par=par, a_g=p.cnt["a"], g_need=g_store[par]):
                    v.wait_ge(p.s_a, a_g)
                    if g_need:
                        v.wait_ge(p.s_g, g_need)
                    v.tensor_tensor(g_t[par][:, 0:1024],
                                    f_sb[:, tb * D: tb * D + 1024], sgm[:], op=ALU.mult)
                    p.inc(v.tensor_tensor(g_t[par][:, 1024:2048],
                                          f_sb[:, tb * D + 1024: tb * D + 2048], sil[:],
                                          op=ALU.mult), "v")
                p.sec("vector", gmul)

                def gst(g, tb=tb, par=par, v_g=p.cnt["v"]):
                    g.wait_ge(p.s_v, v_g)
                    p.inc(g.dma_start(out=p.cc_g[tb * 128:(tb + 1) * 128, :], in_=g_t[par][:]), "g")
                p.sec("gpsimd", gst)
                g_store[par] = p.cnt["g"]
            self.barrier()

    def attention(self):
        p, nc = self, self.nc
        with ExitStack() as es:
            E = es.enter_context
            wqb = E(nc.sbuf_tensor("wqb", [128, 4096], BF16))
            wkb = E(nc.sbuf_tensor("wkb", [128, 4096], BF16))
            wvb = E(nc.sbuf_tensor("wvb", [128, 4096], BF16))
            mask_bf = E(nc.sbuf_tensor("mask_bf", [128, 4096], BF16))
            qT = E(nc.sbuf_tensor("qT", [128, 2 * T], BF16))
            kT = E(nc.sbuf_tensor("kT", [128, 2 * T], BF16))
            vT = E(nc.sbuf_tensor("vT", [128, 2 * T], BF16))
            vtok = E(nc.sbuf_tensor("vtok", [128, 2 * T], BF16))
            xnT = [E(nc.sbuf_tensor(f"xnT{i}", [128, NDC * 512], BF16)) for i in range(2)]
            strip = E(nc.sbuf_tensor("strip", [128, 4 * 1024], BF16))
            mtmp = [E(nc.sbuf_tensor(f"mtmp{i}", [128, 1024], BF16)) for i in range(2)]
            rl = E(nc.sbuf_tensor("rl", [1, 1024], F32))
            rl_bf = E(nc.sbuf_tensor("rl_bf", [1, 1024], BF16))
            Bsb = E(nc.sbuf_tensor("Bsb", [128, 1024], F32))
            Osb = E(nc.sbuf_tensor("Osb", [128, 1024], F32))
            cmb = E(nc.sbuf_tensor("cmb", [128, 1024], F32))
            ocm = E(nc.sbuf_tensor("ocm", [128, T], BF16))
            self.wob = E(nc.sbuf_tensor("wob", [128, 2 * 4 * 512], BF16))
            self.oc = [E(nc.sbuf_tensor(f"oc{i}", [128, 2 * 1024], BF16)) for i in range(4)]
            self.s_sb = [E(nc.sbuf_tensor(f"ssb{i}", [128, D], BF16)) for i in range(2)]
            self.wo_st = {"oc_mm_t": [0, 0, 0, 0], "sb_dma_g": [0, 0],
                          "bank_ev": [0, 0, 0, 0], "c_rs": [0, 0]}
            g_ocm_of = [0, 0]

            def gw(sy, g_stage=self.g_stage):
                sy.wait_ge(p.s_g, g_stage)
                p.inc(sy.dma_start(out=wqb[:], in_=p.wq_d), "s")
                p.inc(sy.dma_start(out=wkb[:], in_=p.wk_d), "s")
                p.inc(sy.dma_start(out=wvb[:], in_=p.wv_d), "s")
                p.inc(sy.dma_start(out=mask_bf[:], in_=p.msk_d), "s")
                p.inc(sy.dma_start(out=self.wob[:], in_=p.wob_d), "s")
            p.sec("sync", gw)
            g_w = p.cnt["s"]
            self.g_wo = p.cnt["s"]

            agxn = p.ag_xn.rearrange("a b c -> (a b) c")
            xnT_mm_t = [0, 0]
            g_ocm = 0
            for b in range(2):
                with nc.psum_tensor(f"ps_qkv{b}", [128, 4 * 512], F32) as ps4:
                    slot_ev = [0, 0, 0, 0]
                    for ts in range(8):
                        par = ts % 2

                        def ltr(sy, b=b, ts=ts, par=par, t_need=xnT_mm_t[par],
                                c_xn=self.c_xn):
                            sy.wait_ge(p.s_c, c_xn)
                            if t_need:
                                sy.wait_ge(p.s_t, t_need)
                            base = ts * 1024 + b * 512
                            for d in range(NDC):
                                p.inc(sy.dma_start(
                                    out=xnT[par][:, d * 512:(d + 1) * 512],
                                    in_=agxn[base:base + 512, d * 128:(d + 1) * 128],
                                    transpose=True), "s")
                        p.sec("sync", ltr)
                        s_tr = p.cnt["s"]

                        t_base = p.cnt["t"]
                        v_base = p.cnt["v"]

                        def mmqkv(t, par=par, s_tr=s_tr, g_w=g_w, ev=list(slot_ev),
                                  v_base=v_base, first=(ts == 0),
                                  a_pre=p.cnt["a"], v_pre=p.cnt["v"]):
                            t.wait_ge(p.s_s, max(s_tr, g_w))
                            if first:
                                t.wait_ge(p.s_a, a_pre)
                                t.wait_ge(p.s_v, v_pre)
                            k = 0
                            for hb in range(2):
                                for wsl in (wqb, wkb, wvb):
                                    pi = k % 4
                                    if k >= 4:
                                        t.wait_ge(p.s_v, v_base + (k - 4) + 1)
                                    elif ev[pi]:
                                        t.wait_ge(p.s_v, ev[pi])
                                    mmi = None
                                    for j in range(NDC):
                                        mmi = t.matmul(
                                            ps4[:, pi * 512:(pi + 1) * 512],
                                            wsl[:, (hb * 16 + j) * 128:(hb * 16 + j + 1) * 128],
                                            xnT[par][:, j * 512:(j + 1) * 512],
                                            start=(j == 0), stop=(j == NDC - 1),
                                            skip_group_check=True)
                                    p.inc(mmi, "t")
                                    k += 1
                        p.sec("tensor", mmqkv)
                        xnT_mm_t[par] = p.cnt["t"]

                        def evqkv(v, ts=ts, t_base=t_base):
                            k = 0
                            for hb in range(2):
                                for dst in (qT, kT, vT):
                                    pi = k % 4
                                    v.wait_ge(p.s_t, t_base + k + 1)
                                    p.inc(v.tensor_copy(
                                        dst[:, hb * T + ts * 512:(hb * T + (ts + 1) * 512)],
                                        ps4[:, pi * 512:(pi + 1) * 512]), "v")
                                    k += 1
                        p.sec("vector", evqkv)
                        for k in range(4):
                            slot_ev[k] = p.cnt["v"]
                        p.flush()

                def vtr(sy, v_q=p.cnt["v"], t_pre=p.cnt["t"]):
                    sy.wait_ge(p.s_v, v_q)
                    sy.wait_ge(p.s_t, t_pre)
                    for hb in range(2):
                        for sc in range(32):
                            p.inc(sy.dma_start(
                                out=vtok[:, hb * T + sc * 128: hb * T + (sc + 1) * 128],
                                in_=vT[:, hb * T + sc * 128: hb * T + (sc + 1) * 128],
                                transpose=True), "s")
                p.sec("sync", vtr)
                s_vt = p.cnt["s"]
                if b == 1:
                    self.wo_batch(0, g_ocm_of[0], p.cnt["v"])

                for hb in range(2):
                    with nc.psum_tensor(f"ps_s{b}{hb}", [128, 2 * 1024], F32) as ps_s, \
                         nc.psum_tensor(f"ps_O1{b}{hb}", [128, 512], F32) as ps_O1, \
                         nc.psum_tensor(f"ps_O2{b}{hb}", [128, 512], F32) as ps_O2, \
                         nc.psum_tensor(f"ps_L1{b}{hb}", [128, 512], F32) as ps_L1, \
                         nc.psum_tensor(f"ps_L2{b}{hb}", [128, 512], F32) as ps_L2:
                        exp_a_par = [0, 0]
                        bs_v_par = [0, 0]
                        o_free_v = 0
                        slot_t = [0, 0, 0, 0]
                        mtmp_v = [0, 0]
                        pending = None
                        for tt in range(8):
                            ns = 4 * (tt + 1)
                            npair = ns // 2
                            t0 = p.cnt["t"]
                            has_p = pending is not None

                            if has_p:
                                t_end_p, tt_p, g_need_p = pending

                                def pvec(v, t_end_p=t_end_p, g_need=g_need_p,
                                         hb=hb, tt_p=tt_p):
                                    v.wait_ge(p.s_t, t_end_p)
                                    if g_need:
                                        v.wait_ge(p.s_g, g_need)
                                    v.reciprocal(Bsb[:, 0:512], ps_L1[:, :])
                                    p.inc(v.reciprocal(Bsb[:, 512:1024],
                                                       ps_L2[:, :]), "v")
                                    v.tensor_tensor(cmb[:, 0:512], Osb[:, 0:512],
                                                    Bsb[:, 0:512], op=ALU.mult)
                                    v.scalar_tensor_tensor(
                                        cmb[:, 512:1024], Osb[:, 512:1024],
                                        p.lamneg[:, hb:hb + 1], Bsb[:, 512:1024],
                                        op0=ALU.mult, op1=ALU.mult)
                                    p.inc(v.tensor_tensor(
                                        ocm[:, tt_p * 512:(tt_p + 1) * 512],
                                        cmb[:, 0:512], cmb[:, 512:1024],
                                        op=ALU.add), "v")
                                p.sec("vector", pvec)
                                bs_v = p.cnt["v"] - 1
                            else:
                                bs_v = 0

                            a0 = p.cnt["a"]
                            v0m = p.cnt["v"]

                            def a_pred(idx, w):
                                return a0 + 2 * idx + w + 1

                            def m_pred(k):
                                return v0m + k + 1

                            slot_snap = {}
                            t_of = {}
                            sim_t = t0
                            fw = (p.cnt["a"], p.cnt["v"]) if tt == 0 else None
                            an0 = (exp_a_par[0], exp_a_par[1])
                            o_need0 = o_free_v
                            mv_prev = tuple(mtmp_v)

                            def tsec(t, tt=tt, hb=hb, npair=npair, t0=t0,
                                     fw=fw, an0=an0, o_need0=o_need0, bs_v=bs_v,
                                     s_vt=s_vt, a0=a0, v0m=v0m):
                                for idx in range(npair):
                                    si0 = 2 * idx
                                    if idx == 0 and fw is not None:
                                        t.wait_ge(p.s_s, s_vt)
                                        t.wait_ge(p.s_a, fw[0])
                                        t.wait_ge(p.s_v, fw[1])
                                    for w, si in enumerate((si0, si0 + 1)):
                                        if idx == 0:
                                            if an0[w]:
                                                t.wait_ge(p.s_a, an0[w])
                                        else:
                                            t.wait_ge(p.s_a, a0 + 2 * (idx - 1) + w + 1)
                                        par = si % 2
                                        t.matmul(ps_s[:, par * 1024: par * 1024 + 512],
                                                 kT[0:64, hb * T + si * 128: hb * T + (si + 1) * 128],
                                                 qT[0:64, hb * T + tt * 512: hb * T + (tt + 1) * 512],
                                                 start=True, stop=True, tile_position=(0, 0))
                                        p.inc(t.matmul(
                                            ps_s[:, par * 1024 + 512: par * 1024 + 1024],
                                            kT[64:128, hb * T + si * 128: hb * T + (si + 1) * 128],
                                            qT[64:128, hb * T + tt * 512: hb * T + (tt + 1) * 512],
                                            start=True, stop=True, tile_position=(64, 0)), "t")
                                    diag = si0 >= 4 * tt
                                    for w, si in enumerate((si0, si0 + 1)):
                                        slot = si % 4
                                        st_ = idx == 0 and w == 0
                                        sp_ = idx == npair - 1 and w == 1
                                        if diag:
                                            k = (si - 4 * tt)
                                            t.wait_ge(p.s_v, v0m + k + 1)
                                        else:
                                            t.wait_ge(p.s_a, a0 + 2 * idx + w + 1)
                                        if st_:
                                            if o_need0:
                                                t.wait_ge(p.s_v, o_need0)
                                            if bs_v:
                                                t.wait_ge(p.s_v, bs_v)
                                        t.matmul(ps_L1[:, :], p.ones_mat[:, :],
                                                 strip[:, slot * 1024: slot * 1024 + 512],
                                                 start=st_, stop=sp_,
                                                 skip_group_check=True)
                                        t.matmul(ps_L2[:, :], p.ones_mat[:, :],
                                                 strip[:, slot * 1024 + 512: (slot + 1) * 1024],
                                                 start=st_, stop=sp_,
                                                 skip_group_check=True)
                                        t.matmul(ps_O1[:],
                                                 vtok[:, hb * T + si * 128: hb * T + (si + 1) * 128],
                                                 strip[:, slot * 1024: slot * 1024 + 512],
                                                 start=st_, stop=sp_,
                                                 skip_group_check=True)
                                        p.inc(t.matmul(ps_O2[:],
                                                       vtok[:, hb * T + si * 128: hb * T + (si + 1) * 128],
                                                       strip[:, slot * 1024 + 512:(slot + 1) * 1024],
                                                       start=st_, stop=sp_,
                                                       skip_group_check=True), "t")
                            for idx in range(npair):
                                si0 = 2 * idx
                                sim_t += 2
                                t_of[si0] = sim_t - 1
                                t_of[si0 + 1] = sim_t
                                slot_snap[si0] = slot_t[si0 % 4]
                                slot_snap[si0 + 1] = slot_t[(si0 + 1) % 4]
                                sim_t += 2
                                slot_t[si0 % 4] = sim_t - 1
                                slot_t[(si0 + 1) % 4] = sim_t
                            p.sec("tensor", tsec)
                            assert p.cnt["t"] == sim_t, (p.cnt["t"], sim_t)
                            t_end = sim_t
                            exp_a_par[0] = a_pred(npair - 1, 0)
                            exp_a_par[1] = a_pred(npair - 1, 1)

                            def asec(sct, tt=tt, npair=npair, t_of=dict(t_of),
                                     slot_snap=dict(slot_snap), mv_prev=mv_prev,
                                     v0m=v0m):
                                for idx in range(npair):
                                    si0 = 2 * idx
                                    for w, si in enumerate((si0, si0 + 1)):
                                        par = si % 2
                                        slot = si % 4
                                        diag = si >= 4 * tt
                                        sct.wait_ge(p.s_t, max(t_of[si], slot_snap[si]))
                                        if diag:
                                            k = si - 4 * tt
                                            if k < 2:
                                                if mv_prev[si % 2]:
                                                    sct.wait_ge(p.s_v, mv_prev[si % 2])
                                            else:
                                                sct.wait_ge(p.s_v, v0m + k - 1)
                                        dst = (mtmp[si % 2][:] if diag
                                               else strip[:, slot * 1024:(slot + 1) * 1024])
                                        p.inc(sct.activation(
                                            dst, ps_s[:, par * 1024:(par + 1) * 1024],
                                            AF.Exp, scale=SCALE), "a")
                            p.sec("scalar", asec)
                            assert p.cnt["a"] == a0 + ns, (p.cnt["a"], a0, ns)

                            def msec(v, tt=tt, a0=a0, ns=ns,
                                     sl=(slot_t[0], slot_t[1], slot_t[2], slot_t[3]),
                                     slot_snap=dict(slot_snap)):
                                for k in range(4):
                                    si = 4 * tt + k
                                    slot = si % 4
                                    v.wait_ge(p.s_a, a0 + ns - 4 + k + 1)
                                    tsl = slot_snap[si]
                                    if tsl:
                                        v.wait_ge(p.s_t, tsl)
                                    p.inc(v.tensor_tensor(
                                        strip[:, slot * 1024:(slot + 1) * 1024],
                                        mtmp[si % 2][:],
                                        mask_bf[:, k * 1024:(k + 1) * 1024],
                                        op=ALU.mult), "v")
                            p.sec("vector", msec)
                            assert p.cnt["v"] == v0m + 4, (p.cnt["v"], v0m)
                            mtmp_v[0] = v0m + 3
                            mtmp_v[1] = v0m + 4

                            def vfl(v, t_end=t_end):
                                v.wait_ge(p.s_t, t_end)
                                v.tensor_copy(Osb[:, 0:512], ps_O1[:])
                                p.inc(v.tensor_copy(Osb[:, 512:1024], ps_O2[:]), "v")
                            p.sec("vector", vfl)
                            o_free_v = p.cnt["v"]
                            pending = (t_end, tt, (g_ocm if tt == 0 else 0))

                        t_end_p, tt_p, g_need_p = pending

                        def fvec(v, t_end_p=t_end_p, g_need=g_need_p, hb=hb,
                                 tt_p=tt_p):
                            v.wait_ge(p.s_t, t_end_p)
                            if g_need:
                                v.wait_ge(p.s_g, g_need)
                            v.reciprocal(Bsb[:, 0:512], ps_L1[:, :])
                            v.reciprocal(Bsb[:, 512:1024], ps_L2[:, :])
                            v.tensor_tensor(cmb[:, 0:512], Osb[:, 0:512],
                                            Bsb[:, 0:512], op=ALU.mult)
                            v.scalar_tensor_tensor(
                                cmb[:, 512:1024], Osb[:, 512:1024],
                                p.lamneg[:, hb:hb + 1], Bsb[:, 512:1024],
                                op0=ALU.mult, op1=ALU.mult)
                            p.inc(v.tensor_tensor(
                                ocm[:, tt_p * 512:(tt_p + 1) * 512],
                                cmb[:, 0:512], cmb[:, 512:1024], op=ALU.add), "v")
                        p.sec("vector", fvec)
                        pending = None


                    def soc(g, b=b, hb=hb, v_oc=p.cnt["v"]):
                        g.wait_ge(p.s_v, v_oc)
                        p.inc(g.dma_start(out=p.cc_o[b, hb], in_=ocm[:]), "g")
                    p.sec("gpsimd", soc)
                    g_ocm = p.cnt["g"]
                g_ocm_of[b] = g_ocm
            self.wo_batch(1, g_ocm_of[1], p.cnt["v"])
            self.barrier()

    def wo_batch(self, b, g_ocm_b, v_pre):
        p, nc = self, self.nc
        st = self.wo_st
        wob, oc, s_sb = self.wob, self.oc, self.s_sb
        with nc.psum_tensor(f"ps_sp{b}", [128, 8 * 512], F32) as ps_sp:
            g_oc_of = [0] * 4
            for tq in range(4):
                def loc(g, b=b, tq=tq, t_need=st["oc_mm_t"][tq],
                        g_ocm_b=(g_ocm_b if tq == 0 else 0)):
                    if g_ocm_b:
                        g.wait_ge(p.s_g, g_ocm_b)
                    if t_need:
                        g.wait_ge(p.s_t, t_need)
                    for hb in range(2):
                        p.inc(g.dma_start(
                            out=oc[tq][:, hb * 1024:(hb + 1) * 1024],
                            in_=p.cc_o[b, hb][:, tq * 1024:(tq + 1) * 1024]), "g")
                p.sec("gpsimd", loc)
                g_oc_of[tq] = p.cnt["g"]

            for tq in range(4):
                t_base = p.cnt["t"]
                a_base = p.cnt["a"]
                s_base = p.cnt["s"]
                ev_prev = (st["bank_ev"][0], st["bank_ev"][1])
                sb_prev = (st["sb_dma_g"][0], st["sb_dma_g"][1])
                for tb in range(8):
                    pp = tb % 2

                    def mmsp(t, tq=tq, tb=tb, pp=pp, g_oc=g_oc_of[tq],
                             g_wo=self.g_wo,
                             ev=(ev_prev[pp] if tb < 2 else a_base + tb - 1),
                             v_pre=(v_pre if tq == 0 and tb == 0 else 0)):
                        t.wait_ge(p.s_g, g_oc)
                        t.wait_ge(p.s_s, g_wo)
                        if v_pre:
                            t.wait_ge(p.s_v, v_pre)
                        if ev:
                            t.wait_ge(p.s_a, ev)
                        for dt in range(4):
                            mmi = None
                            for hb in range(2):
                                mmi = t.matmul(
                                    ps_sp[:, (pp * 4 + dt) * 512:(pp * 4 + dt + 1) * 512],
                                    oc[tq][:, hb * 1024 + tb * 128:
                                            hb * 1024 + (tb + 1) * 128],
                                    wob[:, (hb * 4 + dt) * 512:(hb * 4 + dt + 1) * 512],
                                    start=(hb == 0), stop=(hb == 1),
                                    skip_group_check=True)
                            p.inc(mmi, "t")
                    p.sec("tensor", mmsp)

                def evq(sct, t_base=t_base, s_base=s_base, sb_prev=sb_prev):
                    for tb in range(8):
                        pp = tb % 2
                        par = tb % 2
                        sn = sb_prev[par] if tb < 2 else s_base + 16 * (tb - 1)
                        if sn:
                            sct.wait_ge(p.s_s, sn)
                        sct.wait_ge(p.s_t, t_base + 4 * (tb + 1))
                        p.inc(sct.activation(s_sb[par][:, :],
                                             ps_sp[:, pp * 2048:(pp + 1) * 2048],
                                             AF.Identity), "a")
                p.sec("scalar", evq)
                assert p.cnt["a"] == a_base + 8

                for tb in range(8):
                    par = tb % 2

                    def ssb(sy, b=b, tq=tq, tb=tb, par=par, a_d=a_base + tb + 1):
                        sy.wait_ge(p.s_a, a_d)
                        row = tq * 1024 + tb * 128
                        p.inc(sy.dma_start(out=p.cc_s[b][row:row + 128, :],
                                           in_=s_sb[par][:]), "s")
                    p.sec("sync", ssb)
                assert p.cnt["s"] == s_base + 16 * 8
                st["bank_ev"][0] = a_base + 7
                st["bank_ev"][1] = a_base + 8
                st["sb_dma_g"][0] = s_base + 16 * 7
                st["sb_dma_g"][1] = s_base + 16 * 8
                st["oc_mm_t"][tq] = p.cnt["t"]
                p.flush()

        s_all = p.cnt["s"]

        def coll2(g, s_all=s_all, b=b):
            g.wait_ge(p.s_s, s_all)
            p.inc(g.collective_compute("ReduceScatter", ALU.add, ins=[p.cc_s[b]],
                                       outs=[p.rs_s[b]],
                                       replica_groups=[list(range(NC))]), "c")
        p.sec("gpsimd", coll2)
        st["c_rs"][b] = p.cnt["c"]

    def wo_final(self):
        p, nc = self, self.nc
        with ExitStack() as es:
            E = es.enter_context
            x_f = [E(nc.sbuf_tensor(f"xr{i}", [128, D], F32)) for i in range(2)]
            st_bf = [E(nc.sbuf_tensor(f"stb{i}", [128, D], BF16)) for i in range(2)]
            gt_bf = [E(nc.sbuf_tensor(f"gtb{i}", [128, D], BF16)) for i in range(2)]
            s_f = E(nc.sbuf_tensor("s_fin", [128, D], F32))
            gg_f = E(nc.sbuf_tensor("gg_f", [128, D], F32))
            y_sb = [E(nc.sbuf_tensor(f"ysb{i}", [128, D], F32)) for i in range(2)]
            v_done = [0, 0]
            g_out_of = [0, 0]
            for tb in range(NT):
                par = tb % 2

                half = 0 if tb < 4 else 1
                rs_src = p.rs_s[half][(tb - 4 * half) * 128:(tb - 4 * half + 1) * 128, :]

                def lfin(sy, tb=tb, par=par, v_need=v_done[par],
                         c_rs=self.wo_st["c_rs"][half], rs_src=rs_src):
                    sy.wait_ge(p.s_c, c_rs)
                    if v_need:
                        sy.wait_ge(p.s_v, v_need)
                    p.inc(sy.dma_start(out=x_f[par][:], in_=p.x[tb * 128:(tb + 1) * 128, :]), "s")
                    p.inc(sy.dma_start(out=st_bf[par][:], in_=rs_src), "s")
                    p.inc(sy.dma_start(out=gt_bf[par][:],
                                       in_=p.cc_g[tb * 128:(tb + 1) * 128, :]), "s")
                p.sec("sync", lfin)

                def vfin(v, tb=tb, par=par, s_n=p.cnt["s"], s_y=g_out_of[par]):
                    v.wait_ge(p.s_s, max(s_n, s_y))
                    v.tensor_copy(s_f[:], st_bf[par][:])
                    v.tensor_copy(gg_f[:], gt_bf[par][:])
                    v.tensor_tensor(y_sb[par][:], s_f[:], gg_f[:], op=ALU.mult)
                    p.inc(v.tensor_tensor(y_sb[par][:], y_sb[par][:], x_f[par][:],
                                          op=ALU.add), "v")
                p.sec("vector", vfin)
                v_yt = p.cnt["v"]
                v_done[par] = v_yt

                def sout(sy, tb=tb, par=par, v_yt=v_yt):
                    sy.wait_ge(p.s_v, v_yt)
                    p.inc(sy.dma_start(out=p.y[tb * 128:(tb + 1) * 128, :], in_=y_sb[par][:]), "s")
                p.sec("sync", sout)
                g_out_of[par] = p.cnt["s"]
                if tb % 2 == 1:
                    p.flush()

            if DEBUG_TAPS:
                tapb = E(nc.sbuf_tensor("tapb", [128, D], BF16))
                for (src, dst, n) in ((p.cc_xn, p.tp_xn, NT),):
                    for i in range(n):
                        def tl(g, src=src, i=i, gn=p.cnt["g"]):
                            g.wait_ge(p.s_g, gn)
                            p.inc(g.dma_start(out=tapb[:], in_=src[i * 128:(i + 1) * 128, :]), "g")
                        p.sec("gpsimd", tl)
                        def ts_(g, dst=dst, i=i, gn=p.cnt["g"]):
                            g.wait_ge(p.s_g, gn)
                            p.inc(g.dma_start(out=dst[i * 128:(i + 1) * 128, :], in_=tapb[:]), "g")
                        p.sec("gpsimd", ts_)

            g_fin = p.cnt["g"]
            s_fin = p.cnt["s"]

            def fin(g, g_fin=g_fin, s_fin=s_fin):
                g.wait_ge(p.s_g, g_fin)
                g.wait_ge(p.s_s, s_fin)
            p.sec("gpsimd", fin)

    def close(self):
        self.es.close()


def build_program():
    p = Prog()
    p.setup()
    with ExitStack() as es:
        p.xqT = es.enter_context(p.nc.sbuf_tensor("xqT", [128, NDC * R], BF16))
        p.phase1()
        p.ffn()
    p.attention()
    p.wo_final()
    p.flush()
    p.close()
    return p.nc


_masks_cache = None


def _masks():
    global _masks_cache
    if _masks_cache is None:
        m = np.zeros((4, 128, 1024), np.float32)
        pp = np.arange(128)[:, None]
        ff = np.arange(512)[None, :]
        for v in range(4):
            mv = (pp + v * 128 <= ff).astype(np.float32)
            m[v, :, 0:512] = mv
            m[v, :, 512:1024] = mv
        _masks_cache = m
    return _masks_cache


def kernel(L0_attn_x, L0_attn_Wq, L0_attn_Wk, L0_attn_Wv, L0_attn_Wo,
           L0_attn_lambda_init, L0_ffn_W_g, L0_ffn_W_u, L0_ffn_W_d):
    x = np.ascontiguousarray(np.asarray(L0_attn_x, np.float32)).reshape(B * T, D)
    Wq = np.asarray(L0_attn_Wq, np.float32)
    Wk = np.asarray(L0_attn_Wk, np.float32)
    Wv = np.asarray(L0_attn_Wv, np.float32)
    Wo = np.asarray(L0_attn_Wo, np.float32)
    lam = np.asarray(L0_attn_lambda_init, np.float32)
    Wg = np.asarray(L0_ffn_W_g, np.float32)
    Wu = np.asarray(L0_ffn_W_u, np.float32)
    Wd = np.asarray(L0_ffn_W_d, np.float32)

    def qkv_img(W, c):
        S = W[:, 256 * c:256 * (c + 1)]
        return np.ascontiguousarray(
            S.reshape(16, 128, 2, 128).transpose(1, 2, 0, 3).reshape(128, 4096))

    wg_img = np.ascontiguousarray(
        Wg.reshape(64, 128, 16, 128).transpose(0, 3, 2, 1).reshape(64, 128, 2048))
    wu_img = np.ascontiguousarray(
        Wu.reshape(64, 128, 16, 128).transpose(0, 3, 2, 1).reshape(64, 128, 2048))
    wd_img = np.ascontiguousarray(
        Wd.reshape(4, 512, 64, 128).transpose(0, 2, 3, 1))
    msk = _masks()

    in_maps = []
    for c in range(NC):
        wo_c = np.ascontiguousarray(
            Wo[256 * c:256 * (c + 1)].reshape(2, 128, 4, 512).transpose(0, 2, 1, 3))
        in_maps.append({
            "x": np.ascontiguousarray(np.concatenate(
                [x[c * 512:(c + 1) * 512], x[T + c * 512:T + (c + 1) * 512]])),
            "wq": qkv_img(Wq, c), "wk": qkv_img(Wk, c), "wv": qkv_img(Wv, c),
            "wo": wo_c, "wg": wg_img, "wu": wu_img, "wd": wd_img,
            "lam": np.ascontiguousarray(lam[2 * c:2 * c + 2].reshape(1, 2)),
            "msk": msk,
        })

    nc = build_program()
    res = run_bass_kernel_spmd(nc, in_maps, core_ids=list(range(NC)))
    out = np.empty((B * T, D), np.float32)
    for c in range(NC):
        yc = np.asarray(res.results[c]["y"], np.float32)
        out[c * 512:(c + 1) * 512] = yc[0:512]
        out[T + c * 512:T + (c + 1) * 512] = yc[512:1024]
    return out.reshape(B, T, D)


if __name__ == "__main__":
    import reference as ref
    ins = ref.setup_inputs(0)
    got = kernel(**{k: np.asarray(v) for k, v in ins.items()})
    exp = np.asarray(ref.reference(**ins))
    err = np.abs(got - exp)
    rv = float(np.mean((got - exp) ** 2) / np.mean(np.asarray(exp) ** 2))
    print("maxabs", err.max(), "resvar", rv)


# version 30
# speedup vs baseline: 1.0583x; 1.0054x over previous
import numpy as np
import concourse.bass as bass

DEBUG_TAPS = False
FLUSH_N = {"tensor": 1, "scalar": 1, "vector": 1, "sync": 1, "gpsimd": 1}

import concourse.mybir as mybir
from concourse.bass_utils import run_bass_kernel_spmd
from contextlib import ExitStack

F32 = mybir.dt.float32
BF16 = mybir.dt.bfloat16
FP16 = mybir.dt.float16
AF = mybir.ActivationFunctionType
ALU = mybir.AluOpType
AX = mybir.AxisListType

B, T, D = 2, 4096, 2048
H = 16
HF = 8192
NC = 8
R = (B * T) // NC
NT = R // 128
NDC = D // 128
SCALE = 0.125


class Prog:
    def __init__(self):
        nc = bass.Bass("TRN2", target_bir_lowering=False)
        self.nc = nc
        self.es = ExitStack()
        ap = lambda name, shape, dt, kind: nc.dram_tensor(name, shape, dt, kind=kind).ap()
        self.x = ap("x", [R, D], F32, "ExternalInput")
        self.wq = ap("wq", [128, 4096], F32, "ExternalInput")
        self.wk = ap("wk", [128, 4096], F32, "ExternalInput")
        self.wv = ap("wv", [128, 4096], F32, "ExternalInput")
        self.wo = ap("wo", [2, 4, 128, 512], F32, "ExternalInput")
        self.wg = ap("wg", [64, 128, 2048], F32, "ExternalInput")
        self.wu = ap("wu", [64, 128, 2048], F32, "ExternalInput")
        self.wd = ap("wd", [4, 64, 128, 512], F32, "ExternalInput")
        self.lam = ap("lam", [1, 2], F32, "ExternalInput")
        self.msk = ap("msk", [4, 128, 1024], F32, "ExternalInput")
        self.y = ap("y", [R, D], F32, "ExternalOutput")
        KI = "ExternalOutput" if DEBUG_TAPS else "Internal"
        aps = lambda name, shape, dt: nc.dram_tensor(name, shape, dt, kind="Internal",
                                                     addr_space="Shared").ap()
        self.cc_xn = ap("cc_xn", [R, D], BF16, "Internal")
        self.ag_xn = aps("ag_xn", [NC, R, D], BF16)
        self.cc_o = ap("cc_o", [2, 2, 128, T], BF16, KI)
        self.cc_s = [ap(f"cc_s{i}", [T, D], BF16, "Internal") for i in range(2)]
        self.cc_g = ap("cc_g", [R, D], BF16, KI)
        self.rs_s = [ap(f"rs_s{i}", [512, D], BF16, "Internal") for i in range(2)]
        self.wq_d = ap("wq_d", [128, 4096], BF16, "Internal")
        self.wk_d = ap("wk_d", [128, 4096], BF16, "Internal")
        self.wv_d = ap("wv_d", [128, 4096], BF16, "Internal")
        self.msk_d = ap("msk_d", [128, 4096], BF16, "Internal")
        self.wob_d = ap("wob_d", [128, 4096], BF16, "Internal")
        if DEBUG_TAPS:
            self.tp_xn = ap("tp_xn", [R, D], BF16, "ExternalOutput")
            self.tp_s = ap("tp_s", [NC * R, D], BF16, "ExternalOutput")
            self.tp_rs = ap("tp_rs", [R, D], BF16, "ExternalOutput")

        E = self.es.enter_context
        self.s_t = E(nc.semaphore("s_t"))
        self.s_a = E(nc.semaphore("s_a"))
        self.s_v = E(nc.semaphore("s_v"))
        self.s_s = E(nc.semaphore("s_s"))
        self.s_g = E(nc.semaphore("s_g"))
        self.s_c = E(nc.semaphore("s_c"))
        self.s_w = E(nc.semaphore("s_w"))
        self.cnt = {"t": 0, "a": 0, "v": 0, "s": 0, "g": 0, "c": 0, "w": 0}
        self.sem = {"t": self.s_t, "a": self.s_a, "v": self.s_v,
                    "s": self.s_s, "g": self.s_g, "c": self.s_c, "w": self.s_w}

        sb = lambda n, s, d: E(nc.sbuf_tensor(n, s, d))
        self.xqT = None
        self.ones_col = sb("ones_col", [128, 1], BF16)
        self.ones_mat = sb("ones_mat", [128, 128], BF16)
        self.ones_row = sb("ones_row", [1, 128], BF16)
        self.onesr_f = sb("onesr_f", [1, 128], F32)
        self.b1536 = sb("b1536", [128, 1], F32)
        self.lamneg = sb("lamneg", [128, 2], F32)
        self.lam_sig = sb("lam_sig", [1, 2], F32)
        self.lam_sb = sb("lam_sb", [1, 2], F32)

        self.block = E(nc.Block())
    def sec(self, eng, fn):
        getattr(self.block, eng)(fn)

    def flush(self):
        pass

    def inc(self, instr, key, by=None):
        by = by if by is not None else (16 if key in ("s", "g", "w") else 1)
        instr.then_inc(self.sem[key], by)
        self.cnt[key] += by
        return self.cnt[key]

    def barrier(self):
        self.flush()
        snap = dict(self.cnt)

        def mk():
            def f(e):
                for k in ("t", "a", "v", "s", "g", "w"):
                    if snap[k]:
                        e.wait_ge(self.sem[k], snap[k])
            return f
        for eng in ("tensor", "scalar", "vector", "sync", "gpsimd"):
            self.sec(eng, mk())

    def setup(self):
        p = self

        def vec(v):
            v.memset(p.ones_col[:], 1.0)
            v.memset(p.ones_mat[:], 1.0)
            v.memset(p.ones_row[:], 1.0)
            v.memset(p.onesr_f[:], 1.0)
            p.inc(v.memset(p.b1536[:], 1536.0), "v")
        p.sec("vector", vec)

        def gp(g):
            p.inc(g.dma_start(out=p.lam_sb[:], in_=p.lam), "g")
        p.sec("gpsimd", gp)
        self.g_lam = p.cnt["g"]

    def phase1(self):
        p, nc = self, self.nc
        with ExitStack() as es:
            E = es.enter_context
            x_f = [E(nc.sbuf_tensor(f"x_f{i}", [128, D], F32)) for i in range(2)]
            xc_f = [E(nc.sbuf_tensor(f"xc_f{i}", [128, D], F32)) for i in range(2)]
            xn_f = [E(nc.sbuf_tensor(f"xn_f{i}", [128, D], F32)) for i in range(2)]
            xnn_f = [E(nc.sbuf_tensor(f"xnn_f{i}", [128, D], F32)) for i in range(2)]
            xn_bf = [E(nc.sbuf_tensor(f"xn_bf{i}", [128, D], BF16)) for i in range(2)]
            xq_bf = [E(nc.sbuf_tensor(f"xq_bf{i}", [128, D], BF16)) for i in range(2)]
            f16q = [E(nc.sbuf_tensor(f"f16q{i}", [128, D], FP16)) for i in range(2)]
            xq_f = [E(nc.sbuf_tensor(f"xq_f{i}", [128, D], F32)) for i in range(2)]
            dump = E(nc.sbuf_tensor("dump", [128, D], F32))
            c2 = lambda n: [E(nc.sbuf_tensor(n + str(i), [128, 1], F32)) for i in range(2)]
            s_sum, s_sq, mneg, e2eps, msqn, vare, sqv, rs = (
                c2("s_sum"), c2("s_sq"), c2("mneg"), c2("e2eps"), c2("msqn"),
                c2("vare"), c2("sqv"), c2("rs"))
            s_sum2, s_sq2, m2neg, e2eps2, msqn2, vare2, sqv2, rs2 = (
                c2("s_sum2"), c2("s_sq2"), c2("m2neg"), c2("e2eps2"), c2("msqn2"),
                c2("vare2"), c2("sqv2"), c2("rs2"))
            amax, rtmp, sq, invsq, vgap = c2("amax"), c2("rtmp"), c2("sq"), c2("invsq"), c2("vgap")

            a_last = [0, 0]
            v1_of = [0] * (NT + 2)
            s_tr_of = [0] * NT
            g_xn_of = [0, 0]

            for i0 in range(0, NT, 2):
                pair = (i0, i0 + 1)

                def lx(sy, pair=pair, an=(a_last[0], a_last[1])):
                    for w, i in enumerate(pair):
                        if an[i % 2]:
                            sy.wait_ge(p.s_a, an[i % 2])
                        p.inc(sy.dma_start(out=x_f[i % 2][:], in_=p.x[i * 128:(i + 1) * 128, :]), "s")
                p.sec("sync", lx)
                s_x = p.cnt["s"]

                a0 = p.cnt["a"]
                v0 = p.cnt["v"]

                def asec(s, pair=pair, s_x=s_x, a0=a0, v0=v0,
                         vn=(v1_of[i0 - 2] if i0 >= 2 else 0,
                             v1_of[i0 - 1] if i0 >= 1 else 0)):
                    for w, i in enumerate(pair):
                        par = i % 2
                        s.wait_ge(p.s_s, s_x - 16 * (1 - w))
                        if vn[par]:
                            s.wait_ge(p.s_v, vn[par])
                        s.activation(dump[:], x_f[par][:], AF.Identity, accum_out=s_sum[par][:, 0:1])
                        p.inc(s.activation(dump[:], x_f[par][:], AF.Square,
                                           accum_out=s_sq[par][:, 0:1]), "a")
                    for w in range(2):
                        par = w
                        s.wait_ge(p.s_v, v0 + 1 + w)
                        s.activation(xc_f[par][:], x_f[par][:], AF.Identity, bias=mneg[par][:, 0:1])
                        p.inc(s.activation(msqn[par][:], mneg[par][:], AF.Square), "a")
                    for w in range(2):
                        par = w
                        s.wait_ge(p.s_v, v0 + 3 + w)
                        p.inc(s.activation(sqv[par][:], vare[par][:], AF.Sqrt), "a")
                    for w in range(2):
                        par = w
                        s.wait_ge(p.s_v, v0 + 5 + w)
                        s.activation(xn_f[par][:], xc_f[par][:], AF.Identity, scale=rs[par][:, 0:1],
                                     accum_out=s_sum2[par][:, 0:1])
                        p.inc(s.activation(dump[:], xn_f[par][:], AF.Square,
                                           accum_out=s_sq2[par][:, 0:1]), "a")
                    for w in range(2):
                        par = w
                        s.wait_ge(p.s_v, v0 + 7 + w)
                        s.activation(xc_f[par][:], xn_f[par][:], AF.Identity, bias=m2neg[par][:, 0:1])
                        p.inc(s.activation(msqn2[par][:], m2neg[par][:], AF.Square), "a")
                    for w in range(2):
                        par = w
                        s.wait_ge(p.s_v, v0 + 9 + w)
                        p.inc(s.activation(sqv2[par][:], vare2[par][:], AF.Sqrt), "a")
                    for w in range(2):
                        par = w
                        s.wait_ge(p.s_v, v0 + 11 + w)
                        p.inc(s.activation(xnn_f[par][:], xc_f[par][:], AF.Identity,
                                           scale=rs2[par][:, 0:1]), "a")
                    for w in range(2):
                        par = w
                        s.wait_ge(p.s_v, v0 + 13 + w)
                        p.inc(s.activation(f16q[par][:], xnn_f[par][:], AF.Identity,
                                           bias=p.b1536[:, 0:1], scale=sq[par][:, 0:1]), "a")
                p.sec("scalar", asec)
                assert p.cnt["a"] == a0 + 16

                def vsec(v, a0=a0, v0=v0, gn=(g_xn_of[0], g_xn_of[1]),
                         sn=(s_tr_of[i0 - 2] if i0 >= 2 else 0,
                             s_tr_of[i0 - 1] if i0 >= 1 else 0)):
                    for w in range(2):
                        par = w
                        v.wait_ge(p.s_a, a0 + 1 + w)
                        v.tensor_scalar(mneg[par][:], s_sum[par][:], -1.0 / D, None, op0=ALU.mult)
                        p.inc(v.tensor_scalar(e2eps[par][:], s_sq[par][:], 1.0 / D, 1e-5,
                                              op0=ALU.mult, op1=ALU.add), "v")
                    for w in range(2):
                        par = w
                        v.wait_ge(p.s_a, a0 + 3 + w)
                        p.inc(v.tensor_tensor(vare[par][:], e2eps[par][:], msqn[par][:],
                                              op=ALU.subtract), "v")
                    for w in range(2):
                        par = w
                        v.wait_ge(p.s_a, a0 + 5 + w)
                        p.inc(v.reciprocal(rs[par][:], sqv[par][:]), "v")
                    for w in range(2):
                        par = w
                        v.wait_ge(p.s_a, a0 + 7 + w)
                        if gn[par]:
                            v.wait_ge(p.s_g, gn[par])
                        v.tensor_scalar(m2neg[par][:], s_sum2[par][:], -1.0 / D, None, op0=ALU.mult)
                        v.tensor_scalar(e2eps2[par][:], s_sq2[par][:], 1.0 / D, 1e-8,
                                        op0=ALU.mult, op1=ALU.add)
                        p.inc(v.tensor_copy(xn_bf[par][:], xn_f[par][:]), "v")
                    for w in range(2):
                        par = w
                        v.wait_ge(p.s_a, a0 + 9 + w)
                        p.inc(v.tensor_tensor(vare2[par][:], e2eps2[par][:], msqn2[par][:],
                                              op=ALU.subtract), "v")
                    for w in range(2):
                        par = w
                        v.wait_ge(p.s_a, a0 + 11 + w)
                        p.inc(v.reciprocal(rs2[par][:], sqv2[par][:]), "v")
                    for w in range(2):
                        par = w
                        v.wait_ge(p.s_a, a0 + 13 + w)
                        v.reduce_max(amax[par][:, 0:1], xnn_f[par][:], axis=AX.X,
                                     apply_absolute_value=True)
                        v.tensor_scalar(vgap[par][:], amax[par][:], 1.0, None, op0=ALU.mult)
                        v.reciprocal(rtmp[par][:], amax[par][:])
                        v.tensor_scalar(invsq[par][:], amax[par][:], 1.0 / 127.0, None, op0=ALU.mult)
                        p.inc(v.tensor_scalar(sq[par][:], rtmp[par][:], 127.0, None, op0=ALU.mult), "v")
                    for w in range(2):
                        par = w
                        v.wait_ge(p.s_a, a0 + 15 + w)
                        if sn[par]:
                            v.wait_ge(p.s_s, sn[par])
                        v.tensor_scalar(xq_f[par][:], f16q[par][:], -1536.0, invsq[par][:, 0:1],
                                        op0=ALU.add, op1=ALU.mult)
                        p.inc(v.tensor_copy(xq_bf[par][:], xq_f[par][:]), "v")
                p.sec("vector", vsec)
                assert p.cnt["v"] == v0 + 16
                v1_of[i0] = v0 + 1
                v1_of[i0 + 1] = v0 + 2
                a_last[i0 % 2] = a0 + 15
                a_last[(i0 + 1) % 2] = a0 + 16

                def gs(g, pair=pair, v0=v0):
                    for w, i in enumerate(pair):
                        par = i % 2
                        g.wait_ge(p.s_v, v0 + 7 + w)
                        p.inc(g.dma_start(out=p.cc_xn[i * 128:(i + 1) * 128, :],
                                          in_=xn_bf[par][:]), "g")
                p.sec("gpsimd", gs)
                g_xn_of[i0 % 2] = p.cnt["g"] - 16
                g_xn_of[(i0 + 1) % 2] = p.cnt["g"]

                def tr(sy, pair=pair, v0=v0):
                    for w, i in enumerate(pair):
                        par = i % 2
                        sy.wait_ge(p.s_v, v0 + 15 + w)
                        for d in range(NDC):
                            p.inc(sy.dma_start(
                                out=p.xqT[:, d * R + i * 128: d * R + (i + 1) * 128],
                                in_=xq_bf[par][:, d * 128:(d + 1) * 128], transpose=True), "s")
                p.sec("sync", tr)
                s_tr_of[i0] = p.cnt["s"] - 16 * NDC
                s_tr_of[i0 + 1] = p.cnt["s"]
                p.flush()


            self.s_xqT = p.cnt["s"]
            g_all = p.cnt["g"]

            def coll(g, g_all=g_all):
                g.wait_ge(p.s_g, g_all)
                p.inc(g.collective_compute("AllGather", ALU.bypass, ins=[p.cc_xn],
                                           outs=[p.ag_xn],
                                           replica_groups=[list(range(NC))]), "c")
            p.sec("gpsimd", coll)
            self.c_xn = p.cnt["c"]
            self.barrier()

    def ffn(self):
        p, nc = self, self.nc
        with nc.psum_tensor("ps_lam", [128, 2], F32) as ps_lam:
            def alam(s, gl=self.g_lam):
                s.wait_ge(p.s_g, gl)
                p.inc(s.activation(p.lam_sig[:], p.lam_sb[:], AF.Sigmoid), "a")
            p.sec("scalar", alam)

            def tlam(t, a_lam=p.cnt["a"]):
                t.wait_ge(p.s_a, a_lam)
                p.inc(t.matmul(ps_lam[:], p.onesr_f[:], p.lam_sig[:], start=True, stop=True), "t")
            p.sec("tensor", tlam)

            def vlam(v, t_lam=p.cnt["t"]):
                v.wait_ge(p.s_t, t_lam)
                p.inc(v.tensor_scalar(p.lamneg[:], ps_lam[:], -1.0, None, op0=ALU.mult), "v")
            p.sec("vector", vlam)

        def wstage(g):
            p.inc(g.dma_start(out=p.wq_d, in_=p.wq), "g")
            p.inc(g.dma_start(out=p.wk_d, in_=p.wk), "g")
            p.inc(g.dma_start(out=p.wv_d, in_=p.wv), "g")
            p.inc(g.dma_start(out=p.msk_d.rearrange("b (a c) -> b a c", c=1024),
                              in_=p.msk.rearrange("a b c -> b a c")), "g")
            p.inc(g.dma_start(out=p.wob_d.rearrange("p (a b f) -> p a b f", b=4, f=512),
                              in_=p.wo.rearrange("a b p f -> p a b f")), "g")
        p.sec("gpsimd", wstage)
        self.g_stage = p.cnt["g"]

        with ExitStack() as es:
            E = es.enter_context
            slab = [E(nc.sbuf_tensor(f"wslab{i}", [128, 2048], F32)) for i in range(2)]
            sgW = [E(nc.sbuf_tensor(f"sgW{i}", [128, 2048], BF16)) for i in range(4)]
            sgm = E(nc.sbuf_tensor("sgm", [128, 1024], F32))
            sil = E(nc.sbuf_tensor("sil", [128, 1024], F32))
            pT = E(nc.sbuf_tensor("pT", [128, 16 * R], BF16))
            f_sb = E(nc.sbuf_tensor("f_sb", [128, NT * D], F32))
            wdf = [E(nc.sbuf_tensor(f"wdf{i}", [128, 2048], F32)) for i in range(2)]
            wdb = [E(nc.sbuf_tensor(f"wdb{i}", [128, 2048], BF16)) for i in range(2)]
            g_t = [E(nc.sbuf_tensor(f"g_t{i}", [128, D], BF16)) for i in range(2)]

            sign_a = 0
            sg_mm_t = [0, 0]
            a_w_of = {}
            evac_v = 0

            def emit_lw(hb):
                def lw(sy, hb=hb, fa=sign_a):
                    if fa:
                        sy.wait_ge(p.s_a, fa)
                    p.inc(sy.dma_start(out=slab[0][:], in_=p.wg[hb]), "s")
                    p.inc(sy.dma_start(out=slab[1][:], in_=p.wu[hb]), "s")
                p.sec("sync", lw)
                return p.cnt["s"]

            for q in range(4):
                with nc.psum_tensor(f"ps_yga{q}", [128, 1024], F32) as ps_yga, \
                     nc.psum_tensor(f"ps_yua{q}", [128, 1024], F32) as ps_yua, \
                     nc.psum_tensor(f"ps_ygb{q}", [128, 1024], F32) as ps_ygb, \
                     nc.psum_tensor(f"ps_yub{q}", [128, 1024], F32) as ps_yub:
                    evac_pair = [0, 0]
                    if q == 0:
                        s_w0 = emit_lw(0)

                        def sw0(sct, s_w0=s_w0):
                            sct.wait_ge(p.s_s, s_w0)
                            sct.activation(sgW[0][:], slab[0][:], AF.Sign)
                            p.inc(sct.activation(sgW[1][:], slab[1][:], AF.Sign), "a")
                        p.sec("scalar", sw0)
                        a_w_of[0] = p.cnt["a"]
                        sign_a = p.cnt["a"]
                    for jb in range(16):
                        hb = q * 16 + jb
                        par = hb % 2
                        ps_yg = ps_yga if par == 0 else ps_ygb
                        ps_yu = ps_yua if par == 0 else ps_yub

                        s_wn = emit_lw(hb + 1) if hb + 1 < 64 else 0

                        def mm(t, par=par, a_w=a_w_of[hb], ev=evac_pair[par],
                               s_xqT=self.s_xqT, first=(q == 0 and jb == 0),
                               ev0=(evac_v if jb <= 1 else 0)):
                            if first:
                                t.wait_ge(p.s_s, s_xqT)
                            t.wait_ge(p.s_a, a_w)
                            if ev:
                                t.wait_ge(p.s_v, ev)
                            elif ev0:
                                t.wait_ge(p.s_v, ev0)
                            mmi = None
                            for which in range(2):
                                dst = ps_yg if which == 0 else ps_yu
                                src = sgW[2 * par + which]
                                for th in range(2):
                                    for j in range(NDC):
                                        mmi = t.matmul(dst[:, th * 512:(th + 1) * 512],
                                                       src[:, j * 128:(j + 1) * 128],
                                                       p.xqT[:, j * R + th * 512: j * R + (th + 1) * 512],
                                                       start=(j == 0), stop=(j == NDC - 1),
                                                       skip_group_check=True)
                            p.inc(mmi, "t")
                        p.sec("tensor", mm)
                        t_mm = p.cnt["t"]
                        sg_mm_t[par] = t_mm

                        def ev_a(sct, hb=hb, s_wn=s_wn, t_mm=t_mm, v_prev=evac_v,
                                 npar=(hb + 1) % 2, ftn=sg_mm_t[(hb + 1) % 2]):
                            if s_wn:
                                sct.wait_ge(p.s_s, s_wn)
                                if ftn:
                                    sct.wait_ge(p.s_t, ftn)
                                sct.activation(sgW[2 * npar][:], slab[0][:], AF.Sign)
                                p.inc(sct.activation(sgW[2 * npar + 1][:], slab[1][:],
                                                     AF.Sign), "a")
                            sct.wait_ge(p.s_t, t_mm)
                            if v_prev:
                                sct.wait_ge(p.s_v, v_prev)
                            p.inc(sct.activation(sgm[:], ps_yg[:], AF.Sigmoid), "a")
                        p.sec("scalar", ev_a)
                        if hb + 1 < 64:
                            a_w_of[hb + 1] = p.cnt["a"] - 1
                            sign_a = p.cnt["a"] - 1

                        def ev_v(v, jb=jb, a_sg=p.cnt["a"], t_mm=t_mm):
                            v.wait_ge(p.s_a, a_sg)
                            v.tensor_tensor(sil[:], ps_yg[:], sgm[:], op=ALU.mult)
                            p.inc(v.tensor_tensor(pT[:, jb * R:(jb + 1) * R], sil[:], ps_yu[:],
                                                  op=ALU.mult), "v")
                        p.sec("vector", ev_v)
                        evac_v = p.cnt["v"]

                v_pT = p.cnt["v"]
                if q == 0:
                    wd_sign_a = [0, 0]
                    wd_mm_t = [0, 0]
                with nc.psum_tensor(f"ps_f{q}", [128, 8 * 512], F32) as ps_f:
                    bank_ev = [0] * 8
                    for dt in range(4):
                        for jh in range(2):
                            g_wd_of = [0, 0]
                            for par in range(2):
                                jq = 2 * jh + par

                                def lwd(g, dt=dt, hb0=q * 16 + jq * 4, par=par,
                                        fa=wd_sign_a[par]):
                                    if fa:
                                        g.wait_ge(p.s_a, fa)
                                    p.inc(g.dma_start(
                                        out=wdf[par][:].rearrange("p (c f) -> p c f", f=512),
                                        in_=p.wd[dt, hb0:hb0 + 4].rearrange("c p f -> p c f")), "g")
                                p.sec("gpsimd", lwd)
                                g_wd_of[par] = p.cnt["g"]

                            def swd2(sct, g_wd_of=list(g_wd_of),
                                     fts=(wd_mm_t[0], wd_mm_t[1])):
                                for par in range(2):
                                    sct.wait_ge(p.s_g, g_wd_of[par])
                                    if fts[par]:
                                        sct.wait_ge(p.s_t, fts[par])
                                    p.inc(sct.activation(wdb[par][:], wdf[par][:],
                                                         AF.Sign), "a")
                            p.sec("scalar", swd2)
                            wd_sign_a[0] = p.cnt["a"] - 1
                            wd_sign_a[1] = p.cnt["a"]

                            for par in range(2):
                                jq = 2 * jh + par
                                a_wd = p.cnt["a"] - 1 + par

                                def mm3(t, jq=jq, par=par, a_wd=a_wd, v_pT=v_pT,
                                        ev=(list(bank_ev) if jq == 0 else None)):
                                    t.wait_ge(p.s_a, a_wd)
                                    t.wait_ge(p.s_v, v_pT)
                                    mmi = None
                                    for tb in range(8):
                                        if ev and ev[tb]:
                                            t.wait_ge(p.s_v, ev[tb])
                                        for jc in range(4):
                                            j = jq * 4 + jc
                                            mmi = t.matmul(ps_f[:, tb * 512:(tb + 1) * 512],
                                                           pT[:, j * R + tb * 128: j * R + tb * 128 + 128],
                                                           wdb[par][:, jc * 512:(jc + 1) * 512],
                                                           start=(j == 0), stop=(j == 15),
                                                           skip_group_check=True)
                                    p.inc(mmi, "t")
                                p.sec("tensor", mm3)
                                wd_mm_t[par] = p.cnt["t"]
                        t_dt = p.cnt["t"]

                        def ev3(v, q=q, dt=dt, t_dt=t_dt):
                            v.wait_ge(p.s_t, t_dt)
                            ins = None
                            for tb in range(8):
                                src = ps_f[:, tb * 512:(tb + 1) * 512]
                                dst = f_sb[:, tb * D + dt * 512: tb * D + (dt + 1) * 512]
                                if q == 0:
                                    ins = v.tensor_copy(dst, src)
                                else:
                                    ins = v.tensor_tensor(dst, dst, src, op=ALU.add)
                            p.inc(ins, "v")
                        p.sec("vector", ev3)
                        for tb in range(8):
                            bank_ev[tb] = p.cnt["v"]
                evac_v = p.cnt["v"]

            v_f = p.cnt["v"]
            a_g0 = p.cnt["a"]
            g0 = p.cnt["g"]

            def gsig(s, v_f=v_f, g0=g0):
                s.wait_ge(p.s_v, v_f)
                for tb in range(NT):
                    par = tb % 2
                    if tb >= 2:
                        s.wait_ge(p.s_g, g0 + 16 * (tb - 1))
                    p.inc(s.activation(g_t[par][:], f_sb[:, tb * D:(tb + 1) * D],
                                       AF.Silu), "a")
            p.sec("scalar", gsig)
            assert p.cnt["a"] == a_g0 + NT

            for tb in range(NT):
                par = tb % 2

                def gst(g, tb=tb, par=par, a_g=a_g0 + tb + 1):
                    g.wait_ge(p.s_a, a_g)
                    p.inc(g.dma_start(out=p.cc_g[tb * 128:(tb + 1) * 128, :], in_=g_t[par][:]), "g")
                p.sec("gpsimd", gst)
            self.barrier()

    def attention(self):
        p, nc = self, self.nc
        with ExitStack() as es:
            E = es.enter_context
            wqb = E(nc.sbuf_tensor("wqb", [128, 4096], BF16))
            wkb = E(nc.sbuf_tensor("wkb", [128, 4096], BF16))
            wvb = E(nc.sbuf_tensor("wvb", [128, 4096], BF16))
            mask_bf = E(nc.sbuf_tensor("mask_bf", [128, 4096], BF16))
            qT = E(nc.sbuf_tensor("qT", [128, 2 * T], BF16))
            kT = E(nc.sbuf_tensor("kT", [128, 2 * T], BF16))
            vT = E(nc.sbuf_tensor("vT", [128, 2 * T], BF16))
            vtok = E(nc.sbuf_tensor("vtok", [128, 2 * T], BF16))
            xnT = [E(nc.sbuf_tensor(f"xnT{i}", [128, NDC * 512], BF16)) for i in range(2)]
            strip = E(nc.sbuf_tensor("strip", [128, 4 * 1024], BF16))
            mtmp = [E(nc.sbuf_tensor(f"mtmp{i}", [128, 1024], BF16)) for i in range(2)]
            rl = E(nc.sbuf_tensor("rl", [1, 1024], F32))
            rl_bf = E(nc.sbuf_tensor("rl_bf", [1, 1024], BF16))
            Bsb = E(nc.sbuf_tensor("Bsb", [128, 1024], F32))
            Osb = E(nc.sbuf_tensor("Osb", [128, 1024], F32))
            cmb = E(nc.sbuf_tensor("cmb", [128, 1024], F32))
            ocm = E(nc.sbuf_tensor("ocm", [128, T], BF16))
            self.wob = E(nc.sbuf_tensor("wob", [128, 2 * 4 * 512], BF16))
            self.oc = [E(nc.sbuf_tensor(f"oc{i}", [128, 2 * 1024], BF16)) for i in range(4)]
            self.s_sb = [E(nc.sbuf_tensor(f"ssb{i}", [128, D], BF16)) for i in range(2)]
            self.wo_st = {"oc_mm_t": [0, 0, 0, 0], "sb_dma_g": [0, 0],
                          "bank_ev": [0, 0, 0, 0], "c_rs": [0, 0]}
            g_ocm_of = [0, 0]

            def gw(sy, g_stage=self.g_stage):
                sy.wait_ge(p.s_g, g_stage)
                p.inc(sy.dma_start(out=wqb[:], in_=p.wq_d), "s")
                p.inc(sy.dma_start(out=wkb[:], in_=p.wk_d), "s")
                p.inc(sy.dma_start(out=wvb[:], in_=p.wv_d), "s")
                p.inc(sy.dma_start(out=mask_bf[:], in_=p.msk_d), "s")
                p.inc(sy.dma_start(out=self.wob[:], in_=p.wob_d), "s")
            p.sec("sync", gw)
            g_w = p.cnt["s"]
            self.g_wo = p.cnt["s"]

            agxn = p.ag_xn.rearrange("a b c -> (a b) c")
            xnT_mm_t = [0, 0]
            g_ocm = 0
            for b in range(2):
                with nc.psum_tensor(f"ps_qkv{b}", [128, 4 * 512], F32) as ps4:
                    slot_ev = [0, 0, 0, 0]
                    for ts in range(8):
                        par = ts % 2

                        def ltr(sy, b=b, ts=ts, par=par, t_need=xnT_mm_t[par],
                                c_xn=self.c_xn):
                            sy.wait_ge(p.s_c, c_xn)
                            if t_need:
                                sy.wait_ge(p.s_t, t_need)
                            base = ts * 1024 + b * 512
                            for d in range(NDC):
                                p.inc(sy.dma_start(
                                    out=xnT[par][:, d * 512:(d + 1) * 512],
                                    in_=agxn[base:base + 512, d * 128:(d + 1) * 128],
                                    transpose=True), "s")
                        p.sec("sync", ltr)
                        s_tr = p.cnt["s"]

                        t_base = p.cnt["t"]
                        v_base = p.cnt["v"]

                        def mmqkv(t, par=par, s_tr=s_tr, g_w=g_w, ev=list(slot_ev),
                                  v_base=v_base, first=(ts == 0),
                                  a_pre=p.cnt["a"], v_pre=p.cnt["v"]):
                            t.wait_ge(p.s_s, max(s_tr, g_w))
                            if first:
                                t.wait_ge(p.s_a, a_pre)
                                t.wait_ge(p.s_v, v_pre)
                            k = 0
                            for hb in range(2):
                                for wsl in (wqb, wkb, wvb):
                                    pi = k % 4
                                    if k >= 4:
                                        t.wait_ge(p.s_v, v_base + (k - 4) + 1)
                                    elif ev[pi]:
                                        t.wait_ge(p.s_v, ev[pi])
                                    mmi = None
                                    for j in range(NDC):
                                        mmi = t.matmul(
                                            ps4[:, pi * 512:(pi + 1) * 512],
                                            wsl[:, (hb * 16 + j) * 128:(hb * 16 + j + 1) * 128],
                                            xnT[par][:, j * 512:(j + 1) * 512],
                                            start=(j == 0), stop=(j == NDC - 1),
                                            skip_group_check=True)
                                    p.inc(mmi, "t")
                                    k += 1
                        p.sec("tensor", mmqkv)
                        xnT_mm_t[par] = p.cnt["t"]

                        def evqkv(v, ts=ts, t_base=t_base):
                            k = 0
                            for hb in range(2):
                                for dst in (qT, kT, vT):
                                    pi = k % 4
                                    v.wait_ge(p.s_t, t_base + k + 1)
                                    p.inc(v.tensor_copy(
                                        dst[:, hb * T + ts * 512:(hb * T + (ts + 1) * 512)],
                                        ps4[:, pi * 512:(pi + 1) * 512]), "v")
                                    k += 1
                        p.sec("vector", evqkv)
                        for k in range(4):
                            slot_ev[k] = p.cnt["v"]
                        p.flush()

                def vtr(sy, v_q=p.cnt["v"], t_pre=p.cnt["t"]):
                    sy.wait_ge(p.s_v, v_q)
                    sy.wait_ge(p.s_t, t_pre)
                    for hb in range(2):
                        for sc in range(32):
                            p.inc(sy.dma_start(
                                out=vtok[:, hb * T + sc * 128: hb * T + (sc + 1) * 128],
                                in_=vT[:, hb * T + sc * 128: hb * T + (sc + 1) * 128],
                                transpose=True), "s")
                p.sec("sync", vtr)
                s_vt = p.cnt["s"]
                if b == 1:
                    self.wo_batch(0, g_ocm_of[0], p.cnt["v"])

                for hb in range(2):
                    with nc.psum_tensor(f"ps_s{b}{hb}", [128, 2 * 1024], F32) as ps_s, \
                         nc.psum_tensor(f"ps_O1{b}{hb}", [128, 512], F32) as ps_O1, \
                         nc.psum_tensor(f"ps_O2{b}{hb}", [128, 512], F32) as ps_O2, \
                         nc.psum_tensor(f"ps_L1{b}{hb}", [128, 512], F32) as ps_L1, \
                         nc.psum_tensor(f"ps_L2{b}{hb}", [128, 512], F32) as ps_L2:
                        exp_a_par = [0, 0]
                        bs_v_par = [0, 0]
                        o_free_v = 0
                        slot_t = [0, 0, 0, 0]
                        mtmp_v = [0, 0]
                        pending = None
                        for tt in range(8):
                            ns = 4 * (tt + 1)
                            npair = ns // 2
                            t0 = p.cnt["t"]
                            has_p = pending is not None

                            if has_p:
                                t_end_p, tt_p, g_need_p = pending

                                def pvec(v, t_end_p=t_end_p, g_need=g_need_p,
                                         hb=hb, tt_p=tt_p):
                                    v.wait_ge(p.s_t, t_end_p)
                                    if g_need:
                                        v.wait_ge(p.s_g, g_need)
                                    v.reciprocal(Bsb[:, 0:512], ps_L1[:, :])
                                    p.inc(v.reciprocal(Bsb[:, 512:1024],
                                                       ps_L2[:, :]), "v")
                                    v.tensor_tensor(cmb[:, 0:512], Osb[:, 0:512],
                                                    Bsb[:, 0:512], op=ALU.mult)
                                    v.scalar_tensor_tensor(
                                        cmb[:, 512:1024], Osb[:, 512:1024],
                                        p.lamneg[:, hb:hb + 1], Bsb[:, 512:1024],
                                        op0=ALU.mult, op1=ALU.mult)
                                    p.inc(v.tensor_tensor(
                                        ocm[:, tt_p * 512:(tt_p + 1) * 512],
                                        cmb[:, 0:512], cmb[:, 512:1024],
                                        op=ALU.add), "v")
                                p.sec("vector", pvec)
                                bs_v = p.cnt["v"] - 1
                            else:
                                bs_v = 0

                            a0 = p.cnt["a"]
                            v0m = p.cnt["v"]

                            def a_pred(idx, w):
                                return a0 + 2 * idx + w + 1

                            def m_pred(k):
                                return v0m + k + 1

                            slot_snap = {}
                            t_of = {}
                            sim_t = t0
                            fw = (p.cnt["a"], p.cnt["v"]) if tt == 0 else None
                            an0 = (exp_a_par[0], exp_a_par[1])
                            o_need0 = o_free_v
                            mv_prev = tuple(mtmp_v)

                            def tsec(t, tt=tt, hb=hb, npair=npair, t0=t0,
                                     fw=fw, an0=an0, o_need0=o_need0, bs_v=bs_v,
                                     s_vt=s_vt, a0=a0, v0m=v0m):
                                for idx in range(npair):
                                    si0 = 2 * idx
                                    if idx == 0 and fw is not None:
                                        t.wait_ge(p.s_s, s_vt)
                                        t.wait_ge(p.s_a, fw[0])
                                        t.wait_ge(p.s_v, fw[1])
                                    for w, si in enumerate((si0, si0 + 1)):
                                        if idx == 0:
                                            if an0[w]:
                                                t.wait_ge(p.s_a, an0[w])
                                        else:
                                            t.wait_ge(p.s_a, a0 + 2 * (idx - 1) + w + 1)
                                        par = si % 2
                                        t.matmul(ps_s[:, par * 1024: par * 1024 + 512],
                                                 kT[0:64, hb * T + si * 128: hb * T + (si + 1) * 128],
                                                 qT[0:64, hb * T + tt * 512: hb * T + (tt + 1) * 512],
                                                 start=True, stop=True, tile_position=(0, 0))
                                        p.inc(t.matmul(
                                            ps_s[:, par * 1024 + 512: par * 1024 + 1024],
                                            kT[64:128, hb * T + si * 128: hb * T + (si + 1) * 128],
                                            qT[64:128, hb * T + tt * 512: hb * T + (tt + 1) * 512],
                                            start=True, stop=True, tile_position=(64, 0)), "t")
                                    diag = si0 >= 4 * tt
                                    for w, si in enumerate((si0, si0 + 1)):
                                        slot = si % 4
                                        st_ = idx == 0 and w == 0
                                        sp_ = idx == npair - 1 and w == 1
                                        if diag:
                                            k = (si - 4 * tt)
                                            t.wait_ge(p.s_v, v0m + k + 1)
                                        else:
                                            t.wait_ge(p.s_a, a0 + 2 * idx + w + 1)
                                        if st_:
                                            if o_need0:
                                                t.wait_ge(p.s_v, o_need0)
                                            if bs_v:
                                                t.wait_ge(p.s_v, bs_v)
                                        t.matmul(ps_L1[:, :], p.ones_mat[:, :],
                                                 strip[:, slot * 1024: slot * 1024 + 512],
                                                 start=st_, stop=sp_,
                                                 skip_group_check=True)
                                        t.matmul(ps_L2[:, :], p.ones_mat[:, :],
                                                 strip[:, slot * 1024 + 512: (slot + 1) * 1024],
                                                 start=st_, stop=sp_,
                                                 skip_group_check=True)
                                        t.matmul(ps_O1[:],
                                                 vtok[:, hb * T + si * 128: hb * T + (si + 1) * 128],
                                                 strip[:, slot * 1024: slot * 1024 + 512],
                                                 start=st_, stop=sp_,
                                                 skip_group_check=True)
                                        p.inc(t.matmul(ps_O2[:],
                                                       vtok[:, hb * T + si * 128: hb * T + (si + 1) * 128],
                                                       strip[:, slot * 1024 + 512:(slot + 1) * 1024],
                                                       start=st_, stop=sp_,
                                                       skip_group_check=True), "t")
                            for idx in range(npair):
                                si0 = 2 * idx
                                sim_t += 2
                                t_of[si0] = sim_t - 1
                                t_of[si0 + 1] = sim_t
                                slot_snap[si0] = slot_t[si0 % 4]
                                slot_snap[si0 + 1] = slot_t[(si0 + 1) % 4]
                                sim_t += 2
                                slot_t[si0 % 4] = sim_t - 1
                                slot_t[(si0 + 1) % 4] = sim_t
                            p.sec("tensor", tsec)
                            assert p.cnt["t"] == sim_t, (p.cnt["t"], sim_t)
                            t_end = sim_t
                            exp_a_par[0] = a_pred(npair - 1, 0)
                            exp_a_par[1] = a_pred(npair - 1, 1)

                            def asec(sct, tt=tt, npair=npair, t_of=dict(t_of),
                                     slot_snap=dict(slot_snap), mv_prev=mv_prev,
                                     v0m=v0m):
                                for idx in range(npair):
                                    si0 = 2 * idx
                                    for w, si in enumerate((si0, si0 + 1)):
                                        par = si % 2
                                        slot = si % 4
                                        diag = si >= 4 * tt
                                        sct.wait_ge(p.s_t, max(t_of[si], slot_snap[si]))
                                        if diag:
                                            k = si - 4 * tt
                                            if k < 2:
                                                if mv_prev[si % 2]:
                                                    sct.wait_ge(p.s_v, mv_prev[si % 2])
                                            else:
                                                sct.wait_ge(p.s_v, v0m + k - 1)
                                        dst = (mtmp[si % 2][:] if diag
                                               else strip[:, slot * 1024:(slot + 1) * 1024])
                                        p.inc(sct.activation(
                                            dst, ps_s[:, par * 1024:(par + 1) * 1024],
                                            AF.Exp, scale=SCALE), "a")
                            p.sec("scalar", asec)
                            assert p.cnt["a"] == a0 + ns, (p.cnt["a"], a0, ns)

                            def msec(v, tt=tt, a0=a0, ns=ns,
                                     sl=(slot_t[0], slot_t[1], slot_t[2], slot_t[3]),
                                     slot_snap=dict(slot_snap)):
                                for k in range(4):
                                    si = 4 * tt + k
                                    slot = si % 4
                                    v.wait_ge(p.s_a, a0 + ns - 4 + k + 1)
                                    tsl = slot_snap[si]
                                    if tsl:
                                        v.wait_ge(p.s_t, tsl)
                                    p.inc(v.tensor_tensor(
                                        strip[:, slot * 1024:(slot + 1) * 1024],
                                        mtmp[si % 2][:],
                                        mask_bf[:, k * 1024:(k + 1) * 1024],
                                        op=ALU.mult), "v")
                            p.sec("vector", msec)
                            assert p.cnt["v"] == v0m + 4, (p.cnt["v"], v0m)
                            mtmp_v[0] = v0m + 3
                            mtmp_v[1] = v0m + 4

                            def vfl(v, t_end=t_end):
                                v.wait_ge(p.s_t, t_end)
                                v.tensor_copy(Osb[:, 0:512], ps_O1[:])
                                p.inc(v.tensor_copy(Osb[:, 512:1024], ps_O2[:]), "v")
                            p.sec("vector", vfl)
                            o_free_v = p.cnt["v"]
                            pending = (t_end, tt, (g_ocm if tt == 0 else 0))

                        t_end_p, tt_p, g_need_p = pending

                        def fvec(v, t_end_p=t_end_p, g_need=g_need_p, hb=hb,
                                 tt_p=tt_p):
                            v.wait_ge(p.s_t, t_end_p)
                            if g_need:
                                v.wait_ge(p.s_g, g_need)
                            v.reciprocal(Bsb[:, 0:512], ps_L1[:, :])
                            v.reciprocal(Bsb[:, 512:1024], ps_L2[:, :])
                            v.tensor_tensor(cmb[:, 0:512], Osb[:, 0:512],
                                            Bsb[:, 0:512], op=ALU.mult)
                            v.scalar_tensor_tensor(
                                cmb[:, 512:1024], Osb[:, 512:1024],
                                p.lamneg[:, hb:hb + 1], Bsb[:, 512:1024],
                                op0=ALU.mult, op1=ALU.mult)
                            p.inc(v.tensor_tensor(
                                ocm[:, tt_p * 512:(tt_p + 1) * 512],
                                cmb[:, 0:512], cmb[:, 512:1024], op=ALU.add), "v")
                        p.sec("vector", fvec)
                        pending = None


                    def soc(g, b=b, hb=hb, v_oc=p.cnt["v"]):
                        g.wait_ge(p.s_v, v_oc)
                        p.inc(g.dma_start(out=p.cc_o[b, hb], in_=ocm[:]), "g")
                    p.sec("gpsimd", soc)
                    g_ocm = p.cnt["g"]
                g_ocm_of[b] = g_ocm
            self.wo_batch(1, g_ocm_of[1], p.cnt["v"])
            self.barrier()

    def wo_batch(self, b, g_ocm_b, v_pre):
        p, nc = self, self.nc
        st = self.wo_st
        wob, oc, s_sb = self.wob, self.oc, self.s_sb
        with nc.psum_tensor(f"ps_sp{b}", [128, 8 * 512], F32) as ps_sp:
            g_oc_of = [0] * 4
            for tq in range(4):
                def loc(g, b=b, tq=tq, t_need=st["oc_mm_t"][tq],
                        g_ocm_b=(g_ocm_b if tq == 0 else 0)):
                    if g_ocm_b:
                        g.wait_ge(p.s_g, g_ocm_b)
                    if t_need:
                        g.wait_ge(p.s_t, t_need)
                    for hb in range(2):
                        p.inc(g.dma_start(
                            out=oc[tq][:, hb * 1024:(hb + 1) * 1024],
                            in_=p.cc_o[b, hb][:, tq * 1024:(tq + 1) * 1024]), "g")
                p.sec("gpsimd", loc)
                g_oc_of[tq] = p.cnt["g"]

            for tq in range(4):
                t_base = p.cnt["t"]
                a_base = p.cnt["a"]
                s_base = p.cnt["s"]
                ev_prev = (st["bank_ev"][0], st["bank_ev"][1])
                sb_prev = (st["sb_dma_g"][0], st["sb_dma_g"][1])
                for tb in range(8):
                    pp = tb % 2

                    def mmsp(t, tq=tq, tb=tb, pp=pp, g_oc=g_oc_of[tq],
                             g_wo=self.g_wo,
                             ev=(ev_prev[pp] if tb < 2 else a_base + tb - 1),
                             v_pre=(v_pre if tq == 0 and tb == 0 else 0)):
                        t.wait_ge(p.s_g, g_oc)
                        t.wait_ge(p.s_s, g_wo)
                        if v_pre:
                            t.wait_ge(p.s_v, v_pre)
                        if ev:
                            t.wait_ge(p.s_a, ev)
                        for dt in range(4):
                            mmi = None
                            for hb in range(2):
                                mmi = t.matmul(
                                    ps_sp[:, (pp * 4 + dt) * 512:(pp * 4 + dt + 1) * 512],
                                    oc[tq][:, hb * 1024 + tb * 128:
                                            hb * 1024 + (tb + 1) * 128],
                                    wob[:, (hb * 4 + dt) * 512:(hb * 4 + dt + 1) * 512],
                                    start=(hb == 0), stop=(hb == 1),
                                    skip_group_check=True)
                            p.inc(mmi, "t")
                    p.sec("tensor", mmsp)

                def evq(sct, t_base=t_base, s_base=s_base, sb_prev=sb_prev):
                    for tb in range(8):
                        pp = tb % 2
                        par = tb % 2
                        sn = sb_prev[par] if tb < 2 else s_base + 16 * (tb - 1)
                        if sn:
                            sct.wait_ge(p.s_s, sn)
                        sct.wait_ge(p.s_t, t_base + 4 * (tb + 1))
                        p.inc(sct.activation(s_sb[par][:, :],
                                             ps_sp[:, pp * 2048:(pp + 1) * 2048],
                                             AF.Identity), "a")
                p.sec("scalar", evq)
                assert p.cnt["a"] == a_base + 8

                for tb in range(8):
                    par = tb % 2

                    def ssb(sy, b=b, tq=tq, tb=tb, par=par, a_d=a_base + tb + 1):
                        sy.wait_ge(p.s_a, a_d)
                        row = tq * 1024 + tb * 128
                        p.inc(sy.dma_start(out=p.cc_s[b][row:row + 128, :],
                                           in_=s_sb[par][:]), "s")
                    p.sec("sync", ssb)
                assert p.cnt["s"] == s_base + 16 * 8
                st["bank_ev"][0] = a_base + 7
                st["bank_ev"][1] = a_base + 8
                st["sb_dma_g"][0] = s_base + 16 * 7
                st["sb_dma_g"][1] = s_base + 16 * 8
                st["oc_mm_t"][tq] = p.cnt["t"]
                p.flush()

        s_all = p.cnt["s"]

        def coll2(g, s_all=s_all, b=b):
            g.wait_ge(p.s_s, s_all)
            p.inc(g.collective_compute("ReduceScatter", ALU.add, ins=[p.cc_s[b]],
                                       outs=[p.rs_s[b]],
                                       replica_groups=[list(range(NC))]), "c")
        p.sec("gpsimd", coll2)
        st["c_rs"][b] = p.cnt["c"]

    def wo_final(self):
        p, nc = self, self.nc
        with ExitStack() as es:
            E = es.enter_context
            x_f = [E(nc.sbuf_tensor(f"xr{i}", [128, D], F32)) for i in range(2)]
            st_bf = [E(nc.sbuf_tensor(f"stb{i}", [128, D], BF16)) for i in range(2)]
            gt_bf = [E(nc.sbuf_tensor(f"gtb{i}", [128, D], BF16)) for i in range(2)]
            s_f = E(nc.sbuf_tensor("s_fin", [128, D], F32))
            gg_f = E(nc.sbuf_tensor("gg_f", [128, D], F32))
            y_sb = [E(nc.sbuf_tensor(f"ysb{i}", [128, D], F32)) for i in range(2)]
            v_done = [0, 0]
            g_out_of = [0, 0]
            for tb in range(NT):
                par = tb % 2

                half = 0 if tb < 4 else 1
                rs_src = p.rs_s[half][(tb - 4 * half) * 128:(tb - 4 * half + 1) * 128, :]

                def lfin(sy, tb=tb, par=par, v_need=v_done[par],
                         c_rs=self.wo_st["c_rs"][half], rs_src=rs_src):
                    sy.wait_ge(p.s_c, c_rs)
                    if v_need:
                        sy.wait_ge(p.s_v, v_need)
                    p.inc(sy.dma_start(out=x_f[par][:], in_=p.x[tb * 128:(tb + 1) * 128, :]), "s")
                    p.inc(sy.dma_start(out=st_bf[par][:], in_=rs_src), "s")
                    p.inc(sy.dma_start(out=gt_bf[par][:],
                                       in_=p.cc_g[tb * 128:(tb + 1) * 128, :]), "s")
                p.sec("sync", lfin)

                def vfin(v, tb=tb, par=par, s_n=p.cnt["s"], s_y=g_out_of[par]):
                    v.wait_ge(p.s_s, max(s_n, s_y))
                    v.tensor_copy(s_f[:], st_bf[par][:])
                    v.tensor_copy(gg_f[:], gt_bf[par][:])
                    v.tensor_tensor(y_sb[par][:], s_f[:], gg_f[:], op=ALU.mult)
                    p.inc(v.tensor_tensor(y_sb[par][:], y_sb[par][:], x_f[par][:],
                                          op=ALU.add), "v")
                p.sec("vector", vfin)
                v_yt = p.cnt["v"]
                v_done[par] = v_yt

                def sout(sy, tb=tb, par=par, v_yt=v_yt):
                    sy.wait_ge(p.s_v, v_yt)
                    p.inc(sy.dma_start(out=p.y[tb * 128:(tb + 1) * 128, :], in_=y_sb[par][:]), "s")
                p.sec("sync", sout)
                g_out_of[par] = p.cnt["s"]
                if tb % 2 == 1:
                    p.flush()

            if DEBUG_TAPS:
                tapb = E(nc.sbuf_tensor("tapb", [128, D], BF16))
                for (src, dst, n) in ((p.cc_xn, p.tp_xn, NT),):
                    for i in range(n):
                        def tl(g, src=src, i=i, gn=p.cnt["g"]):
                            g.wait_ge(p.s_g, gn)
                            p.inc(g.dma_start(out=tapb[:], in_=src[i * 128:(i + 1) * 128, :]), "g")
                        p.sec("gpsimd", tl)
                        def ts_(g, dst=dst, i=i, gn=p.cnt["g"]):
                            g.wait_ge(p.s_g, gn)
                            p.inc(g.dma_start(out=dst[i * 128:(i + 1) * 128, :], in_=tapb[:]), "g")
                        p.sec("gpsimd", ts_)

            g_fin = p.cnt["g"]
            s_fin = p.cnt["s"]

            def fin(g, g_fin=g_fin, s_fin=s_fin):
                g.wait_ge(p.s_g, g_fin)
                g.wait_ge(p.s_s, s_fin)
            p.sec("gpsimd", fin)

    def close(self):
        self.es.close()


def build_program():
    p = Prog()
    p.setup()
    with ExitStack() as es:
        p.xqT = es.enter_context(p.nc.sbuf_tensor("xqT", [128, NDC * R], BF16))
        p.phase1()
        p.ffn()
    p.attention()
    p.wo_final()
    p.flush()
    p.close()
    return p.nc


_masks_cache = None


def _masks():
    global _masks_cache
    if _masks_cache is None:
        m = np.zeros((4, 128, 1024), np.float32)
        pp = np.arange(128)[:, None]
        ff = np.arange(512)[None, :]
        for v in range(4):
            mv = (pp + v * 128 <= ff).astype(np.float32)
            m[v, :, 0:512] = mv
            m[v, :, 512:1024] = mv
        _masks_cache = m
    return _masks_cache


def kernel(L0_attn_x, L0_attn_Wq, L0_attn_Wk, L0_attn_Wv, L0_attn_Wo,
           L0_attn_lambda_init, L0_ffn_W_g, L0_ffn_W_u, L0_ffn_W_d):
    x = np.ascontiguousarray(np.asarray(L0_attn_x, np.float32)).reshape(B * T, D)
    Wq = np.asarray(L0_attn_Wq, np.float32)
    Wk = np.asarray(L0_attn_Wk, np.float32)
    Wv = np.asarray(L0_attn_Wv, np.float32)
    Wo = np.asarray(L0_attn_Wo, np.float32)
    lam = np.asarray(L0_attn_lambda_init, np.float32)
    Wg = np.asarray(L0_ffn_W_g, np.float32)
    Wu = np.asarray(L0_ffn_W_u, np.float32)
    Wd = np.asarray(L0_ffn_W_d, np.float32)

    def qkv_img(W, c):
        S = W[:, 256 * c:256 * (c + 1)]
        return np.ascontiguousarray(
            S.reshape(16, 128, 2, 128).transpose(1, 2, 0, 3).reshape(128, 4096))

    wg_img = np.ascontiguousarray(
        Wg.reshape(64, 128, 16, 128).transpose(0, 3, 2, 1).reshape(64, 128, 2048))
    wu_img = np.ascontiguousarray(
        Wu.reshape(64, 128, 16, 128).transpose(0, 3, 2, 1).reshape(64, 128, 2048))
    wd_img = np.ascontiguousarray(
        Wd.reshape(4, 512, 64, 128).transpose(0, 2, 3, 1))
    msk = _masks()

    in_maps = []
    for c in range(NC):
        wo_c = np.ascontiguousarray(
            Wo[256 * c:256 * (c + 1)].reshape(2, 128, 4, 512).transpose(0, 2, 1, 3))
        in_maps.append({
            "x": np.ascontiguousarray(np.concatenate(
                [x[c * 512:(c + 1) * 512], x[T + c * 512:T + (c + 1) * 512]])),
            "wq": qkv_img(Wq, c), "wk": qkv_img(Wk, c), "wv": qkv_img(Wv, c),
            "wo": wo_c, "wg": wg_img, "wu": wu_img, "wd": wd_img,
            "lam": np.ascontiguousarray(lam[2 * c:2 * c + 2].reshape(1, 2)),
            "msk": msk,
        })

    nc = build_program()
    res = run_bass_kernel_spmd(nc, in_maps, core_ids=list(range(NC)))
    out = np.empty((B * T, D), np.float32)
    for c in range(NC):
        yc = np.asarray(res.results[c]["y"], np.float32)
        out[c * 512:(c + 1) * 512] = yc[0:512]
        out[T + c * 512:T + (c + 1) * 512] = yc[512:1024]
    return out.reshape(B, T, D)


if __name__ == "__main__":
    import reference as ref
    ins = ref.setup_inputs(0)
    got = kernel(**{k: np.asarray(v) for k, v in ins.items()})
    exp = np.asarray(ref.reference(**ins))
    err = np.abs(got - exp)
    rv = float(np.mean((got - exp) ** 2) / np.mean(np.asarray(exp) ** 2))
    print("maxabs", err.max(), "resvar", rv)


# version 31 (final)
# speedup vs baseline: 1.0638x; 1.0052x over previous
import numpy as np
import concourse.bass as bass

DEBUG_TAPS = False
FLUSH_N = {"tensor": 1, "scalar": 1, "vector": 1, "sync": 1, "gpsimd": 1}

import concourse.mybir as mybir
from concourse.bass_utils import run_bass_kernel_spmd
from contextlib import ExitStack

F32 = mybir.dt.float32
BF16 = mybir.dt.bfloat16
FP16 = mybir.dt.float16
AF = mybir.ActivationFunctionType
ALU = mybir.AluOpType
AX = mybir.AxisListType

B, T, D = 2, 4096, 2048
H = 16
HF = 8192
NC = 8
R = (B * T) // NC
NT = R // 128
NDC = D // 128
SCALE = 0.125


class Prog:
    def __init__(self):
        nc = bass.Bass("TRN2", target_bir_lowering=False)
        self.nc = nc
        self.es = ExitStack()
        ap = lambda name, shape, dt, kind: nc.dram_tensor(name, shape, dt, kind=kind).ap()
        self.x = ap("x", [R, D], F32, "ExternalInput")
        self.wq = ap("wq", [128, 4096], F32, "ExternalInput")
        self.wk = ap("wk", [128, 4096], F32, "ExternalInput")
        self.wv = ap("wv", [128, 4096], F32, "ExternalInput")
        self.wo = ap("wo", [2, 4, 128, 512], F32, "ExternalInput")
        self.wg = ap("wg", [64, 128, 2048], F32, "ExternalInput")
        self.wu = ap("wu", [64, 128, 2048], F32, "ExternalInput")
        self.wd = ap("wd", [4, 64, 128, 512], F32, "ExternalInput")
        self.lam = ap("lam", [1, 2], F32, "ExternalInput")
        self.msk = ap("msk", [4, 128, 1024], F32, "ExternalInput")
        self.y = ap("y", [R, D], F32, "ExternalOutput")
        KI = "ExternalOutput" if DEBUG_TAPS else "Internal"
        aps = lambda name, shape, dt: nc.dram_tensor(name, shape, dt, kind="Internal",
                                                     addr_space="Shared").ap()
        self.cc_xn = ap("cc_xn", [R, D], BF16, "Internal")
        self.ag_xn = aps("ag_xn", [NC, R, D], BF16)
        self.cc_o = ap("cc_o", [2, 2, 128, T], BF16, KI)
        self.cc_s = [ap(f"cc_s{i}", [T, D], BF16, "Internal") for i in range(2)]
        self.cc_g = ap("cc_g", [R, D], BF16, KI)
        self.rs_s = [ap(f"rs_s{i}", [512, D], BF16, "Internal") for i in range(2)]
        self.wq_d = ap("wq_d", [128, 4096], BF16, "Internal")
        self.wk_d = ap("wk_d", [128, 4096], BF16, "Internal")
        self.wv_d = ap("wv_d", [128, 4096], BF16, "Internal")
        self.msk_d = ap("msk_d", [128, 4096], BF16, "Internal")
        self.wob_d = ap("wob_d", [128, 4096], BF16, "Internal")
        if DEBUG_TAPS:
            self.tp_xn = ap("tp_xn", [R, D], BF16, "ExternalOutput")
            self.tp_s = ap("tp_s", [NC * R, D], BF16, "ExternalOutput")
            self.tp_rs = ap("tp_rs", [R, D], BF16, "ExternalOutput")

        E = self.es.enter_context
        self.s_t = E(nc.semaphore("s_t"))
        self.s_a = E(nc.semaphore("s_a"))
        self.s_v = E(nc.semaphore("s_v"))
        self.s_s = E(nc.semaphore("s_s"))
        self.s_g = E(nc.semaphore("s_g"))
        self.s_c = E(nc.semaphore("s_c"))
        self.s_w = E(nc.semaphore("s_w"))
        self.cnt = {"t": 0, "a": 0, "v": 0, "s": 0, "g": 0, "c": 0, "w": 0}
        self.sem = {"t": self.s_t, "a": self.s_a, "v": self.s_v,
                    "s": self.s_s, "g": self.s_g, "c": self.s_c, "w": self.s_w}

        sb = lambda n, s, d: E(nc.sbuf_tensor(n, s, d))
        self.xqT = None
        self.ones_col = sb("ones_col", [128, 1], BF16)
        self.ones_mat = sb("ones_mat", [128, 128], BF16)
        self.ones_row = sb("ones_row", [1, 128], BF16)
        self.onesr_f = sb("onesr_f", [1, 128], F32)
        self.b1536 = sb("b1536", [128, 1], F32)
        self.lamneg = sb("lamneg", [128, 2], F32)
        self.lam_sig = sb("lam_sig", [1, 2], F32)
        self.lam_sb = sb("lam_sb", [1, 2], F32)

        self.block = E(nc.Block())
    def sec(self, eng, fn):
        getattr(self.block, eng)(fn)

    def flush(self):
        pass

    def inc(self, instr, key, by=None):
        by = by if by is not None else (16 if key in ("s", "g", "w") else 1)
        instr.then_inc(self.sem[key], by)
        self.cnt[key] += by
        return self.cnt[key]

    def barrier(self):
        self.flush()
        snap = dict(self.cnt)

        def mk():
            def f(e):
                for k in ("t", "a", "v", "s", "g", "w"):
                    if snap[k]:
                        e.wait_ge(self.sem[k], snap[k])
            return f
        for eng in ("tensor", "scalar", "vector", "sync", "gpsimd"):
            self.sec(eng, mk())

    def setup(self):
        p = self

        def vec(v):
            v.memset(p.ones_col[:], 1.0)
            v.memset(p.ones_mat[:], 1.0)
            v.memset(p.ones_row[:], 1.0)
            v.memset(p.onesr_f[:], 1.0)
            p.inc(v.memset(p.b1536[:], 1536.0), "v")
        p.sec("vector", vec)

        def gp(g):
            p.inc(g.dma_start(out=p.lam_sb[:], in_=p.lam), "g")
        p.sec("gpsimd", gp)
        self.g_lam = p.cnt["g"]

    def phase1(self):
        p, nc = self, self.nc
        with ExitStack() as es:
            E = es.enter_context
            x_f = [E(nc.sbuf_tensor(f"x_f{i}", [128, D], F32)) for i in range(2)]
            xc_f = [E(nc.sbuf_tensor(f"xc_f{i}", [128, D], F32)) for i in range(2)]
            xn_f = [E(nc.sbuf_tensor(f"xn_f{i}", [128, D], F32)) for i in range(2)]
            xnn_f = [E(nc.sbuf_tensor(f"xnn_f{i}", [128, D], F32)) for i in range(2)]
            xn_bf = [E(nc.sbuf_tensor(f"xn_bf{i}", [128, D], BF16)) for i in range(2)]
            xq_bf = [E(nc.sbuf_tensor(f"xq_bf{i}", [128, D], BF16)) for i in range(2)]
            f16q = [E(nc.sbuf_tensor(f"f16q{i}", [128, D], FP16)) for i in range(2)]
            xq_f = [E(nc.sbuf_tensor(f"xq_f{i}", [128, D], F32)) for i in range(2)]
            dump = E(nc.sbuf_tensor("dump", [128, D], F32))
            c2 = lambda n: [E(nc.sbuf_tensor(n + str(i), [128, 1], F32)) for i in range(2)]
            s_sum, s_sq, mneg, e2eps, msqn, vare, sqv, rs = (
                c2("s_sum"), c2("s_sq"), c2("mneg"), c2("e2eps"), c2("msqn"),
                c2("vare"), c2("sqv"), c2("rs"))
            s_sum2, s_sq2, m2neg, e2eps2, msqn2, vare2, sqv2, rs2 = (
                c2("s_sum2"), c2("s_sq2"), c2("m2neg"), c2("e2eps2"), c2("msqn2"),
                c2("vare2"), c2("sqv2"), c2("rs2"))
            amax, rtmp, sq, invsq, vgap = c2("amax"), c2("rtmp"), c2("sq"), c2("invsq"), c2("vgap")

            a_last = [0, 0]
            v1_of = [0] * (NT + 2)
            s_tr_of = [0] * NT
            g_xn_of = [0, 0]

            for i0 in range(0, NT, 2):
                pair = (i0, i0 + 1)

                def lx(sy, pair=pair, an=(a_last[0], a_last[1])):
                    for w, i in enumerate(pair):
                        if an[i % 2]:
                            sy.wait_ge(p.s_a, an[i % 2])
                        p.inc(sy.dma_start(out=x_f[i % 2][:], in_=p.x[i * 128:(i + 1) * 128, :]), "s")
                p.sec("sync", lx)
                s_x = p.cnt["s"]
                if i0 == 0:
                    def lw01(sy):
                        p.inc(sy.dma_start(out=self.slab[0][:], in_=p.wg[0]), "s")
                        p.inc(sy.dma_start(out=self.slab[1][:], in_=p.wu[0]), "s")
                    p.sec("sync", lw01)
                    self.s_lw0 = p.cnt["s"]

                a0 = p.cnt["a"]
                v0 = p.cnt["v"]

                def asec(s, pair=pair, s_x=s_x, a0=a0, v0=v0,
                         vn=(v1_of[i0 - 2] if i0 >= 2 else 0,
                             v1_of[i0 - 1] if i0 >= 1 else 0)):
                    for w, i in enumerate(pair):
                        par = i % 2
                        s.wait_ge(p.s_s, s_x - 16 * (1 - w))
                        if vn[par]:
                            s.wait_ge(p.s_v, vn[par])
                        s.activation(dump[:], x_f[par][:], AF.Identity, accum_out=s_sum[par][:, 0:1])
                        p.inc(s.activation(dump[:], x_f[par][:], AF.Square,
                                           accum_out=s_sq[par][:, 0:1]), "a")
                    for w in range(2):
                        par = w
                        s.wait_ge(p.s_v, v0 + 1 + w)
                        s.activation(xc_f[par][:], x_f[par][:], AF.Identity, bias=mneg[par][:, 0:1])
                        p.inc(s.activation(msqn[par][:], mneg[par][:], AF.Square), "a")
                    for w in range(2):
                        par = w
                        s.wait_ge(p.s_v, v0 + 3 + w)
                        p.inc(s.activation(sqv[par][:], vare[par][:], AF.Sqrt), "a")
                    for w in range(2):
                        par = w
                        s.wait_ge(p.s_v, v0 + 5 + w)
                        s.activation(xn_f[par][:], xc_f[par][:], AF.Identity, scale=rs[par][:, 0:1],
                                     accum_out=s_sum2[par][:, 0:1])
                        p.inc(s.activation(dump[:], xn_f[par][:], AF.Square,
                                           accum_out=s_sq2[par][:, 0:1]), "a")
                    for w in range(2):
                        par = w
                        s.wait_ge(p.s_v, v0 + 7 + w)
                        s.activation(xc_f[par][:], xn_f[par][:], AF.Identity, bias=m2neg[par][:, 0:1])
                        p.inc(s.activation(msqn2[par][:], m2neg[par][:], AF.Square), "a")
                    for w in range(2):
                        par = w
                        s.wait_ge(p.s_v, v0 + 9 + w)
                        p.inc(s.activation(sqv2[par][:], vare2[par][:], AF.Sqrt), "a")
                    for w in range(2):
                        par = w
                        s.wait_ge(p.s_v, v0 + 11 + w)
                        p.inc(s.activation(xnn_f[par][:], xc_f[par][:], AF.Identity,
                                           scale=rs2[par][:, 0:1]), "a")
                    for w in range(2):
                        par = w
                        s.wait_ge(p.s_v, v0 + 13 + w)
                        p.inc(s.activation(f16q[par][:], xnn_f[par][:], AF.Identity,
                                           bias=p.b1536[:, 0:1], scale=sq[par][:, 0:1]), "a")
                p.sec("scalar", asec)
                assert p.cnt["a"] == a0 + 16

                def vsec(v, a0=a0, v0=v0, gn=(g_xn_of[0], g_xn_of[1]),
                         sn=(s_tr_of[i0 - 2] if i0 >= 2 else 0,
                             s_tr_of[i0 - 1] if i0 >= 1 else 0)):
                    for w in range(2):
                        par = w
                        v.wait_ge(p.s_a, a0 + 1 + w)
                        v.tensor_scalar(mneg[par][:], s_sum[par][:], -1.0 / D, None, op0=ALU.mult)
                        p.inc(v.tensor_scalar(e2eps[par][:], s_sq[par][:], 1.0 / D, 1e-5,
                                              op0=ALU.mult, op1=ALU.add), "v")
                    for w in range(2):
                        par = w
                        v.wait_ge(p.s_a, a0 + 3 + w)
                        p.inc(v.tensor_tensor(vare[par][:], e2eps[par][:], msqn[par][:],
                                              op=ALU.subtract), "v")
                    for w in range(2):
                        par = w
                        v.wait_ge(p.s_a, a0 + 5 + w)
                        p.inc(v.reciprocal(rs[par][:], sqv[par][:]), "v")
                    for w in range(2):
                        par = w
                        v.wait_ge(p.s_a, a0 + 7 + w)
                        if gn[par]:
                            v.wait_ge(p.s_g, gn[par])
                        v.tensor_scalar(m2neg[par][:], s_sum2[par][:], -1.0 / D, None, op0=ALU.mult)
                        v.tensor_scalar(e2eps2[par][:], s_sq2[par][:], 1.0 / D, 1e-8,
                                        op0=ALU.mult, op1=ALU.add)
                        p.inc(v.tensor_copy(xn_bf[par][:], xn_f[par][:]), "v")
                    for w in range(2):
                        par = w
                        v.wait_ge(p.s_a, a0 + 9 + w)
                        p.inc(v.tensor_tensor(vare2[par][:], e2eps2[par][:], msqn2[par][:],
                                              op=ALU.subtract), "v")
                    for w in range(2):
                        par = w
                        v.wait_ge(p.s_a, a0 + 11 + w)
                        p.inc(v.reciprocal(rs2[par][:], sqv2[par][:]), "v")
                    for w in range(2):
                        par = w
                        v.wait_ge(p.s_a, a0 + 13 + w)
                        v.reduce_max(amax[par][:, 0:1], xnn_f[par][:], axis=AX.X,
                                     apply_absolute_value=True)
                        v.tensor_scalar(vgap[par][:], amax[par][:], 1.0, None, op0=ALU.mult)
                        v.reciprocal(rtmp[par][:], amax[par][:])
                        v.tensor_scalar(invsq[par][:], amax[par][:], 1.0 / 127.0, None, op0=ALU.mult)
                        p.inc(v.tensor_scalar(sq[par][:], rtmp[par][:], 127.0, None, op0=ALU.mult), "v")
                    for w in range(2):
                        par = w
                        v.wait_ge(p.s_a, a0 + 15 + w)
                        if sn[par]:
                            v.wait_ge(p.s_s, sn[par])
                        v.tensor_scalar(xq_f[par][:], f16q[par][:], -1536.0, invsq[par][:, 0:1],
                                        op0=ALU.add, op1=ALU.mult)
                        p.inc(v.tensor_copy(xq_bf[par][:], xq_f[par][:]), "v")
                p.sec("vector", vsec)
                assert p.cnt["v"] == v0 + 16
                v1_of[i0] = v0 + 1
                v1_of[i0 + 1] = v0 + 2
                a_last[i0 % 2] = a0 + 15
                a_last[(i0 + 1) % 2] = a0 + 16

                def gs(g, pair=pair, v0=v0):
                    for w, i in enumerate(pair):
                        par = i % 2
                        g.wait_ge(p.s_v, v0 + 7 + w)
                        p.inc(g.dma_start(out=p.cc_xn[i * 128:(i + 1) * 128, :],
                                          in_=xn_bf[par][:]), "g")
                p.sec("gpsimd", gs)
                g_xn_of[i0 % 2] = p.cnt["g"] - 16
                g_xn_of[(i0 + 1) % 2] = p.cnt["g"]

                def tr(sy, pair=pair, v0=v0):
                    for w, i in enumerate(pair):
                        par = i % 2
                        sy.wait_ge(p.s_v, v0 + 15 + w)
                        for d in range(NDC):
                            p.inc(sy.dma_start(
                                out=p.xqT[:, d * R + i * 128: d * R + (i + 1) * 128],
                                in_=xq_bf[par][:, d * 128:(d + 1) * 128], transpose=True), "s")
                p.sec("sync", tr)
                s_tr_of[i0] = p.cnt["s"] - 16 * NDC
                s_tr_of[i0 + 1] = p.cnt["s"]
                if i0 == 2:
                    self.s_xqT_half = p.cnt["s"]
                p.flush()


            self.s_xqT = p.cnt["s"]
            g_all = p.cnt["g"]

            def coll(g, g_all=g_all):
                g.wait_ge(p.s_g, g_all)
                p.inc(g.collective_compute("AllGather", ALU.bypass, ins=[p.cc_xn],
                                           outs=[p.ag_xn],
                                           replica_groups=[list(range(NC))]), "c")
            p.sec("gpsimd", coll)
            self.c_xn = p.cnt["c"]
            self.barrier()

    def ffn(self):
        p, nc = self, self.nc
        with nc.psum_tensor("ps_lam", [128, 2], F32) as ps_lam:
            def alam(s, gl=self.g_lam):
                s.wait_ge(p.s_g, gl)
                p.inc(s.activation(p.lam_sig[:], p.lam_sb[:], AF.Sigmoid), "a")
            p.sec("scalar", alam)

            def tlam(t, a_lam=p.cnt["a"]):
                t.wait_ge(p.s_a, a_lam)
                p.inc(t.matmul(ps_lam[:], p.onesr_f[:], p.lam_sig[:], start=True, stop=True), "t")
            p.sec("tensor", tlam)

            def vlam(v, t_lam=p.cnt["t"]):
                v.wait_ge(p.s_t, t_lam)
                p.inc(v.tensor_scalar(p.lamneg[:], ps_lam[:], -1.0, None, op0=ALU.mult), "v")
            p.sec("vector", vlam)

        def wstage(g):
            p.inc(g.dma_start(out=p.wq_d, in_=p.wq), "g")
            p.inc(g.dma_start(out=p.wk_d, in_=p.wk), "g")
            p.inc(g.dma_start(out=p.wv_d, in_=p.wv), "g")
            p.inc(g.dma_start(out=p.msk_d.rearrange("b (a c) -> b a c", c=1024),
                              in_=p.msk.rearrange("a b c -> b a c")), "g")
            p.inc(g.dma_start(out=p.wob_d.rearrange("p (a b f) -> p a b f", b=4, f=512),
                              in_=p.wo.rearrange("a b p f -> p a b f")), "g")
        p.sec("gpsimd", wstage)
        self.g_stage = p.cnt["g"]

        with ExitStack() as es:
            E = es.enter_context
            slab = self.slab
            sgW = [E(nc.sbuf_tensor(f"sgW{i}", [128, 2048], BF16)) for i in range(4)]
            sgm = E(nc.sbuf_tensor("sgm", [128, 1024], F32))
            sil = E(nc.sbuf_tensor("sil", [128, 1024], F32))
            pT = E(nc.sbuf_tensor("pT", [128, 16 * R], BF16))
            f_sb = E(nc.sbuf_tensor("f_sb", [128, NT * D], F32))
            wdf = [E(nc.sbuf_tensor(f"wdf{i}", [128, 2048], F32)) for i in range(2)]
            wdb = [E(nc.sbuf_tensor(f"wdb{i}", [128, 2048], BF16)) for i in range(2)]
            g_t = [E(nc.sbuf_tensor(f"g_t{i}", [128, D], BF16)) for i in range(2)]

            sign_a = 0
            sg_mm_t = [0, 0]
            a_w_of = {}
            evac_v = 0

            def emit_lw(hb):
                def lw(sy, hb=hb, fa=sign_a):
                    if fa:
                        sy.wait_ge(p.s_a, fa)
                    p.inc(sy.dma_start(out=slab[0][:], in_=p.wg[hb]), "s")
                    p.inc(sy.dma_start(out=slab[1][:], in_=p.wu[hb]), "s")
                p.sec("sync", lw)
                return p.cnt["s"]

            for q in range(4):
                with nc.psum_tensor(f"ps_yga{q}", [128, 1024], F32) as ps_yga, \
                     nc.psum_tensor(f"ps_yua{q}", [128, 1024], F32) as ps_yua, \
                     nc.psum_tensor(f"ps_ygb{q}", [128, 1024], F32) as ps_ygb, \
                     nc.psum_tensor(f"ps_yub{q}", [128, 1024], F32) as ps_yub:
                    evac_pair = [0, 0]
                    if q == 0:
                        s_w0 = self.s_lw0

                        def sw0(sct, s_w0=s_w0):
                            sct.wait_ge(p.s_s, s_w0)
                            sct.activation(sgW[0][:], slab[0][:], AF.Sign)
                            p.inc(sct.activation(sgW[1][:], slab[1][:], AF.Sign), "a")
                        p.sec("scalar", sw0)
                        a_w_of[0] = p.cnt["a"]
                        sign_a = p.cnt["a"]
                    for jb in range(16):
                        hb = q * 16 + jb
                        par = hb % 2
                        ps_yg = ps_yga if par == 0 else ps_ygb
                        ps_yu = ps_yua if par == 0 else ps_yub

                        s_wn = emit_lw(hb + 1) if hb + 1 < 64 else 0

                        if q == 0 and jb == 0:
                            for th in range(2):
                                def mm0(t, th=th, a_w=a_w_of[0],
                                        s_need=(self.s_xqT_half if th == 0
                                                else self.s_xqT),
                                        ev0=evac_v):
                                    t.wait_ge(p.s_s, s_need)
                                    t.wait_ge(p.s_a, a_w)
                                    if ev0:
                                        t.wait_ge(p.s_v, ev0)
                                    mmi = None
                                    for which in range(2):
                                        dst = ps_yg if which == 0 else ps_yu
                                        src = sgW[which]
                                        for j in range(NDC):
                                            mmi = t.matmul(dst[:, th * 512:(th + 1) * 512],
                                                           src[:, j * 128:(j + 1) * 128],
                                                           p.xqT[:, j * R + th * 512: j * R + (th + 1) * 512],
                                                           start=(j == 0), stop=(j == NDC - 1),
                                                           skip_group_check=True)
                                    p.inc(mmi, "t")
                                p.sec("tensor", mm0)
                        else:
                            def mm(t, par=par, a_w=a_w_of[hb], ev=evac_pair[par],
                                   ev0=(evac_v if jb <= 1 else 0)):
                                t.wait_ge(p.s_a, a_w)
                                if ev:
                                    t.wait_ge(p.s_v, ev)
                                elif ev0:
                                    t.wait_ge(p.s_v, ev0)
                                mmi = None
                                for which in range(2):
                                    dst = ps_yg if which == 0 else ps_yu
                                    src = sgW[2 * par + which]
                                    for th in range(2):
                                        for j in range(NDC):
                                            mmi = t.matmul(dst[:, th * 512:(th + 1) * 512],
                                                           src[:, j * 128:(j + 1) * 128],
                                                           p.xqT[:, j * R + th * 512: j * R + (th + 1) * 512],
                                                           start=(j == 0), stop=(j == NDC - 1),
                                                           skip_group_check=True)
                                p.inc(mmi, "t")
                            p.sec("tensor", mm)
                        t_mm = p.cnt["t"]
                        sg_mm_t[par] = t_mm

                        def ev_a(sct, hb=hb, s_wn=s_wn, t_mm=t_mm, v_prev=evac_v,
                                 npar=(hb + 1) % 2, ftn=sg_mm_t[(hb + 1) % 2]):
                            if s_wn:
                                sct.wait_ge(p.s_s, s_wn)
                                if ftn:
                                    sct.wait_ge(p.s_t, ftn)
                                sct.activation(sgW[2 * npar][:], slab[0][:], AF.Sign)
                                p.inc(sct.activation(sgW[2 * npar + 1][:], slab[1][:],
                                                     AF.Sign), "a")
                            sct.wait_ge(p.s_t, t_mm)
                            if v_prev:
                                sct.wait_ge(p.s_v, v_prev)
                            p.inc(sct.activation(sgm[:], ps_yg[:], AF.Sigmoid), "a")
                        p.sec("scalar", ev_a)
                        if hb + 1 < 64:
                            a_w_of[hb + 1] = p.cnt["a"] - 1
                            sign_a = p.cnt["a"] - 1

                        def ev_v(v, jb=jb, a_sg=p.cnt["a"], t_mm=t_mm):
                            v.wait_ge(p.s_a, a_sg)
                            v.tensor_tensor(sil[:], ps_yg[:], sgm[:], op=ALU.mult)
                            p.inc(v.tensor_tensor(pT[:, jb * R:(jb + 1) * R], sil[:], ps_yu[:],
                                                  op=ALU.mult), "v")
                        p.sec("vector", ev_v)
                        evac_v = p.cnt["v"]

                v_pT = p.cnt["v"]
                if q == 0:
                    wd_sign_a = [0, 0]
                    wd_mm_t = [0, 0]
                with nc.psum_tensor(f"ps_f{q}", [128, 8 * 512], F32) as ps_f:
                    bank_ev = [0] * 8
                    for dt in range(4):
                        for jh in range(2):
                            g_wd_of = [0, 0]
                            for par in range(2):
                                jq = 2 * jh + par

                                def lwd(g, dt=dt, hb0=q * 16 + jq * 4, par=par,
                                        fa=wd_sign_a[par]):
                                    if fa:
                                        g.wait_ge(p.s_a, fa)
                                    p.inc(g.dma_start(
                                        out=wdf[par][:].rearrange("p (c f) -> p c f", f=512),
                                        in_=p.wd[dt, hb0:hb0 + 4].rearrange("c p f -> p c f")), "g")
                                p.sec("gpsimd", lwd)
                                g_wd_of[par] = p.cnt["g"]

                            def swd2(sct, g_wd_of=list(g_wd_of),
                                     fts=(wd_mm_t[0], wd_mm_t[1])):
                                for par in range(2):
                                    sct.wait_ge(p.s_g, g_wd_of[par])
                                    if fts[par]:
                                        sct.wait_ge(p.s_t, fts[par])
                                    p.inc(sct.activation(wdb[par][:], wdf[par][:],
                                                         AF.Sign), "a")
                            p.sec("scalar", swd2)
                            wd_sign_a[0] = p.cnt["a"] - 1
                            wd_sign_a[1] = p.cnt["a"]

                            for par in range(2):
                                jq = 2 * jh + par
                                a_wd = p.cnt["a"] - 1 + par

                                def mm3(t, jq=jq, par=par, a_wd=a_wd, v_pT=v_pT,
                                        ev=(list(bank_ev) if jq == 0 else None)):
                                    t.wait_ge(p.s_a, a_wd)
                                    t.wait_ge(p.s_v, v_pT)
                                    mmi = None
                                    for tb in range(8):
                                        if ev and ev[tb]:
                                            t.wait_ge(p.s_v, ev[tb])
                                        for jc in range(4):
                                            j = jq * 4 + jc
                                            mmi = t.matmul(ps_f[:, tb * 512:(tb + 1) * 512],
                                                           pT[:, j * R + tb * 128: j * R + tb * 128 + 128],
                                                           wdb[par][:, jc * 512:(jc + 1) * 512],
                                                           start=(j == 0), stop=(j == 15),
                                                           skip_group_check=True)
                                    p.inc(mmi, "t")
                                p.sec("tensor", mm3)
                                wd_mm_t[par] = p.cnt["t"]
                        t_dt = p.cnt["t"]

                        def ev3(v, q=q, dt=dt, t_dt=t_dt):
                            v.wait_ge(p.s_t, t_dt)
                            ins = None
                            for tb in range(8):
                                src = ps_f[:, tb * 512:(tb + 1) * 512]
                                dst = f_sb[:, tb * D + dt * 512: tb * D + (dt + 1) * 512]
                                if q == 0:
                                    ins = v.tensor_copy(dst, src)
                                else:
                                    ins = v.tensor_tensor(dst, dst, src, op=ALU.add)
                            p.inc(ins, "v")
                        p.sec("vector", ev3)
                        for tb in range(8):
                            bank_ev[tb] = p.cnt["v"]
                evac_v = p.cnt["v"]

            v_f = p.cnt["v"]
            g_store = [0, 0]
            for tb in range(NT):
                par = tb % 2

                def gsig(s, tb=tb, v_f=v_f):
                    if tb == 0:
                        s.wait_ge(p.s_v, v_f)
                    p.inc(s.activation(sgm[:], f_sb[:, tb * D: tb * D + 1024], AF.Sigmoid), "a")
                    p.inc(s.activation(sil[:], f_sb[:, tb * D + 1024: tb * D + 2048],
                                       AF.Sigmoid), "a")
                p.sec("scalar", gsig)

                def gmul(v, tb=tb, par=par, a_g=p.cnt["a"], g_need=g_store[par]):
                    v.wait_ge(p.s_a, a_g)
                    if g_need:
                        v.wait_ge(p.s_g, g_need)
                    v.tensor_tensor(g_t[par][:, 0:1024],
                                    f_sb[:, tb * D: tb * D + 1024], sgm[:], op=ALU.mult)
                    p.inc(v.tensor_tensor(g_t[par][:, 1024:2048],
                                          f_sb[:, tb * D + 1024: tb * D + 2048], sil[:],
                                          op=ALU.mult), "v")
                p.sec("vector", gmul)

                def gst(g, tb=tb, par=par, v_g=p.cnt["v"]):
                    g.wait_ge(p.s_v, v_g)
                    p.inc(g.dma_start(out=p.cc_g[tb * 128:(tb + 1) * 128, :], in_=g_t[par][:]), "g")
                p.sec("gpsimd", gst)
                g_store[par] = p.cnt["g"]
            self.barrier()

    def attention(self):
        p, nc = self, self.nc
        with ExitStack() as es:
            E = es.enter_context
            wqb = E(nc.sbuf_tensor("wqb", [128, 4096], BF16))
            wkb = E(nc.sbuf_tensor("wkb", [128, 4096], BF16))
            wvb = E(nc.sbuf_tensor("wvb", [128, 4096], BF16))
            mask_bf = E(nc.sbuf_tensor("mask_bf", [128, 4096], BF16))
            qT = E(nc.sbuf_tensor("qT", [128, 2 * T], BF16))
            kT = E(nc.sbuf_tensor("kT", [128, 2 * T], BF16))
            vT = E(nc.sbuf_tensor("vT", [128, 2 * T], BF16))
            vtok = E(nc.sbuf_tensor("vtok", [128, 2 * T], BF16))
            xnT = [E(nc.sbuf_tensor(f"xnT{i}", [128, NDC * 512], BF16)) for i in range(2)]
            strip = E(nc.sbuf_tensor("strip", [128, 4 * 1024], BF16))
            mtmp = [E(nc.sbuf_tensor(f"mtmp{i}", [128, 1024], BF16)) for i in range(2)]
            rl = E(nc.sbuf_tensor("rl", [1, 1024], F32))
            rl_bf = E(nc.sbuf_tensor("rl_bf", [1, 1024], BF16))
            Bsb = E(nc.sbuf_tensor("Bsb", [128, 1024], F32))
            Osb = E(nc.sbuf_tensor("Osb", [128, 1024], F32))
            cmb = E(nc.sbuf_tensor("cmb", [128, 1024], F32))
            ocm = E(nc.sbuf_tensor("ocm", [128, T], BF16))
            self.wob = E(nc.sbuf_tensor("wob", [128, 2 * 4 * 512], BF16))
            self.oc = [E(nc.sbuf_tensor(f"oc{i}", [128, 2 * 1024], BF16)) for i in range(4)]
            self.s_sb = [E(nc.sbuf_tensor(f"ssb{i}", [128, D], BF16)) for i in range(2)]
            self.wo_st = {"oc_mm_t": [0, 0, 0, 0], "sb_dma_g": [0, 0],
                          "bank_ev": [0, 0, 0, 0], "c_rs": [0, 0]}
            g_ocm_of = [None, None]
            g_ocm_hb = [0, 0]

            def gw(sy, g_stage=self.g_stage):
                sy.wait_ge(p.s_g, g_stage)
                p.inc(sy.dma_start(out=wqb[:], in_=p.wq_d), "s")
                p.inc(sy.dma_start(out=wkb[:], in_=p.wk_d), "s")
                p.inc(sy.dma_start(out=wvb[:], in_=p.wv_d), "s")
                p.inc(sy.dma_start(out=mask_bf[:], in_=p.msk_d), "s")
                p.inc(sy.dma_start(out=self.wob[:], in_=p.wob_d), "s")
            p.sec("sync", gw)
            g_w = p.cnt["s"]
            self.g_wo = p.cnt["s"]

            agxn = p.ag_xn.rearrange("a b c -> (a b) c")
            xnT_mm_t = [0, 0]
            g_ocm = 0
            for b in range(2):
                with nc.psum_tensor(f"ps_qkv{b}", [128, 4 * 512], F32) as ps4:
                    slot_ev = [0, 0, 0, 0]
                    for ts in range(8):
                        par = ts % 2

                        def ltr(sy, b=b, ts=ts, par=par, t_need=xnT_mm_t[par],
                                c_xn=self.c_xn):
                            sy.wait_ge(p.s_c, c_xn)
                            if t_need:
                                sy.wait_ge(p.s_t, t_need)
                            base = ts * 1024 + b * 512
                            for d in range(NDC):
                                p.inc(sy.dma_start(
                                    out=xnT[par][:, d * 512:(d + 1) * 512],
                                    in_=agxn[base:base + 512, d * 128:(d + 1) * 128],
                                    transpose=True), "s")
                        p.sec("sync", ltr)
                        s_tr = p.cnt["s"]

                        t_base = p.cnt["t"]
                        v_base = p.cnt["v"]

                        def mmqkv(t, par=par, s_tr=s_tr, g_w=g_w, ev=list(slot_ev),
                                  v_base=v_base, first=(ts == 0),
                                  a_pre=p.cnt["a"], v_pre=p.cnt["v"]):
                            t.wait_ge(p.s_s, max(s_tr, g_w))
                            if first:
                                t.wait_ge(p.s_a, a_pre)
                                t.wait_ge(p.s_v, v_pre)
                            k = 0
                            for hb in range(2):
                                for wsl in (wqb, wkb, wvb):
                                    pi = k % 4
                                    if k >= 4:
                                        t.wait_ge(p.s_v, v_base + (k - 4) + 1)
                                    elif ev[pi]:
                                        t.wait_ge(p.s_v, ev[pi])
                                    mmi = None
                                    for j in range(NDC):
                                        mmi = t.matmul(
                                            ps4[:, pi * 512:(pi + 1) * 512],
                                            wsl[:, (hb * 16 + j) * 128:(hb * 16 + j + 1) * 128],
                                            xnT[par][:, j * 512:(j + 1) * 512],
                                            start=(j == 0), stop=(j == NDC - 1),
                                            skip_group_check=True)
                                    p.inc(mmi, "t")
                                    k += 1
                        p.sec("tensor", mmqkv)
                        xnT_mm_t[par] = p.cnt["t"]

                        def evqkv(v, ts=ts, t_base=t_base):
                            k = 0
                            for hb in range(2):
                                for dst in (qT, kT, vT):
                                    pi = k % 4
                                    v.wait_ge(p.s_t, t_base + k + 1)
                                    p.inc(v.tensor_copy(
                                        dst[:, hb * T + ts * 512:(hb * T + (ts + 1) * 512)],
                                        ps4[:, pi * 512:(pi + 1) * 512]), "v")
                                    k += 1
                        p.sec("vector", evqkv)
                        for k in range(4):
                            slot_ev[k] = p.cnt["v"]
                        p.flush()

                def vtr(sy, v_q=p.cnt["v"], t_pre=p.cnt["t"]):
                    sy.wait_ge(p.s_v, v_q)
                    sy.wait_ge(p.s_t, t_pre)
                    for hb in range(2):
                        for sc in range(32):
                            p.inc(sy.dma_start(
                                out=vtok[:, hb * T + sc * 128: hb * T + (sc + 1) * 128],
                                in_=vT[:, hb * T + sc * 128: hb * T + (sc + 1) * 128],
                                transpose=True), "s")
                p.sec("sync", vtr)
                s_vt = p.cnt["s"]
                if b == 1:
                    self.wo_batch(0, g_ocm_of[0], p.cnt["v"])

                for hb in range(2):
                    with nc.psum_tensor(f"ps_s{b}{hb}", [128, 2 * 1024], F32) as ps_s, \
                         nc.psum_tensor(f"ps_O1{b}{hb}", [128, 512], F32) as ps_O1, \
                         nc.psum_tensor(f"ps_O2{b}{hb}", [128, 512], F32) as ps_O2, \
                         nc.psum_tensor(f"ps_L1{b}{hb}", [128, 512], F32) as ps_L1, \
                         nc.psum_tensor(f"ps_L2{b}{hb}", [128, 512], F32) as ps_L2:
                        exp_a_par = [0, 0]
                        bs_v_par = [0, 0]
                        o_free_v = 0
                        slot_t = [0, 0, 0, 0]
                        mtmp_v = [0, 0]
                        pending = None
                        for tt in range(8):
                            ns = 4 * (tt + 1)
                            npair = ns // 2
                            t0 = p.cnt["t"]
                            has_p = pending is not None

                            if has_p:
                                t_end_p, tt_p, g_need_p = pending

                                def pvec(v, t_end_p=t_end_p, g_need=g_need_p,
                                         hb=hb, tt_p=tt_p):
                                    v.wait_ge(p.s_t, t_end_p)
                                    if g_need:
                                        v.wait_ge(p.s_g, g_need)
                                    v.reciprocal(Bsb[:, 0:512], ps_L1[:, :])
                                    p.inc(v.reciprocal(Bsb[:, 512:1024],
                                                       ps_L2[:, :]), "v")
                                    v.tensor_tensor(cmb[:, 0:512], Osb[:, 0:512],
                                                    Bsb[:, 0:512], op=ALU.mult)
                                    v.scalar_tensor_tensor(
                                        cmb[:, 512:1024], Osb[:, 512:1024],
                                        p.lamneg[:, hb:hb + 1], Bsb[:, 512:1024],
                                        op0=ALU.mult, op1=ALU.mult)
                                    p.inc(v.tensor_tensor(
                                        ocm[:, tt_p * 512:(tt_p + 1) * 512],
                                        cmb[:, 0:512], cmb[:, 512:1024],
                                        op=ALU.add), "v")
                                p.sec("vector", pvec)
                                bs_v = p.cnt["v"] - 1
                            else:
                                bs_v = 0

                            a0 = p.cnt["a"]
                            v0m = p.cnt["v"]

                            def a_pred(idx, w):
                                return a0 + 2 * idx + w + 1

                            def m_pred(k):
                                return v0m + k + 1

                            slot_snap = {}
                            t_of = {}
                            sim_t = t0
                            fw = (p.cnt["a"], p.cnt["v"]) if tt == 0 else None
                            an0 = (exp_a_par[0], exp_a_par[1])
                            o_need0 = o_free_v
                            mv_prev = tuple(mtmp_v)

                            def tsec(t, tt=tt, hb=hb, npair=npair, t0=t0,
                                     fw=fw, an0=an0, o_need0=o_need0, bs_v=bs_v,
                                     s_vt=s_vt, a0=a0, v0m=v0m):
                                for idx in range(npair):
                                    si0 = 2 * idx
                                    if idx == 0 and fw is not None:
                                        t.wait_ge(p.s_s, s_vt)
                                        t.wait_ge(p.s_a, fw[0])
                                        t.wait_ge(p.s_v, fw[1])
                                    for w, si in enumerate((si0, si0 + 1)):
                                        if idx == 0:
                                            if an0[w]:
                                                t.wait_ge(p.s_a, an0[w])
                                        else:
                                            t.wait_ge(p.s_a, a0 + 2 * (idx - 1) + w + 1)
                                        par = si % 2
                                        t.matmul(ps_s[:, par * 1024: par * 1024 + 512],
                                                 kT[0:64, hb * T + si * 128: hb * T + (si + 1) * 128],
                                                 qT[0:64, hb * T + tt * 512: hb * T + (tt + 1) * 512],
                                                 start=True, stop=True, tile_position=(0, 0))
                                        p.inc(t.matmul(
                                            ps_s[:, par * 1024 + 512: par * 1024 + 1024],
                                            kT[64:128, hb * T + si * 128: hb * T + (si + 1) * 128],
                                            qT[64:128, hb * T + tt * 512: hb * T + (tt + 1) * 512],
                                            start=True, stop=True, tile_position=(64, 0)), "t")
                                    diag = si0 >= 4 * tt
                                    for w, si in enumerate((si0, si0 + 1)):
                                        slot = si % 4
                                        st_ = idx == 0 and w == 0
                                        sp_ = idx == npair - 1 and w == 1
                                        if diag:
                                            k = (si - 4 * tt)
                                            t.wait_ge(p.s_v, v0m + k + 1)
                                        else:
                                            t.wait_ge(p.s_a, a0 + 2 * idx + w + 1)
                                        if st_:
                                            if o_need0:
                                                t.wait_ge(p.s_v, o_need0)
                                            if bs_v:
                                                t.wait_ge(p.s_v, bs_v)
                                        t.matmul(ps_L1[:, :], p.ones_mat[:, :],
                                                 strip[:, slot * 1024: slot * 1024 + 512],
                                                 start=st_, stop=sp_,
                                                 skip_group_check=True)
                                        t.matmul(ps_L2[:, :], p.ones_mat[:, :],
                                                 strip[:, slot * 1024 + 512: (slot + 1) * 1024],
                                                 start=st_, stop=sp_,
                                                 skip_group_check=True)
                                        t.matmul(ps_O1[:],
                                                 vtok[:, hb * T + si * 128: hb * T + (si + 1) * 128],
                                                 strip[:, slot * 1024: slot * 1024 + 512],
                                                 start=st_, stop=sp_,
                                                 skip_group_check=True)
                                        p.inc(t.matmul(ps_O2[:],
                                                       vtok[:, hb * T + si * 128: hb * T + (si + 1) * 128],
                                                       strip[:, slot * 1024 + 512:(slot + 1) * 1024],
                                                       start=st_, stop=sp_,
                                                       skip_group_check=True), "t")
                            for idx in range(npair):
                                si0 = 2 * idx
                                sim_t += 2
                                t_of[si0] = sim_t - 1
                                t_of[si0 + 1] = sim_t
                                slot_snap[si0] = slot_t[si0 % 4]
                                slot_snap[si0 + 1] = slot_t[(si0 + 1) % 4]
                                sim_t += 2
                                slot_t[si0 % 4] = sim_t - 1
                                slot_t[(si0 + 1) % 4] = sim_t
                            p.sec("tensor", tsec)
                            assert p.cnt["t"] == sim_t, (p.cnt["t"], sim_t)
                            t_end = sim_t
                            exp_a_par[0] = a_pred(npair - 1, 0)
                            exp_a_par[1] = a_pred(npair - 1, 1)

                            def asec(sct, tt=tt, npair=npair, t_of=dict(t_of),
                                     slot_snap=dict(slot_snap), mv_prev=mv_prev,
                                     v0m=v0m):
                                for idx in range(npair):
                                    si0 = 2 * idx
                                    for w, si in enumerate((si0, si0 + 1)):
                                        par = si % 2
                                        slot = si % 4
                                        diag = si >= 4 * tt
                                        sct.wait_ge(p.s_t, max(t_of[si], slot_snap[si]))
                                        if diag:
                                            k = si - 4 * tt
                                            if k < 2:
                                                if mv_prev[si % 2]:
                                                    sct.wait_ge(p.s_v, mv_prev[si % 2])
                                            else:
                                                sct.wait_ge(p.s_v, v0m + k - 1)
                                        dst = (mtmp[si % 2][:] if diag
                                               else strip[:, slot * 1024:(slot + 1) * 1024])
                                        p.inc(sct.activation(
                                            dst, ps_s[:, par * 1024:(par + 1) * 1024],
                                            AF.Exp, scale=SCALE), "a")
                            p.sec("scalar", asec)
                            assert p.cnt["a"] == a0 + ns, (p.cnt["a"], a0, ns)

                            def msec(v, tt=tt, a0=a0, ns=ns,
                                     sl=(slot_t[0], slot_t[1], slot_t[2], slot_t[3]),
                                     slot_snap=dict(slot_snap)):
                                for k in range(4):
                                    si = 4 * tt + k
                                    slot = si % 4
                                    v.wait_ge(p.s_a, a0 + ns - 4 + k + 1)
                                    tsl = slot_snap[si]
                                    if tsl:
                                        v.wait_ge(p.s_t, tsl)
                                    p.inc(v.tensor_tensor(
                                        strip[:, slot * 1024:(slot + 1) * 1024],
                                        mtmp[si % 2][:],
                                        mask_bf[:, k * 1024:(k + 1) * 1024],
                                        op=ALU.mult), "v")
                            p.sec("vector", msec)
                            assert p.cnt["v"] == v0m + 4, (p.cnt["v"], v0m)
                            mtmp_v[0] = v0m + 3
                            mtmp_v[1] = v0m + 4

                            def vfl(v, t_end=t_end):
                                v.wait_ge(p.s_t, t_end)
                                v.tensor_copy(Osb[:, 0:512], ps_O1[:])
                                p.inc(v.tensor_copy(Osb[:, 512:1024], ps_O2[:]), "v")
                            p.sec("vector", vfl)
                            o_free_v = p.cnt["v"]
                            pending = (t_end, tt, (g_ocm if tt == 0 else 0))

                        t_end_p, tt_p, g_need_p = pending

                        def fvec(v, t_end_p=t_end_p, g_need=g_need_p, hb=hb,
                                 tt_p=tt_p):
                            v.wait_ge(p.s_t, t_end_p)
                            if g_need:
                                v.wait_ge(p.s_g, g_need)
                            v.reciprocal(Bsb[:, 0:512], ps_L1[:, :])
                            v.reciprocal(Bsb[:, 512:1024], ps_L2[:, :])
                            v.tensor_tensor(cmb[:, 0:512], Osb[:, 0:512],
                                            Bsb[:, 0:512], op=ALU.mult)
                            v.scalar_tensor_tensor(
                                cmb[:, 512:1024], Osb[:, 512:1024],
                                p.lamneg[:, hb:hb + 1], Bsb[:, 512:1024],
                                op0=ALU.mult, op1=ALU.mult)
                            p.inc(v.tensor_tensor(
                                ocm[:, tt_p * 512:(tt_p + 1) * 512],
                                cmb[:, 0:512], cmb[:, 512:1024], op=ALU.add), "v")
                        p.sec("vector", fvec)
                        pending = None


                    def soc(g, b=b, hb=hb, v_oc=p.cnt["v"]):
                        g.wait_ge(p.s_v, v_oc)
                        p.inc(g.dma_start(out=p.cc_o[b, hb], in_=ocm[:]), "g")
                    p.sec("gpsimd", soc)
                    g_ocm = p.cnt["g"]
                    g_ocm_hb[hb] = p.cnt["g"]
                g_ocm_of[b] = list(g_ocm_hb)
            self.wo_batch(1, g_ocm_of[1], p.cnt["v"])
            self.barrier()

    def wo_batch(self, b, g_ocm_b, v_pre):
        p, nc = self, self.nc
        st = self.wo_st
        wob, oc, s_sb = self.wob, self.oc, self.s_sb
        with nc.psum_tensor(f"ps_sp{b}", [128, 8 * 512], F32) as ps_sp:
            g_oc_of = [0] * 4
            for hb in range(2):
                def loc(sct, b=b, hb=hb, g_need=g_ocm_b[hb],
                        tns=list(st["oc_mm_t"]), first=(hb == 0)):
                    sct.wait_ge(p.s_g, g_need)
                    for tq in range(4):
                        if first and tns[tq]:
                            sct.wait_ge(p.s_t, tns[tq])
                        p.inc(sct.dma_start(
                            out=oc[tq][:, hb * 1024:(hb + 1) * 1024],
                            in_=p.cc_o[b, hb][:, tq * 1024:(tq + 1) * 1024]), "w")
                p.sec("scalar", loc)
            for tq in range(4):
                g_oc_of[tq] = p.cnt["w"]

            for tq in range(4):
                t_base = p.cnt["t"]
                a_base = p.cnt["a"]
                s_base = p.cnt["s"]
                ev_prev = (st["bank_ev"][0], st["bank_ev"][1])
                sb_prev = (st["sb_dma_g"][0], st["sb_dma_g"][1])
                for tb in range(8):
                    pp = tb % 2

                    def mmsp(t, tq=tq, tb=tb, pp=pp, g_oc=g_oc_of[tq],
                             g_wo=self.g_wo,
                             ev=(ev_prev[pp] if tb < 2 else a_base + tb - 1),
                             v_pre=(v_pre if tq == 0 and tb == 0 else 0)):
                        t.wait_ge(p.s_w, g_oc)
                        t.wait_ge(p.s_s, g_wo)
                        if v_pre:
                            t.wait_ge(p.s_v, v_pre)
                        if ev:
                            t.wait_ge(p.s_a, ev)
                        for dt in range(4):
                            mmi = None
                            for hb in range(2):
                                mmi = t.matmul(
                                    ps_sp[:, (pp * 4 + dt) * 512:(pp * 4 + dt + 1) * 512],
                                    oc[tq][:, hb * 1024 + tb * 128:
                                            hb * 1024 + (tb + 1) * 128],
                                    wob[:, (hb * 4 + dt) * 512:(hb * 4 + dt + 1) * 512],
                                    start=(hb == 0), stop=(hb == 1),
                                    skip_group_check=True)
                            p.inc(mmi, "t")
                    p.sec("tensor", mmsp)

                def evq(sct, t_base=t_base, s_base=s_base, sb_prev=sb_prev):
                    for tb in range(8):
                        pp = tb % 2
                        par = tb % 2
                        sn = sb_prev[par] if tb < 2 else s_base + 16 * (tb - 1)
                        if sn:
                            sct.wait_ge(p.s_s, sn)
                        sct.wait_ge(p.s_t, t_base + 4 * (tb + 1))
                        p.inc(sct.activation(s_sb[par][:, :],
                                             ps_sp[:, pp * 2048:(pp + 1) * 2048],
                                             AF.Identity), "a")
                p.sec("scalar", evq)
                assert p.cnt["a"] == a_base + 8

                for tb in range(8):
                    par = tb % 2

                    def ssb(sy, b=b, tq=tq, tb=tb, par=par, a_d=a_base + tb + 1):
                        sy.wait_ge(p.s_a, a_d)
                        row = tq * 1024 + tb * 128
                        p.inc(sy.dma_start(out=p.cc_s[b][row:row + 128, :],
                                           in_=s_sb[par][:]), "s")
                    p.sec("sync", ssb)
                assert p.cnt["s"] == s_base + 16 * 8
                st["bank_ev"][0] = a_base + 7
                st["bank_ev"][1] = a_base + 8
                st["sb_dma_g"][0] = s_base + 16 * 7
                st["sb_dma_g"][1] = s_base + 16 * 8
                st["oc_mm_t"][tq] = p.cnt["t"]
                p.flush()

        s_all = p.cnt["s"]

        def coll2(g, s_all=s_all, b=b):
            g.wait_ge(p.s_s, s_all)
            p.inc(g.collective_compute("ReduceScatter", ALU.add, ins=[p.cc_s[b]],
                                       outs=[p.rs_s[b]],
                                       replica_groups=[list(range(NC))]), "c")
        p.sec("gpsimd", coll2)
        st["c_rs"][b] = p.cnt["c"]

    def wo_final(self):
        p, nc = self, self.nc
        with ExitStack() as es:
            E = es.enter_context
            x_f = [E(nc.sbuf_tensor(f"xr{i}", [128, D], F32)) for i in range(2)]
            st_bf = [E(nc.sbuf_tensor(f"stb{i}", [128, D], BF16)) for i in range(2)]
            gt_bf = [E(nc.sbuf_tensor(f"gtb{i}", [128, D], BF16)) for i in range(2)]
            s_f = E(nc.sbuf_tensor("s_fin", [128, D], F32))
            gg_f = E(nc.sbuf_tensor("gg_f", [128, D], F32))
            y_sb = [E(nc.sbuf_tensor(f"ysb{i}", [128, D], F32)) for i in range(2)]
            v_done = [0, 0]
            g_out_of = [0, 0]
            for tb in range(NT):
                par = tb % 2

                half = 0 if tb < 4 else 1
                rs_src = p.rs_s[half][(tb - 4 * half) * 128:(tb - 4 * half + 1) * 128, :]

                def lfin(sy, tb=tb, par=par, v_need=v_done[par],
                         c_rs=self.wo_st["c_rs"][half], rs_src=rs_src):
                    sy.wait_ge(p.s_c, c_rs)
                    if v_need:
                        sy.wait_ge(p.s_v, v_need)
                    p.inc(sy.dma_start(out=x_f[par][:], in_=p.x[tb * 128:(tb + 1) * 128, :]), "s")
                    p.inc(sy.dma_start(out=st_bf[par][:], in_=rs_src), "s")
                    p.inc(sy.dma_start(out=gt_bf[par][:],
                                       in_=p.cc_g[tb * 128:(tb + 1) * 128, :]), "s")
                p.sec("sync", lfin)

                def vfin(v, tb=tb, par=par, s_n=p.cnt["s"], s_y=g_out_of[par]):
                    v.wait_ge(p.s_s, max(s_n, s_y))
                    v.tensor_copy(s_f[:], st_bf[par][:])
                    v.tensor_copy(gg_f[:], gt_bf[par][:])
                    v.tensor_tensor(y_sb[par][:], s_f[:], gg_f[:], op=ALU.mult)
                    p.inc(v.tensor_tensor(y_sb[par][:], y_sb[par][:], x_f[par][:],
                                          op=ALU.add), "v")
                p.sec("vector", vfin)
                v_yt = p.cnt["v"]
                v_done[par] = v_yt

                def sout(sy, tb=tb, par=par, v_yt=v_yt):
                    sy.wait_ge(p.s_v, v_yt)
                    p.inc(sy.dma_start(out=p.y[tb * 128:(tb + 1) * 128, :], in_=y_sb[par][:]), "s")
                p.sec("sync", sout)
                g_out_of[par] = p.cnt["s"]
                if tb % 2 == 1:
                    p.flush()

            if DEBUG_TAPS:
                tapb = E(nc.sbuf_tensor("tapb", [128, D], BF16))
                for (src, dst, n) in ((p.cc_xn, p.tp_xn, NT),):
                    for i in range(n):
                        def tl(g, src=src, i=i, gn=p.cnt["g"]):
                            g.wait_ge(p.s_g, gn)
                            p.inc(g.dma_start(out=tapb[:], in_=src[i * 128:(i + 1) * 128, :]), "g")
                        p.sec("gpsimd", tl)
                        def ts_(g, dst=dst, i=i, gn=p.cnt["g"]):
                            g.wait_ge(p.s_g, gn)
                            p.inc(g.dma_start(out=dst[i * 128:(i + 1) * 128, :], in_=tapb[:]), "g")
                        p.sec("gpsimd", ts_)

            g_fin = p.cnt["g"]
            s_fin = p.cnt["s"]

            def fin(g, g_fin=g_fin, s_fin=s_fin):
                g.wait_ge(p.s_g, g_fin)
                g.wait_ge(p.s_s, s_fin)
            p.sec("gpsimd", fin)

    def close(self):
        self.es.close()


def build_program():
    p = Prog()
    p.setup()
    with ExitStack() as es:
        p.xqT = es.enter_context(p.nc.sbuf_tensor("xqT", [128, NDC * R], BF16))
        p.slab = [es.enter_context(p.nc.sbuf_tensor(f"wslab{i}", [128, 2048], F32))
                  for i in range(2)]
        p.phase1()
        p.ffn()
    p.attention()
    p.wo_final()
    p.flush()
    p.close()
    return p.nc


_masks_cache = None


def _masks():
    global _masks_cache
    if _masks_cache is None:
        m = np.zeros((4, 128, 1024), np.float32)
        pp = np.arange(128)[:, None]
        ff = np.arange(512)[None, :]
        for v in range(4):
            mv = (pp + v * 128 <= ff).astype(np.float32)
            m[v, :, 0:512] = mv
            m[v, :, 512:1024] = mv
        _masks_cache = m
    return _masks_cache


def kernel(L0_attn_x, L0_attn_Wq, L0_attn_Wk, L0_attn_Wv, L0_attn_Wo,
           L0_attn_lambda_init, L0_ffn_W_g, L0_ffn_W_u, L0_ffn_W_d):
    x = np.ascontiguousarray(np.asarray(L0_attn_x, np.float32)).reshape(B * T, D)
    Wq = np.asarray(L0_attn_Wq, np.float32)
    Wk = np.asarray(L0_attn_Wk, np.float32)
    Wv = np.asarray(L0_attn_Wv, np.float32)
    Wo = np.asarray(L0_attn_Wo, np.float32)
    lam = np.asarray(L0_attn_lambda_init, np.float32)
    Wg = np.asarray(L0_ffn_W_g, np.float32)
    Wu = np.asarray(L0_ffn_W_u, np.float32)
    Wd = np.asarray(L0_ffn_W_d, np.float32)

    def qkv_img(W, c):
        S = W[:, 256 * c:256 * (c + 1)]
        return np.ascontiguousarray(
            S.reshape(16, 128, 2, 128).transpose(1, 2, 0, 3).reshape(128, 4096))

    wg_img = np.ascontiguousarray(
        Wg.reshape(64, 128, 16, 128).transpose(0, 3, 2, 1).reshape(64, 128, 2048))
    wu_img = np.ascontiguousarray(
        Wu.reshape(64, 128, 16, 128).transpose(0, 3, 2, 1).reshape(64, 128, 2048))
    wd_img = np.ascontiguousarray(
        Wd.reshape(4, 512, 64, 128).transpose(0, 2, 3, 1))
    msk = _masks()

    in_maps = []
    for c in range(NC):
        wo_c = np.ascontiguousarray(
            Wo[256 * c:256 * (c + 1)].reshape(2, 128, 4, 512).transpose(0, 2, 1, 3))
        in_maps.append({
            "x": np.ascontiguousarray(np.concatenate(
                [x[c * 512:(c + 1) * 512], x[T + c * 512:T + (c + 1) * 512]])),
            "wq": qkv_img(Wq, c), "wk": qkv_img(Wk, c), "wv": qkv_img(Wv, c),
            "wo": wo_c, "wg": wg_img, "wu": wu_img, "wd": wd_img,
            "lam": np.ascontiguousarray(lam[2 * c:2 * c + 2].reshape(1, 2)),
            "msk": msk,
        })

    nc = build_program()
    res = run_bass_kernel_spmd(nc, in_maps, core_ids=list(range(NC)))
    out = np.empty((B * T, D), np.float32)
    for c in range(NC):
        yc = np.asarray(res.results[c]["y"], np.float32)
        out[c * 512:(c + 1) * 512] = yc[0:512]
        out[T + c * 512:T + (c + 1) * 512] = yc[512:1024]
    return out.reshape(B, T, D)


if __name__ == "__main__":
    import reference as ref
    ins = ref.setup_inputs(0)
    got = kernel(**{k: np.asarray(v) for k, v in ins.items()})
    exp = np.asarray(ref.reference(**ins))
    err = np.abs(got - exp)
    rv = float(np.mean((got - exp) ** 2) / np.mean(np.asarray(exp) ** 2))
    print("maxabs", err.max(), "resvar", rv)
